# version 19
# baseline (speedup 1.0000x reference)
import numpy as np
import ml_dtypes

import concourse.bass as bass
import concourse.mybir as mybir
from concourse.bass_utils import run_bass_kernel_spmd

F32 = mybir.dt.float32
BF16 = mybir.dt.bfloat16
AF = mybir.ActivationFunctionType
ALU = mybir.AluOpType
AX = mybir.AxisListType

D = 1024
DC = 8
NEXP = 15
NCORES = 8
TILE = 512
EPS = 1e-5


class Buf:
    __slots__ = ("ready", "readers")

    def __init__(self):
        self.ready = None
        self.readers = []


def seed(dst_bufs, src_bufs):
    toks = []
    for s in src_bufs:
        if s.ready is not None:
            toks.append(s.ready)
        toks.extend(s.readers)
    for d in dst_bufs:
        d.ready = None
        d.readers = list(toks)


NQSEM = 8


class Prog:
    ENG = {
        "pe": ("tensor", "pe", 1, 1),
        "act": ("scalar", "act", 1, 1),
        "dve": ("vector", "dve", 1, 1),
        "sync": ("sync", "qsp", 16, NQSEM),
        "actdma": ("scalar", "qact", 16, NQSEM),
        "gps": ("gpsimd", "qgps", 16, NQSEM),
        "cc": ("gpsimd", "cc", 1, 1),
    }

    def __init__(self, block, sems):
        self.block = block
        self.sems = sems
        self.cnt = {k: 0 for k in sems}
        self.wm = {}
        self.qidx = {}
        self.streams = {s: [] for s in
                        ("tensor", "scalar", "vector", "sync", "gpsimd")}

    def emit(self, eng, fns, reads=(), writes=(), waits=()):
        sec_name, semkey, inc, ring = self.ENG[eng]
        if ring > 1:
            i = self.qidx.get(eng, 0)
            self.qidx[eng] = i + 1
            semkey = "%s%d" % (semkey, i % ring)
        req = {}

        def add(tok):
            if tok is None:
                return
            sk, v = tok
            if req.get(sk, -1) < v:
                req[sk] = v

        for b in reads:
            add(b.ready)
        for b in writes:
            add(b.ready)
            for t in b.readers:
                add(t)
        for t in waits:
            add(t)
        if inc == 16 and self.cnt[semkey] > 0:
            add((semkey, self.cnt[semkey]))
        wm = self.wm.setdefault(sec_name, {})
        todo = [(sk, v) for sk, v in req.items() if wm.get(sk, 0) < v]
        self.cnt[semkey] += inc
        tok = (semkey, self.cnt[semkey])
        self.streams[sec_name].append((todo, fns, semkey, inc))
        for sk, v in todo:
            if wm.get(sk, 0) < v:
                wm[sk] = v
        for b in reads:
            b.readers.append(tok)
        for b in writes:
            b.ready = tok
            b.readers = []
        return tok

    def final_wait(self, eng, waits):
        sec_name = self.ENG[eng][0]
        self.streams[sec_name].append((list(waits), [], None, 0))

    def flush(self):
        sems = self.sems
        for sec_name, items in self.streams.items():
            if not items:
                continue
            sec = getattr(self.block, sec_name)

            @sec
            def _(eng_obj, items=items):
                for todo, fns, semkey, inc in items:
                    for sk, v in todo:
                        eng_obj.wait_ge(sems[sk], v)
                    inst = None
                    for fn in fns:
                        inst = fn(eng_obj)
                    if semkey is not None:
                        inst.then_inc(sems[semkey], inc)


def build_nc(T):
    assert T % TILE == 0
    NT = T // TILE
    QT = min(1024, T)
    NQ = T // QT
    TPQ = QT // TILE
    NTOK_GLOBAL = float(NCORES * T)

    nc = bass.Bass("TRN2", target_bir_lowering=False)

    x_in = nc.dram_tensor("x", [T, D], F32, kind="ExternalInput").ap()
    wqTp = nc.dram_tensor("wqTp", [128, 8192], F32, kind="ExternalInput").ap()
    wkTp = nc.dram_tensor("wkTp", [128, 8192], F32, kind="ExternalInput").ap()
    wvTp = nc.dram_tensor("wvTp", [128, 8192], F32, kind="ExternalInput").ap()
    wpTp = nc.dram_tensor("wpTp", [128, 8192], F32, kind="ExternalInput").ap()
    sWcat = nc.dram_tensor("sWcat", [128, 16384], F32, kind="ExternalInput").ap()
    eWcat = nc.dram_tensor("eWcat", [NEXP, 128, 16384], F32,
                           kind="ExternalInput").ap()
    routW = nc.dram_tensor("routW", [D, NEXP], F32, kind="ExternalInput").ap()
    eb2_in = nc.dram_tensor("eb2", [NEXP, D], F32, kind="ExternalInput").ap()
    vecs_in = nc.dram_tensor("vecs", [128, 176], F32, kind="ExternalInput").ap()
    rows_in = nc.dram_tensor("rows", [1, 2064], F32, kind="ExternalInput").ap()
    y_out = nc.dram_tensor("y", [T, D], F32, kind="ExternalOutput").ap()

    xn_sc = nc.dram_tensor("xn_sc", [T, D], BF16, kind="Internal").ap()
    h_sc = nc.dram_tensor("h_sc", [3 * D, T], BF16, kind="Internal").ap()
    qk_sc = nc.dram_tensor("qk_sc", [D, T], BF16, kind="Internal").ap()
    v_sc = nc.dram_tensor("v_sc", [D, T], BF16, kind="Internal").ap()
    cc_in = nc.dram_tensor("cc_in", [128, 48], F32, kind="Internal").ap()
    cc_out = nc.dram_tensor("cc_out", [128, 48], F32, kind="Internal").ap()

    ident_c = nc.inline_tensor(np.eye(128, dtype=ml_dtypes.bfloat16),
                               name="identc").ap()
    ones_c = nc.inline_tensor(np.ones((1, 128), dtype=ml_dtypes.bfloat16),
                              name="onesc").ap()
    rowhot_np = np.zeros((NEXP, NEXP * 128), dtype=ml_dtypes.bfloat16)
    for e in range(NEXP):
        rowhot_np[e, e * 128:(e + 1) * 128] = 1.0
    rowhot_c = nc.inline_tensor(rowhot_np, name="rowhotc").ap()

    from contextlib import ExitStack
    with ExitStack() as ctx:
        en = ctx.enter_context
        w_ring = en(nc.sbuf_tensor("w_ring", [128, 4 * 8192], BF16))
        acc = en(nc.sbuf_tensor("acc", [128, max(TPQ * 4096, 5632)], F32))
        xn_q = en(nc.sbuf_tensor("xn_q", [128, max(DC * QT, 8192)], BF16))
        h_arena = en(nc.sbuf_tensor("h_arena", [128, 12288], BF16))
        qk_bufs = en(nc.sbuf_tensor("qk_bufs", [128, 8192], BF16))
        d_extra = en(nc.sbuf_tensor("d_extra", [128, 8192], BF16))
        x_sb = en(nc.sbuf_tensor("x_sb", [128, 4096], F32))
        sq_junk = en(nc.sbuf_tensor("sq_junk", [128, 512], BF16))
        vecs_sb = en(nc.sbuf_tensor("vecs_sb", [128, 176], F32))
        stats_acc = en(nc.sbuf_tensor("stats_acc", [128, 48], F32))
        stats_tmp = en(nc.sbuf_tensor("stats_tmp", [128, 8], F32))
        stats_gl = en(nc.sbuf_tensor("stats_gl", [128, 48], F32))
        ab_coef = en(nc.sbuf_tensor("ab_coef", [128, 48], F32))
        ab_tmp = en(nc.sbuf_tensor("ab_tmp", [128, 24], F32))
        ln_t = en(nc.sbuf_tensor("ln_t", [128, 16], F32))
        bias_bf = en(nc.sbuf_tensor("bias_bf", [1, 1024], BF16))
        rbrow_bf = en(nc.sbuf_tensor("rbrow_bf", [1, 16], BF16))
        rb_bc = en(nc.sbuf_tensor("rb_bc", [128, 15], F32))
        eb2_sb = en(nc.sbuf_tensor("eb2_sb", [15, 1024], BF16))
        routW_sb = en(nc.sbuf_tensor("routW_sb", [128, 8 * 15], BF16))
        vv_sb = en(nc.sbuf_tensor("vv_sb", [128, 1024], F32))
        w_tm_all = en(nc.sbuf_tensor("w_tm_all", [128, 512], BF16))
        w_fm_sb = en(nc.sbuf_tensor("w_fm_sb", [16, 1024], BF16))
        w_tm_t = en(nc.sbuf_tensor("w_tm_t", [128, 16], BF16))
        wbc_sb = en(nc.sbuf_tensor("wbc_sb", [128, 512], BF16))
        l_sb = en(nc.sbuf_tensor("l_sb", [128, 64], F32))
        sm_a = en(nc.sbuf_tensor("sm_a", [128, 16], F32))
        sm_b = en(nc.sbuf_tensor("sm_b", [128, 16], F32))
        sm_c = en(nc.sbuf_tensor("sm_c", [128, 16], F32))
        sm_t = en(nc.sbuf_tensor("sm_t", [128, 8], F32))
        eps_t = en(nc.sbuf_tensor("eps_t", [128, 1], F32))
        c125_t = en(nc.sbuf_tensor("c125_t", [128, 1], F32))
        ident_sb = en(nc.sbuf_tensor("ident_sb", [128, 128], BF16))
        ones_sb = en(nc.sbuf_tensor("ones_sb", [1, 128], BF16))
        rowhot_sb = en(nc.sbuf_tensor("rowhot_sb", [15, 15 * 128], BF16))

        ps_out0 = en(nc.psum_tensor("ps_out0", [128, 1024], F32))
        ps_out1 = en(nc.psum_tensor("ps_out1", [128, 1024], F32))
        ps_h0 = en(nc.psum_tensor("ps_h0", [128, 512], F32))
        ps_h1 = en(nc.psum_tensor("ps_h1", [128, 512], F32))
        ps_wbc = en(nc.psum_tensor("ps_wbc", [128, 512], F32))
        ps_tr = en(nc.psum_tensor("ps_tr", [128, 128], BF16))

        xsq_junk = acc[:, 4096:5120]
        sq_junk = acc[:, 5120:5632]
        bd_vv = d_extra[:, 4096:5120]
        sems = {}
        for k in ("pe", "act", "dve", "cc"):
            sems[k] = en(nc.semaphore(k))
        for base in ("qsp", "qact", "qgps"):
            for i in range(NQSEM):
                k = "%s%d" % (base, i)
                sems[k] = en(nc.semaphore(k))
        block = en(nc.Block())
        P = Prog(block, sems)

        ps_out = [ps_out0, ps_out1]
        ps_h = [ps_h0, ps_h1]
        b_psout = [Buf(), Buf()]
        b_psh = [Buf(), Buf()]
        b_pswbc = Buf()
        b_pstr = Buf()
        b_xnq = [Buf() for _ in range(DC)]
        b_acc = [Buf() for _ in range(TPQ * 4)]
        b_hraw = [Buf() for _ in range(DC)]
        b_hscl = [[Buf() for _ in range(DC)], [Buf() for _ in range(DC)]]
        b_hD = [Buf(), Buf(), Buf()]
        b_qkbuf = [Buf(), Buf()]
        b_dx = [Buf(), Buf()]
        b_xsb = Buf()
        b_xin1 = Buf()
        b_xsq = Buf()
        b_sqj = Buf()
        b_vecs = Buf()
        b_stats = Buf()
        b_stmp = [Buf() for _ in range(4)]
        b_stgl = Buf()
        b_ab = Buf()
        b_abt = Buf()
        b_lnt = Buf()
        b_rows = Buf()
        b_bias = Buf()
        b_rbrow = Buf()
        b_rbbc = Buf()
        b_eb2 = Buf()
        b_routW = Buf()
        b_bdvv = Buf()
        b_vvsb = Buf()
        b_wtmall = Buf()
        b_wfm = [Buf(), Buf()]
        b_wtm = Buf()
        b_wbc = Buf()
        b_lsb = Buf()
        b_sma = Buf()
        b_smb = Buf()
        b_smc = Buf()
        b_smt = Buf()
        b_ident = Buf()
        b_eps = Buf()
        b_c125 = Buf()
        b_ones = Buf()
        b_rowhot = Buf()
        b_xnsc = [Buf() for _ in range(NT)]
        b_hsc = [[Buf() for _ in range(NT)] for _ in range(3)]
        b_qksc = [Buf() for _ in range(NT)]
        b_vsc = [Buf() for _ in range(NT)]
        b_ysc = [Buf() for _ in range(NT)]
        b_ccin = Buf()
        b_ccout = Buf()

        P.emit("sync", [lambda e: e.dma_start(out=vecs_sb[:], in_=vecs_in)],
               writes=[b_vecs])
        rows_sb = x_sb[0:1, 0:2064]
        P.emit("sync", [lambda e: e.dma_start(out=rows_sb, in_=rows_in)],
               writes=[b_rows])
        P.emit("sync", [lambda e: e.dma_start(out=ident_sb[:], in_=ident_c)],
               writes=[b_ident])
        P.emit("sync", [lambda e: e.dma_start(out=ones_sb[:], in_=ones_c)],
               writes=[b_ones])
        P.emit("sync", [lambda e: e.dma_start(out=rowhot_sb[:], in_=rowhot_c)],
               writes=[b_rowhot])
        P.emit("gps", [lambda e: e.dma_start(out=eb2_sb[:], in_=eb2_in)],
               writes=[b_eb2])
        P.emit("gps", [lambda e: e.dma_start(
            out=routW_sb[:].rearrange("p (dc e) -> p dc e", dc=DC),
            in_=routW.rearrange("(dc p) e -> p dc e", p=128))],
            writes=[b_routW])

        def load_slot(slot, src2, wbuf):
            P.emit("gps", [lambda e, slot=slot, s=src2: e.dma_start(
                out=w_ring[:, slot * 8192:(slot + 1) * 8192], in_=s)],
                writes=[wbuf])

        def load_pair(base, srccat, wbuf):
            P.emit("gps", [lambda e, base=base, s=srccat: e.dma_start(
                out=w_ring[:, base * 16384:(base + 1) * 16384], in_=s)],
                writes=[wbuf])

        b_wpair = [Buf(), Buf()]
        load_slot(0, wqTp, b_wpair[0])
        load_slot(1, wkTp, b_wpair[0])
        load_slot(2, wvTp, b_wpair[1])

        P.emit("dve", [lambda e: e.tensor_add(
            xsq_junk[0:1, 0:1024], rows_sb[:, 0:1024],
            rows_sb[:, 1024:2048])],
            reads=[b_rows], writes=[b_xsq])
        P.emit("act", [lambda e: e.activation(
            out=bias_bf[0:1, :], in_=xsq_junk[0:1, 0:1024], func=AF.Copy)],
            reads=[b_xsq], writes=[b_bias])
        P.emit("act", [lambda e: e.activation(
            out=rbrow_bf[0:1, 0:15], in_=rows_sb[:, 2048:2063],
            func=AF.Copy)],
            reads=[b_rows], writes=[b_rbrow])
        P.emit("pe", [lambda e: e.matmul(
            ps_wbc[:, 0:15], ones_sb[0:1, :], rbrow_bf[0:1, 0:15],
            start=True, stop=True)],
            reads=[b_ones, b_rbrow], writes=[b_pswbc])
        P.emit("dve", [lambda e: e.tensor_copy(rb_bc[:], ps_wbc[:, 0:15])],
               reads=[b_pswbc], writes=[b_rbbc])
        P.emit("dve", [lambda e: e.memset(stats_acc[:], 0.0)],
               writes=[b_stats])
        P.emit("dve", [lambda e: e.memset(w_tm_t[:, 15:16], 0.0)],
               writes=[b_wtm])
        P.emit("dve", [lambda e: e.memset(vv_sb[:], 0.0)],
               writes=[b_vvsb])
        P.emit("dve", [lambda e: e.memset(eps_t[:], EPS)], writes=[b_eps])
        P.emit("dve", [lambda e: e.memset(c125_t[:], 0.125)], writes=[b_c125])

        seed([b_xsb], [b_xsb, b_rows])
        x_bufs = [x_sb[:, 0:4096], acc[:, 0:4096]]
        bx_bufs = [b_xsb, b_xin1]
        xn_tm = d_extra[:, 0:4096]
        b_xntm = b_dx[0]
        hbufs = [d_extra[:, 4096:8192], qk_bufs[:, 0:4096]]
        b_hbufs = [b_dx[1], b_qkbuf[0]]

        def ab_ln_tile(t):
            bi = t % 2
            xb, bxb = x_bufs[bi], bx_bufs[bi]
            src = x_in[t * TILE:(t + 1) * TILE, :].rearrange(
                "(c p) d -> p c d", p=128)
            P.emit("sync", [lambda e, d=xb, s=src: e.dma_start(
                out=d.rearrange("p (c d) -> p c d", c=4), in_=s)],
                writes=[bxb])
            for c in range(4):
                xc = xb[:, c * 1024:(c + 1) * 1024]
                P.emit("dve", [lambda e, xc=xc, c=c: e.reduce_sum(
                    ln_t[:, c:c + 1], xc, axis=AX.X)],
                    reads=[bxb], writes=[b_lnt])
                P.emit("act", [lambda e, xc=xc, c=c: e.activation(
                    out=xsq_junk[:, 0:1024], in_=xc, func=AF.Square,
                    accum_out=ln_t[:, 4 + c:5 + c])],
                    reads=[bxb], writes=[b_xsq, b_lnt])
            P.emit("dve", [lambda e: e.tensor_scalar_mul(
                ln_t[:, 8:12], ln_t[:, 0:4], 1.0 / 1024.0)],
                reads=[b_lnt], writes=[b_lnt])
            P.emit("dve", [lambda e: e.tensor_scalar_mul(
                ln_t[:, 4:8], ln_t[:, 4:8], 1.0 / 1024.0)],
                reads=[b_lnt], writes=[b_lnt])
            P.emit("dve", [lambda e: e.tensor_mul(
                ln_t[:, 12:16], ln_t[:, 8:12], ln_t[:, 8:12])],
                reads=[b_lnt], writes=[b_lnt])
            P.emit("dve", [lambda e: e.tensor_sub(
                ln_t[:, 4:8], ln_t[:, 4:8], ln_t[:, 12:16])],
                reads=[b_lnt], writes=[b_lnt])
            P.emit("act", [lambda e: e.activation(
                out=ln_t[:, 4:8], in_=ln_t[:, 4:8], func=AF.Sqrt,
                bias=eps_t[:, 0:1])],
                reads=[b_lnt, b_eps], writes=[b_lnt])
            P.emit("dve", [lambda e: e.reciprocal(
                ln_t[:, 12:16], ln_t[:, 4:8])],
                reads=[b_lnt], writes=[b_lnt])
            P.emit("dve", [lambda e: e.scalar_tensor_tensor(
                ln_t[:, 8:12], ln_t[:, 8:12], -1.0, ln_t[:, 12:16],
                ALU.mult, ALU.mult)],
                reads=[b_lnt], writes=[b_lnt])
            for c in range(4):
                xc = xb[:, c * 1024:(c + 1) * 1024]
                P.emit("dve", [lambda e, xc=xc, c=c: e.tensor_scalar(
                    xn_tm[:, c * 1024:(c + 1) * 1024], xc,
                    ln_t[:, 12 + c:13 + c], ln_t[:, 8 + c:9 + c],
                    ALU.mult, ALU.add)],
                    reads=[bxb, b_lnt], writes=[b_xntm])
            dst = xn_sc[t * TILE:(t + 1) * TILE, :].rearrange(
                "(c p) d -> p c d", p=128)
            P.emit("sync", [lambda e, d=dst: e.dma_start(
                out=d, in_=xn_tm.rearrange("p (c d) -> p c d", c=4))],
                reads=[b_xntm], writes=[b_xnsc[t]])

        def xn_quarter_load(q):
            qreads = [b_xnsc[q * TPQ + i] for i in range(TPQ)]
            for dc in range(DC):
                P.emit("actdma", [lambda e, q=q, dc=dc:
                                  e.dma_start_transpose(
                    xn_q[:, dc * QT:(dc + 1) * QT],
                    xn_sc[q * QT:(q + 1) * QT, dc * 128:(dc + 1) * 128])],
                    reads=qreads, writes=[b_xnq[dc]])

        psh_cnt = [0]

        def ab_proj_quarter(q):
            for ti in range(TPQ):
                t = q * TPQ + ti
                toff = ti * TILE
                for br in range(3):
                    hb = (t * 3 + br) % 2
                    hbuf, b_hbuf = hbufs[hb], b_hbufs[hb]
                    wslot = w_ring[:, br * 8192:(br + 1) * 8192]
                    for ec in range(DC):
                        par = psh_cnt[0] % 2
                        psh_cnt[0] += 1
                        fns = []
                        for dc in range(DC):
                            fns.append(
                                lambda e, w=wslot, dc=dc, ec=ec, toff=toff,
                                par=par:
                                e.matmul(
                                    ps_h[par][:, 0:512],
                                    w[:, dc * 1024 + ec * 128:
                                      dc * 1024 + ec * 128 + 128],
                                    xn_q[:, dc * QT + toff:
                                         dc * QT + toff + 512],
                                    start=(dc == 0), stop=(dc == DC - 1)))
                        P.emit("pe", fns,
                               reads=[b_wpair[0] if br < 2 else b_wpair[1]]
                               + b_xnq,
                               writes=[b_psh[par]])
                        st_r = (br * DC + ec) % 4
                        P.emit("act", [lambda e, ec=ec, par=par, st_r=st_r,
                                       hbuf=hbuf: e.activation(
                            out=hbuf[:, ec * 512:(ec + 1) * 512],
                            in_=ps_h[par][:, 0:512], func=AF.Copy,
                            accum_out=stats_tmp[:, st_r * 2:st_r * 2 + 1])],
                            reads=[b_psh[par]],
                            writes=[b_hbuf, b_stmp[st_r]])
                        P.emit("dve", [lambda e, ec=ec, st_r=st_r,
                                       hbuf=hbuf: e.scalar_tensor_tensor(
                            sq_junk[:, 0:512],
                            hbuf[:, ec * 512:(ec + 1) * 512], 1.0,
                            hbuf[:, ec * 512:(ec + 1) * 512],
                            ALU.mult, ALU.mult,
                            accum_out=stats_tmp[:, st_r * 2 + 1:
                                                st_r * 2 + 2])],
                            reads=[b_hbuf], writes=[b_sqj, b_stmp[st_r]])
                        cs = br * 16 + ec
                        P.emit("dve", [lambda e, cs=cs, st_r=st_r:
                                       e.tensor_add(
                            stats_acc[:, cs:cs + 16].rearrange(
                                "p (two ec) -> p (ec two)", two=2
                            )[:, 0:2] if False else
                            stats_acc[:, cs:cs + 9].rearrange(
                                "p (a b) -> p a b", a=1)[:, 0, 0:9:8],
                            stats_acc[:, cs:cs + 9].rearrange(
                                "p (a b) -> p a b", a=1)[:, 0, 0:9:8],
                            stats_tmp[:, st_r * 2:st_r * 2 + 2])],
                            reads=[b_stmp[st_r], b_stats], writes=[b_stats])
                    dsth = h_sc[br * D:(br + 1) * D, :].rearrange(
                        "(dc p) n -> p dc n", p=128)
                    for half in range(2):
                        P.emit("sync", [lambda e, d=dsth, half=half, t=t,
                                        hbuf=hbuf: e.dma_start(
                            out=d[:, half * 4:half * 4 + 4,
                                  t * TILE:(t + 1) * TILE],
                            in_=hbuf[:, half * 2048:half * 2048 + 2048]
                            .rearrange("p (dc n) -> p dc n", dc=4))],
                            reads=[b_hbuf], writes=[b_hsc[br][t]])

        def f_router(tq):
            ti = tq % TPQ
            toff = ti * TILE
            for c in range(4):
                fns = []
                for dc in range(DC):
                    fns.append(lambda e, dc=dc, c=c, toff=toff:
                               e.matmul(
                        ps_wbc[:, c * 128:c * 128 + 15],
                        xn_q[:, dc * QT + toff + c * 128:
                             dc * QT + toff + c * 128 + 128],
                        routW_sb[:, dc * 15:(dc + 1) * 15],
                        start=(dc == 0), stop=(dc == DC - 1)))
                P.emit("pe", fns, reads=b_xnq + [b_routW], writes=[b_pswbc])
                lc = l_sb[:, c * 16:c * 16 + 15]
                P.emit("dve", [lambda e, lc=lc, c=c: e.tensor_add(
                    lc, ps_wbc[:, c * 128:c * 128 + 15], rb_bc[:])],
                    reads=[b_pswbc, b_rbbc], writes=[b_lsb])
            for c in range(4):
                lc = l_sb[:, c * 16:c * 16 + 15]
                P.emit("dve", [lambda e, lc=lc: e.reduce_max(
                    sm_t[:, 0:1], lc, axis=AX.X)],
                    reads=[b_lsb], writes=[b_smt])
                P.emit("dve", [lambda e, lc=lc: e.tensor_scalar(
                    sm_a[:, 0:15], lc, sm_t[:, 0:1], None, ALU.is_ge)],
                    reads=[b_lsb, b_smt], writes=[b_sma])
                P.emit("dve", [lambda e, lc=lc: e.scalar_tensor_tensor(
                    sm_b[:, 0:15], sm_a[:, 0:15], -30000.0, lc,
                    ALU.mult, ALU.add)],
                    reads=[b_sma, b_lsb], writes=[b_smb])
                P.emit("dve", [lambda e: e.reduce_max(
                    sm_t[:, 1:2], sm_b[:, 0:15], axis=AX.X)],
                    reads=[b_smb], writes=[b_smt])
                P.emit("dve", [lambda e: e.tensor_scalar(
                    sm_a[:, 0:15], sm_b[:, 0:15], sm_t[:, 1:2], None,
                    ALU.is_ge)],
                    reads=[b_smb, b_smt], writes=[b_sma])
                P.emit("dve", [lambda e: e.scalar_tensor_tensor(
                    sm_b[:, 0:15], sm_a[:, 0:15], -30000.0, sm_b[:, 0:15],
                    ALU.mult, ALU.add)],
                    reads=[b_sma, b_smb], writes=[b_smb])
                P.emit("dve", [lambda e: e.reduce_max(
                    sm_t[:, 2:3], sm_b[:, 0:15], axis=AX.X)],
                    reads=[b_smb], writes=[b_smt])
                P.emit("dve", [lambda e: e.tensor_scalar_mul(
                    sm_t[:, 3:4], sm_t[:, 0:1], -1.0)],
                    reads=[b_smt], writes=[b_smt])
                P.emit("act", [lambda e, lc=lc: e.activation(
                    out=sm_c[:, 0:15], in_=lc, func=AF.Exp,
                    bias=sm_t[:, 3:4], scale=1.0)],
                    reads=[b_lsb, b_smt], writes=[b_smc])
                P.emit("dve", [lambda e: e.reduce_sum(
                    sm_t[:, 4:5], sm_c[:, 0:15], axis=AX.X)],
                    reads=[b_smc], writes=[b_smt])
                P.emit("dve", [lambda e: e.reciprocal(
                    sm_t[:, 5:6], sm_t[:, 4:5])],
                    reads=[b_smt], writes=[b_smt])
                P.emit("dve", [lambda e, lc=lc: e.tensor_scalar(
                    sm_a[:, 0:15], lc, sm_t[:, 2:3], None, ALU.is_ge)],
                    reads=[b_lsb, b_smt], writes=[b_sma])
                P.emit("dve", [lambda e: e.tensor_mul(
                    sm_c[:, 0:15], sm_c[:, 0:15], sm_a[:, 0:15])],
                    reads=[b_sma, b_smc], writes=[b_smc])
                P.emit("dve", [lambda e: e.tensor_scalar_mul(
                    w_tm_t[:, 0:15], sm_c[:, 0:15], sm_t[:, 5:6])],
                    reads=[b_smc, b_smt], writes=[b_wtm])
                P.emit("dve", [lambda e, tq=tq, c=c: e.tensor_copy(
                    w_tm_all[:, (tq * 4 + c) * 16:(tq * 4 + c) * 16 + 16],
                    w_tm_t[:, 0:16])],
                    reads=[b_wtm], writes=[b_wtmall])

        for q in range(NQ + 1):
            if q < NQ:
                for ti in range(TPQ):
                    ab_ln_tile(q * TPQ + ti)
            if q >= 1:
                ab_proj_quarter(q - 1)
                for ti in range(TPQ):
                    f_router((q - 1) * TPQ + ti)
            if q < NQ:
                xn_quarter_load(q)

        def emit_phase_c():
            P.emit("sync", [lambda e: e.dma_start(out=cc_in, in_=stats_acc[:])],
                   reads=[b_stats], writes=[b_ccin])
            P.emit("cc", [lambda e: e.collective_compute(
                "AllReduce", ALU.add,
                replica_groups=[list(range(NCORES))],
                ins=[cc_in.opt()], outs=[cc_out.opt()])],
                reads=[b_ccin], writes=[b_ccout])
            P.emit("gps", [lambda e: e.dma_start(out=stats_gl[:], in_=cc_out)],
                   reads=[b_ccout], writes=[b_stgl])
            sgl = stats_gl[:].rearrange("p (br k c) -> p br k c", br=3, k=2)
            vec3 = vecs_sb[:, 0:48].rearrange("p (br k c) -> p br k c", br=3, k=2)
            abc = ab_coef[:].rearrange("p (br k c) -> p br k c", br=3, k=2)
            abt = ab_tmp[:].rearrange("p (g c) -> p g c", g=3)
            sums3 = sgl[:, :, 0, :]
            sqs3 = sgl[:, :, 1, :]
            gam3 = vec3[:, :, 0, :]
            bet3 = vec3[:, :, 1, :]
            A3 = abc[:, :, 0, :]
            B3 = abc[:, :, 1, :]
            mean3 = abt[:, 0, :].rearrange("p (br c) -> p br c", br=1)[:, 0, :]
            mean3 = ab_tmp[:, 0:8]
            mean_t = ab_tmp[:, 0:24].rearrange("p (br c) -> p br c", br=3)
            ee_t = stats_tmp[:, 0:8]
            P.emit("dve", [lambda e: e.tensor_scalar_mul(
                mean_t, sums3, 1.0 / NTOK_GLOBAL)],
                reads=[b_stgl], writes=[b_abt])
            P.emit("dve", [lambda e: e.tensor_scalar_mul(
                B3, sqs3, 1.0 / NTOK_GLOBAL)],
                reads=[b_stgl], writes=[b_ab])
            P.emit("dve", [lambda e: e.scalar_tensor_tensor(
                A3, mean_t, 1.0, mean_t, ALU.mult, ALU.mult)],
                reads=[b_abt], writes=[b_ab])
            P.emit("dve", [lambda e: e.tensor_sub(B3, B3, A3)],
                   reads=[b_ab], writes=[b_ab])
            P.emit("act", [lambda e: e.activation(
                out=ab_coef[:].rearrange("p (br k c) -> p br k c",
                                         br=3, k=2)[:, :, 1, :],
                in_=ab_coef[:].rearrange("p (br k c) -> p br k c",
                                         br=3, k=2)[:, :, 1, :],
                func=AF.Sqrt, bias=eps_t[:, 0:1])],
                reads=[b_ab, b_eps], writes=[b_ab])
            P.emit("dve", [lambda e: e.reciprocal(A3, B3)],
                   reads=[b_ab], writes=[b_ab])
            P.emit("dve", [lambda e: e.tensor_mul(A3, A3, gam3)],
                   reads=[b_ab, b_vecs], writes=[b_ab])
            P.emit("dve", [lambda e: e.tensor_mul(mean_t, mean_t, A3)],
                   reads=[b_ab, b_abt], writes=[b_abt])
            P.emit("dve", [lambda e: e.tensor_scalar(
                B3, bet3, 4.0, -4.0, ALU.mult, ALU.add)],
                reads=[b_vecs, b_ab], writes=[b_ab])
            P.emit("dve", [lambda e: e.scalar_tensor_tensor(
                B3, mean_t, -4.0, B3, ALU.mult, ALU.add)],
                reads=[b_abt, b_ab], writes=[b_ab])
            P.emit("dve", [lambda e: e.tensor_scalar_mul(A3, A3, 4.0)],
                   reads=[b_ab], writes=[b_ab])


        d_stage = qk_bufs[:, 0:4096]
        d_qv = qk_bufs[:, 4096:8192]
        d_k = d_extra[:, 4096:8192]
        sig_dst = [d_qv, d_k, d_qv]
        sig_dbuf = [b_qkbuf[1], b_dx[1], b_qkbuf[1]]

        def d_load(t, br):
            srch = h_sc[br * D:(br + 1) * D, :].rearrange(
                "(dc p) n -> p dc n", p=128)
            for half in range(2):
                P.emit("sync", [lambda e, s=srch, half=half, t=t, br=br:
                                e.dma_start(
                    out=d_stage[:, half * 2048:half * 2048 + 2048]
                    .rearrange("p (dc n) -> p dc n", dc=4),
                    in_=s[:, half * 4:half * 4 + 4,
                          t * TILE:(t + 1) * TILE])],
                    reads=[b_hsc[br][t]], writes=[b_qkbuf[0]])

        def d_sigs(t, br):
            for dc in range(DC):
                P.emit("act", [lambda e, br=br, dc=dc: e.activation(
                    out=sig_dst[br][:, dc * 512:(dc + 1) * 512],
                    in_=d_stage[:, dc * 512:(dc + 1) * 512],
                    func=AF.Sigmoid,
                    scale=ab_coef[:, br * 16 + dc:br * 16 + dc + 1],
                    bias=ab_coef[:, br * 16 + 8 + dc:
                                 br * 16 + 8 + dc + 1])],
                    reads=[b_qkbuf[0], b_ab], writes=[sig_dbuf[br]])

        def d_qkmul(t):
            P.emit("dve", [lambda e: e.tensor_mul(
                d_stage[:], d_qv[:], d_k[:])],
                reads=[b_qkbuf[1], b_dx[1]], writes=[b_qkbuf[0]])

        def d_stq(t, which):
            srcb, dstd, btok, bsrc = (
                (d_stage, qk_sc, b_qksc[t], b_qkbuf[0]) if which == 0 else
                (d_qv, v_sc, b_vsc[t], b_qkbuf[1]))
            dd = dstd.rearrange("(dc p) n -> p dc n", p=128)
            for half in range(2):
                P.emit("sync", [lambda e, s=srcb, d=dd, half=half, t=t:
                                e.dma_start(
                    out=d[:, half * 4:half * 4 + 4,
                          t * TILE:(t + 1) * TILE],
                    in_=s[:, half * 2048:half * 2048 + 2048].rearrange(
                        "p (dc n) -> p dc n", dc=4))],
                    reads=[bsrc], writes=[btok])

        def d_vtm(t):
            for c in range(4):
                reg = d_extra[:, c * 1024:c * 1024 + 1024]
                P.emit("actdma", [lambda e, reg=reg, t=t, c=c:
                                  e.dma_start_transpose(
                    reg, v_sc[:, t * TILE + c * 128:
                              t * TILE + (c + 1) * 128])],
                    reads=[b_vsc[t]], writes=[b_dx[0]])

        def d_vv(t):
            paro = pso_cnt[0] % 2
            pso_cnt[0] += 1
            fns = []
            for c in range(4):
                reg = d_extra[:, c * 1024:c * 1024 + 1024]
                for hp in range(8):
                    fns.append(lambda e, reg=reg, hp=hp, c=c, paro=paro:
                               e.matmul(
                        ps_out[paro][:, hp * 128:(hp + 1) * 128],
                        reg[:, hp * 128:(hp + 1) * 128],
                        reg[:, hp * 128:(hp + 1) * 128],
                        start=(c == 0 and hp in (0, 4)),
                        stop=(c == 3 and hp in (3, 7)),
                        skip_group_check=True))
            P.emit("pe", fns, reads=[b_dx[0]], writes=[b_psout[paro]])
            P.emit("dve", [lambda e, paro=paro: e.tensor_add(
                vv_sb[:], vv_sb[:], ps_out[paro][:])],
                reads=[b_psout[paro], b_vvsb], writes=[b_vvsb])

        def d_bdvv():
            P.emit("dve", [lambda e: e.memset(bd_vv[:], 0.0)],
                   reads=[b_dx[1]], writes=[b_bdvv])
            for dc in range(DC):
                for half in range(2):
                    P.emit("dve", [lambda e, dc=dc, half=half:
                                   e.tensor_scalar_mul(
                        bd_vv[half * 64:half * 64 + 64,
                              dc * 128 + half * 64:
                              dc * 128 + half * 64 + 64],
                        vv_sb[half * 64:half * 64 + 64,
                              dc * 128 + half * 64:
                              dc * 128 + half * 64 + 64], 0.125)],
                        reads=[b_vvsb], writes=[b_bdvv])

        def d_sched(gp):
            t, step = gp // 5, gp % 5
            if t >= NT:
                if gp == NT * 5:
                    d_bdvv()
                return
            if step == 0:
                d_load(t, 0)
                d_sigs(t, 0)
                d_load(t, 1)
            elif step == 1:
                d_sigs(t, 1)
                d_qkmul(t)
                d_stq(t, 0)
            elif step == 2:
                d_load(t, 2)
                d_sigs(t, 2)
                d_stq(t, 1)
            elif step == 3:
                d_vtm(t)
            else:
                d_vv(t)

        seed(b_hraw, [b_hD[0]])
        seed(b_hscl[0], [b_hD[1]])
        seed(b_hscl[1], [b_hD[2]])
        seed(b_acc[:4], [b_xin1])

        h_raw = h_arena[:, 0:4096]
        h_scl = [h_arena[:, 4096:8192], h_arena[:, 8192:12288]]
        attn_fm = d_extra[:, 0:4096]
        b_attnfm = b_dx[0]
        pso_cnt = [0]
        hsl_cnt = [0]

        pair_defs = []
        for q in range(NQ):
            pair_defs.append(("pair", sWcat))
            for e_i in range(NEXP):
                pair_defs.append(
                    ("pair", eWcat[e_i:e_i + 1, :, :].rearrange(
                        "one p m -> p (one m)")))
        pair_defs.append(("single", wpTp))
        pair_slots = []

        def emit_pair_load(g):
            base = g % 2
            kind, ap = pair_defs[g]
            if kind == "pair":
                load_pair(base, ap, b_wpair[base])
            else:
                load_slot(base * 2, ap, b_wpair[base])
            pair_slots.append((base * 2, base * 2 + 1, b_wpair[base]))

        emit_pair_load(0)
        emit_pair_load(1)

        def f_first_and_h(p, tq, slot_a, wbuf, shared, hsl):
            ti = tq % TPQ
            toff = ti * TILE
            wslot = w_ring[:, slot_a * 8192:(slot_a + 1) * 8192]
            if not shared:
                P.emit("pe", [lambda e, p=p, ti=ti: e.matmul(
                    ps_wbc[:, 0:512],
                    rowhot_sb[:, (p - 1) * 128:p * 128],
                    w_fm_sb[0:15, ti * 512:(ti + 1) * 512],
                    start=True, stop=True)],
                    reads=[b_rowhot, b_wfm[ti]], writes=[b_pswbc])
                P.emit("dve", [lambda e: e.tensor_copy(
                    wbc_sb[:], ps_wbc[:, 0:512])],
                    reads=[b_pswbc], writes=[b_wbc])
            dstbufs = []
            for ec in range(DC):
                par = psh_cnt[0] % 2
                psh_cnt[0] += 1
                fns = []
                for dc in range(DC):
                    fns.append(lambda e, w=wslot, dc=dc, ec=ec, toff=toff,
                               par=par:
                               e.matmul(
                        ps_h[par][:, 0:512],
                        w[:, dc * 1024 + ec * 128:dc * 1024 + ec * 128 + 128],
                        xn_q[:, dc * QT + toff:dc * QT + toff + 512],
                        start=(dc == 0), stop=(dc == DC - 1)))
                P.emit("pe", fns, reads=[wbuf] + b_xnq,
                       writes=[b_psh[par]])
                bcol = 48 if shared else (56 + (p - 1) * 8)
                P.emit("act", [lambda e, ec=ec, par=par, bcol=bcol:
                               e.activation(
                    out=h_raw[:, ec * 512:(ec + 1) * 512],
                    in_=ps_h[par][:, 0:512], func=AF.Relu,
                    bias=vecs_sb[:, bcol + ec:bcol + ec + 1])],
                    reads=[b_psh[par], b_vecs], writes=[b_hraw[ec]])
                if shared:
                    dstbufs.append(b_hraw[ec])
                else:
                    P.emit("dve", [lambda e, ec=ec, hsl=hsl: e.tensor_mul(
                        h_scl[hsl][:, ec * 512:(ec + 1) * 512],
                        h_raw[:, ec * 512:(ec + 1) * 512], wbc_sb[:])],
                        reads=[b_hraw[ec], b_wbc], writes=[b_hscl[hsl][ec]])
                    dstbufs.append(b_hscl[hsl][ec])
            return dstbufs

        def f_second(lhs_t, lhs_bufs, slot_b, wbuf, tq, first_pass):
            ti = tq % TPQ
            wslot = w_ring[:, slot_b * 8192:(slot_b + 1) * 8192]
            for c in range(4):
                paro = pso_cnt[0] % 2
                pso_cnt[0] += 1
                fns = []
                for hc in range(DC):
                    for half in range(2):
                        fns.append(lambda e, w=wslot, hc=hc, half=half, c=c,
                                   paro=paro:
                                   e.matmul(
                            ps_out[paro][:, half * 512:half * 512 + 512],
                            lhs_t[:, hc * 512 + c * 128:
                                  hc * 512 + c * 128 + 128],
                            w[:, hc * 1024 + half * 512:
                              hc * 1024 + half * 512 + 512],
                            start=(hc == 0), stop=(hc == DC - 1)))
                P.emit("pe", fns, reads=lhs_bufs + [wbuf],
                       writes=[b_psout[paro]])
                aslab = acc[:, (ti * 4 + c) * 1024:(ti * 4 + c + 1) * 1024]
                ab = b_acc[ti * 4 + c]
                if first_pass:
                    P.emit("dve", [lambda e, a=aslab, paro=paro:
                                   e.tensor_copy(a, ps_out[paro][:])],
                           reads=[b_psout[paro]], writes=[ab])
                else:
                    P.emit("dve", [lambda e, a=aslab, paro=paro:
                                   e.tensor_add(a, a, ps_out[paro][:])],
                           reads=[b_psout[paro], ab], writes=[ab])

        for q in range(NQ):
            qreads = [b_xnsc[q * TPQ + i] for i in range(TPQ)]
            for dc in range(DC):
                P.emit("actdma", [lambda e, q=q, dc=dc:
                                  e.dma_start_transpose(
                    xn_q[:, dc * QT:(dc + 1) * QT],
                    xn_sc[q * QT:(q + 1) * QT, dc * 128:(dc + 1) * 128])],
                    reads=qreads, writes=[b_xnq[dc]])
            if q == 0:
                emit_phase_c()
            for ti in range(TPQ):
                tq0 = q * TPQ + ti
                for c in range(4):
                    P.emit("pe", [lambda e, tq0=tq0, c=c: e.transpose(
                        ps_tr[0:16, :],
                        w_tm_all[:, (tq0 * 4 + c) * 16:
                                 (tq0 * 4 + c) * 16 + 16],
                        ident_sb[:])],
                        reads=[b_wtmall, b_ident], writes=[b_pstr])
                    P.emit("dve", [lambda e, c=c, ti=ti: e.tensor_copy(
                        w_fm_sb[0:16, ti * 512 + c * 128:
                                ti * 512 + (c + 1) * 128],
                        ps_tr[0:16, :])],
                        reads=[b_pstr], writes=[b_wfm[ti]])
            for p in range(16):
                g = q * 16 + p
                for ti in range(TPQ):
                    tq = q * TPQ + ti
                    sa, sb, wb = pair_slots[g]
                    shared = (p == 0)
                    hsl = hsl_cnt[0] % 2
                    hsl_cnt[0] += 1
                    lhs_bufs = f_first_and_h(p, tq, sa, wb, shared, hsl)
                    lhs_t = h_raw if shared else h_scl[hsl]
                    f_second(lhs_t, lhs_bufs, sb, wb, tq,
                             first_pass=shared)
                d_sched(g)
                if g + 2 < len(pair_defs):
                    emit_pair_load(g + 2)
            for ti in range(TPQ):
                tq = q * TPQ + ti
                src = x_in[tq * TILE:(tq + 1) * TILE, :].rearrange(
                    "(c p) d -> p c d", p=128)
                P.emit("sync", [lambda e, s=src: e.dma_start(
                    out=x_sb[:].rearrange("p (c d) -> p c d", c=4), in_=s)],
                    writes=[b_xsb])
                P.emit("dve", [lambda e, ti=ti: e.tensor_add(
                    x_sb[:], acc[:, ti * 4096:(ti + 1) * 4096], x_sb[:])],
                    reads=[b_acc[ti * 4], b_acc[ti * 4 + 1],
                           b_acc[ti * 4 + 2], b_acc[ti * 4 + 3], b_xsb],
                    writes=[b_xsb])
                dsty = y_out[tq * TILE:(tq + 1) * TILE, :].rearrange(
                    "(c p) d -> p c d", p=128)
                P.emit("sync", [lambda e, d=dsty: e.dma_start(
                    out=d, in_=x_sb[:].rearrange("p (c d) -> p c d", c=4))],
                    reads=[b_xsb], writes=[b_xsb, b_ysc[tq]])

        wp_slot = pair_slots[NQ * 16][0]
        wp_buf = pair_slots[NQ * 16][2]
        wslot_wp = w_ring[:, wp_slot * 8192:(wp_slot + 1) * 8192]
        srcqk = qk_sc.rearrange("(dc p) n -> p dc n", p=128)
        y_bufs = [x_sb[:, 0:4096], acc[:, 0:4096]]
        by_bufs = [b_xsb, b_acc[0]]
        seed([b_acc[0]], b_acc)
        for t in range(NT):
            qkb = qk_bufs[:, (t % 2) * 4096:(t % 2) * 4096 + 4096]
            bqkb = b_qkbuf[t % 2]
            yb, byb = y_bufs[t % 2], by_bufs[t % 2]
            for half in range(2):
                P.emit("sync", [lambda e, s=srcqk, half=half, t=t, qkb=qkb:
                                e.dma_start(
                    out=qkb[:, half * 2048:half * 2048 + 2048]
                    .rearrange("p (dc n) -> p dc n", dc=4),
                    in_=s[:, half * 4:half * 4 + 4,
                          t * TILE:(t + 1) * TILE])],
                    reads=[b_qksc[t]], writes=[bqkb])
            srcy = y_out[t * TILE:(t + 1) * TILE, :].rearrange(
                "(c p) d -> p c d", p=128)
            P.emit("sync", [lambda e, s=srcy, yb=yb: e.dma_start(
                out=yb.rearrange("p (c d) -> p c d", c=4), in_=s)],
                reads=[b_ysc[t]], writes=[byb])
            for c in range(4):
                P.emit("pe", [lambda e, t=t, c=c: e.transpose(
                    ps_tr[0:16, :],
                    w_tm_all[:, (t * 4 + c) * 16:(t * 4 + c) * 16 + 16],
                    ident_sb[:])],
                    reads=[b_wtmall, b_ident], writes=[b_pstr])
                P.emit("dve", [lambda e, c=c, t=t: e.tensor_copy(
                    w_fm_sb[0:16, (t % 2) * 512 + c * 128:
                            (t % 2) * 512 + (c + 1) * 128],
                    ps_tr[0:16, :])],
                    reads=[b_pstr], writes=[b_wfm[t % 2]])
            for dc in range(DC):
                par = psh_cnt[0] % 2
                psh_cnt[0] += 1
                P.emit("pe", [lambda e, dc=dc, par=par, qkb=qkb: e.matmul(
                    ps_h[par][:, 0:512], bd_vv[:, dc * 128:(dc + 1) * 128],
                    qkb[:, dc * 512:(dc + 1) * 512], start=True, stop=True)],
                    reads=[b_bdvv, bqkb], writes=[b_psh[par]])
                P.emit("act", [lambda e, dc=dc, par=par: e.activation(
                    out=attn_fm[:, dc * 512:(dc + 1) * 512],
                    in_=ps_h[par][:, 0:512], func=AF.Copy)],
                    reads=[b_psh[par]], writes=[b_attnfm])
            for c in range(4):
                paro = pso_cnt[0] % 2
                pso_cnt[0] += 1
                fns = []
                for hc in range(DC):
                    for half in range(2):
                        fns.append(lambda e, hc=hc, half=half, c=c,
                                   paro=paro:
                                   e.matmul(
                            ps_out[paro][:, half * 512:half * 512 + 512],
                            attn_fm[:, hc * 512 + c * 128:
                                    hc * 512 + c * 128 + 128],
                            wslot_wp[:, hc * 1024 + half * 512:
                                     hc * 1024 + half * 512 + 512],
                            start=(hc == 0), stop=False))
                for half in range(2):
                    fns.append(lambda e, half=half, c=c, paro=paro, t=t:
                               e.matmul(
                        ps_out[paro][:, half * 512:half * 512 + 512],
                        w_fm_sb[0:15, (t % 2) * 512 + c * 128:
                                (t % 2) * 512 + (c + 1) * 128],
                        eb2_sb[0:15, half * 512:half * 512 + 512],
                        start=False, stop=False))
                for half in range(2):
                    fns.append(lambda e, half=half, paro=paro:
                               e.matmul(
                        ps_out[paro][:, half * 512:half * 512 + 512],
                        ones_sb[0:1, 0:128],
                        bias_bf[0:1, half * 512:half * 512 + 512],
                        start=False, stop=True))
                P.emit("pe", fns,
                       reads=[b_attnfm, wp_buf, b_wfm[t % 2], b_eb2,
                              b_ones, b_bias],
                       writes=[b_psout[paro]])
                P.emit("dve", [lambda e, c=c, paro=paro, yb=yb: e.tensor_add(
                    yb[:, c * 1024:(c + 1) * 1024],
                    yb[:, c * 1024:(c + 1) * 1024], ps_out[paro][:])],
                    reads=[b_psout[paro], byb], writes=[byb])
            dsty = y_out[t * TILE:(t + 1) * TILE, :].rearrange(
                "(c p) d -> p c d", p=128)
            P.emit("sync", [lambda e, d=dsty, yb=yb: e.dma_start(
                out=d, in_=yb.rearrange("p (c d) -> p c d", c=4))],
                reads=[byb], writes=[byb, b_ysc[t]])

        P.final_wait("sync", [("qsp%d" % i, P.cnt["qsp%d" % i])
                              for i in range(NQSEM)
                              if P.cnt["qsp%d" % i] > 0])
        P.flush()
    return nc


def kernel(L0_attn_x, L0_attn_Wq, L0_attn_gq, L0_attn_bq, L0_attn_Wk,
           L0_attn_gk, L0_attn_bk, L0_attn_Wv, L0_attn_gv, L0_attn_bv,
           L0_attn_Wp, L0_attn_bp, L0_ffn_router_W, L0_ffn_router_b,
           L0_ffn_sW1, L0_ffn_sb1, L0_ffn_sW2, L0_ffn_sb2, L0_ffn_eW1,
           L0_ffn_eb1, L0_ffn_eW2, L0_ffn_eb2):
    f = lambda a: np.ascontiguousarray(np.asarray(a, dtype=np.float32))
    x = f(L0_attn_x)
    B, T, _ = x.shape

    def pc(v):
        return np.ascontiguousarray(f(v).reshape(8, 128).T)

    vecs = np.concatenate(
        [pc(L0_attn_gq), pc(L0_attn_bq), pc(L0_attn_gk), pc(L0_attn_bk),
         pc(L0_attn_gv), pc(L0_attn_bv), pc(L0_ffn_sb1),
         np.ascontiguousarray(
             np.transpose(f(L0_ffn_eb1).reshape(15, 8, 128), (2, 0, 1))
             .reshape(128, 120))],
        axis=1)
    rows = np.zeros((1, 2064), dtype=np.float32)
    rows[0, 0:1024] = f(L0_attn_bp)
    rows[0, 1024:2048] = f(L0_ffn_sb2)
    rows[0, 2048:2063] = f(L0_ffn_router_b)

    def pm(w):
        w = f(w)
        return np.ascontiguousarray(
            w.reshape(8, 128, w.shape[1]).transpose(1, 0, 2).reshape(128, -1))

    eW1f, eW2f = f(L0_ffn_eW1), f(L0_ffn_eW2)
    eWcat = np.ascontiguousarray(np.stack(
        [np.concatenate([pm(eW1f[e]), pm(eW2f[e])], axis=1)
         for e in range(15)], axis=0))
    common = {
        "wqTp": pm(f(L0_attn_Wq).T),
        "wkTp": pm(f(L0_attn_Wk).T),
        "wvTp": pm(f(L0_attn_Wv).T),
        "wpTp": pm(f(L0_attn_Wp).T),
        "sWcat": np.concatenate([pm(L0_ffn_sW1), pm(L0_ffn_sW2)], axis=1),
        "eWcat": eWcat,
        "routW": f(L0_ffn_router_W),
        "eb2": f(L0_ffn_eb2),
        "vecs": vecs,
        "rows": rows,
    }
    in_maps = []
    for b in range(B):
        m = dict(common)
        m["x"] = np.ascontiguousarray(x[b])
        in_maps.append(m)

    nc = build_nc(T)
    res = run_bass_kernel_spmd(nc, in_maps, core_ids=list(range(B)))
    y = np.stack([res.results[i]["y"] for i in range(B)], axis=0)
    return y


# version 20
# speedup vs baseline: 1.0123x; 1.0123x over previous
import numpy as np
import ml_dtypes

import concourse.bass as bass
import concourse.mybir as mybir
from concourse.bass_utils import run_bass_kernel_spmd

F32 = mybir.dt.float32
BF16 = mybir.dt.bfloat16
AF = mybir.ActivationFunctionType
ALU = mybir.AluOpType
AX = mybir.AxisListType

D = 1024
DC = 8
NEXP = 15
NCORES = 8
TILE = 512
EPS = 1e-5


class Buf:
    __slots__ = ("ready", "readers")

    def __init__(self):
        self.ready = None
        self.readers = []


def seed(dst_bufs, src_bufs):
    toks = []
    for s in src_bufs:
        if s.ready is not None:
            toks.append(s.ready)
        toks.extend(s.readers)
    for d in dst_bufs:
        d.ready = None
        d.readers = list(toks)


NQSEM = 8


class Prog:
    ENG = {
        "pe": ("tensor", "pe", 1, 1),
        "act": ("scalar", "act", 1, 1),
        "dve": ("vector", "dve", 1, 1),
        "sync": ("sync", "qsp", 16, NQSEM),
        "actdma": ("scalar", "qact", 16, NQSEM),
        "gps": ("gpsimd", "qgps", 16, NQSEM),
        "cc": ("gpsimd", "cc", 1, 1),
    }

    def __init__(self, block, sems):
        self.block = block
        self.sems = sems
        self.cnt = {k: 0 for k in sems}
        self.wm = {}
        self.qidx = {}
        self.streams = {s: [] for s in
                        ("tensor", "scalar", "vector", "sync", "gpsimd")}

    def emit(self, eng, fns, reads=(), writes=(), waits=()):
        sec_name, semkey, inc, ring = self.ENG[eng]
        if ring > 1:
            i = self.qidx.get(eng, 0)
            self.qidx[eng] = i + 1
            semkey = "%s%d" % (semkey, i % ring)
        req = {}

        def add(tok):
            if tok is None:
                return
            sk, v = tok
            if req.get(sk, -1) < v:
                req[sk] = v

        for b in reads:
            add(b.ready)
        for b in writes:
            add(b.ready)
            for t in b.readers:
                add(t)
        for t in waits:
            add(t)
        if inc == 16 and self.cnt[semkey] > 0:
            add((semkey, self.cnt[semkey]))
        wm = self.wm.setdefault(sec_name, {})
        todo = [(sk, v) for sk, v in req.items() if wm.get(sk, 0) < v]
        self.cnt[semkey] += inc
        tok = (semkey, self.cnt[semkey])
        self.streams[sec_name].append((todo, fns, semkey, inc))
        for sk, v in todo:
            if wm.get(sk, 0) < v:
                wm[sk] = v
        for b in reads:
            b.readers.append(tok)
        for b in writes:
            b.ready = tok
            b.readers = []
        return tok

    def final_wait(self, eng, waits):
        sec_name = self.ENG[eng][0]
        self.streams[sec_name].append((list(waits), [], None, 0))

    def flush(self):
        sems = self.sems
        for sec_name, items in self.streams.items():
            if not items:
                continue
            sec = getattr(self.block, sec_name)

            @sec
            def _(eng_obj, items=items):
                for todo, fns, semkey, inc in items:
                    for sk, v in todo:
                        eng_obj.wait_ge(sems[sk], v)
                    inst = None
                    for fn in fns:
                        inst = fn(eng_obj)
                    if semkey is not None:
                        inst.then_inc(sems[semkey], inc)


def build_nc(T):
    assert T % TILE == 0
    NT = T // TILE
    QT = min(1024, T)
    NQ = T // QT
    TPQ = QT // TILE
    NTOK_GLOBAL = float(NCORES * T)

    nc = bass.Bass("TRN2", target_bir_lowering=False)

    x_in = nc.dram_tensor("x", [T, D], F32, kind="ExternalInput").ap()
    wqTp = nc.dram_tensor("wqTp", [128, 8192], F32, kind="ExternalInput").ap()
    wkTp = nc.dram_tensor("wkTp", [128, 8192], F32, kind="ExternalInput").ap()
    wvTp = nc.dram_tensor("wvTp", [128, 8192], F32, kind="ExternalInput").ap()
    wpTp = nc.dram_tensor("wpTp", [128, 8192], F32, kind="ExternalInput").ap()
    sWcat = nc.dram_tensor("sWcat", [128, 16384], F32, kind="ExternalInput").ap()
    eWcat = nc.dram_tensor("eWcat", [NEXP, 128, 16384], F32,
                           kind="ExternalInput").ap()
    routW = nc.dram_tensor("routW", [D, NEXP], F32, kind="ExternalInput").ap()
    eb2_in = nc.dram_tensor("eb2", [NEXP, D], F32, kind="ExternalInput").ap()
    vecs_in = nc.dram_tensor("vecs", [128, 176], F32, kind="ExternalInput").ap()
    rows_in = nc.dram_tensor("rows", [1, 2064], F32, kind="ExternalInput").ap()
    y_out = nc.dram_tensor("y", [T, D], F32, kind="ExternalOutput").ap()

    xn_sc = nc.dram_tensor("xn_sc", [T, D], BF16, kind="Internal").ap()
    h_sc = nc.dram_tensor("h_sc", [3 * D, T], BF16, kind="Internal").ap()
    qk_sc = nc.dram_tensor("qk_sc", [D, T], BF16, kind="Internal").ap()
    v_sc = nc.dram_tensor("v_sc", [D, T], BF16, kind="Internal").ap()
    cc_in = nc.dram_tensor("cc_in", [128, 48], F32, kind="Internal").ap()
    cc_out = nc.dram_tensor("cc_out", [128, 48], F32, kind="Internal").ap()

    ident_c = nc.inline_tensor(np.eye(128, dtype=ml_dtypes.bfloat16),
                               name="identc").ap()
    ones_c = nc.inline_tensor(np.ones((1, 128), dtype=ml_dtypes.bfloat16),
                              name="onesc").ap()
    rowhot_np = np.zeros((NEXP, NEXP * 128), dtype=ml_dtypes.bfloat16)
    for e in range(NEXP):
        rowhot_np[e, e * 128:(e + 1) * 128] = 1.0
    rowhot_c = nc.inline_tensor(rowhot_np, name="rowhotc").ap()

    from contextlib import ExitStack
    with ExitStack() as ctx:
        en = ctx.enter_context
        w_ring = en(nc.sbuf_tensor("w_ring", [128, 4 * 8192], BF16))
        acc = en(nc.sbuf_tensor("acc", [128, max(TPQ * 4096, 5632)], F32))
        xn_q = en(nc.sbuf_tensor("xn_q", [128, max(DC * QT, 8192)], BF16))
        h_arena = en(nc.sbuf_tensor("h_arena", [128, 12288], BF16))
        qk_bufs = en(nc.sbuf_tensor("qk_bufs", [128, 8192], BF16))
        d_extra = en(nc.sbuf_tensor("d_extra", [128, 8192], BF16))
        x_sb = en(nc.sbuf_tensor("x_sb", [128, 4096], F32))
        sq_junk = en(nc.sbuf_tensor("sq_junk", [128, 512], BF16))
        vecs_sb = en(nc.sbuf_tensor("vecs_sb", [128, 176], F32))
        stats_acc = en(nc.sbuf_tensor("stats_acc", [128, 48], F32))
        stats_tmp = en(nc.sbuf_tensor("stats_tmp", [128, 8], F32))
        stats_gl = en(nc.sbuf_tensor("stats_gl", [128, 48], F32))
        ab_coef = en(nc.sbuf_tensor("ab_coef", [128, 48], F32))
        ab_tmp = en(nc.sbuf_tensor("ab_tmp", [128, 24], F32))
        ln_t = en(nc.sbuf_tensor("ln_t", [128, 16], F32))
        bias_bf = en(nc.sbuf_tensor("bias_bf", [1, 1024], BF16))
        rbrow_bf = en(nc.sbuf_tensor("rbrow_bf", [1, 16], BF16))
        rb_bc = en(nc.sbuf_tensor("rb_bc", [128, 15], F32))
        eb2_sb = en(nc.sbuf_tensor("eb2_sb", [15, 1024], BF16))
        routW_sb = en(nc.sbuf_tensor("routW_sb", [128, 8 * 15], BF16))
        vv_sb = en(nc.sbuf_tensor("vv_sb", [128, 1024], F32))
        w_tm_all = en(nc.sbuf_tensor("w_tm_all", [128, 512], BF16))
        w_fm_sb = en(nc.sbuf_tensor("w_fm_sb", [16, 1536], BF16))
        bd_vv = en(nc.sbuf_tensor("bd_vv", [128, 1024], BF16))
        w_tm_t = en(nc.sbuf_tensor("w_tm_t", [128, 16], BF16))
        wbc_sb = en(nc.sbuf_tensor("wbc_sb", [128, 512], BF16))
        l_sb = en(nc.sbuf_tensor("l_sb", [128, 64], F32))
        sm_a = en(nc.sbuf_tensor("sm_a", [128, 16], F32))
        sm_b = en(nc.sbuf_tensor("sm_b", [128, 16], F32))
        sm_c = en(nc.sbuf_tensor("sm_c", [128, 16], F32))
        sm_t = en(nc.sbuf_tensor("sm_t", [128, 8], F32))
        eps_t = en(nc.sbuf_tensor("eps_t", [128, 1], F32))
        c125_t = en(nc.sbuf_tensor("c125_t", [128, 1], F32))
        ident_sb = en(nc.sbuf_tensor("ident_sb", [128, 128], BF16))
        ones_sb = en(nc.sbuf_tensor("ones_sb", [1, 128], BF16))
        rowhot_sb = en(nc.sbuf_tensor("rowhot_sb", [15, 15 * 128], BF16))

        ps_out0 = en(nc.psum_tensor("ps_out0", [128, 1024], F32))
        ps_out1 = en(nc.psum_tensor("ps_out1", [128, 1024], F32))
        ps_h0 = en(nc.psum_tensor("ps_h0", [128, 512], F32))
        ps_h1 = en(nc.psum_tensor("ps_h1", [128, 512], F32))
        ps_wbc = en(nc.psum_tensor("ps_wbc", [128, 512], F32))
        ps_tr = en(nc.psum_tensor("ps_tr", [128, 128], BF16))

        xsq_junk = acc[:, 4096:5120]
        sq_junk = acc[:, 5120:5632]
        sems = {}
        for k in ("pe", "act", "dve", "cc"):
            sems[k] = en(nc.semaphore(k))
        for base in ("qsp", "qact", "qgps"):
            for i in range(NQSEM):
                k = "%s%d" % (base, i)
                sems[k] = en(nc.semaphore(k))
        block = en(nc.Block())
        P = Prog(block, sems)

        ps_out = [ps_out0, ps_out1]
        ps_h = [ps_h0, ps_h1]
        b_psout = [Buf(), Buf()]
        b_psh = [Buf(), Buf()]
        b_pswbc = Buf()
        b_pstr = Buf()
        b_xnq = [Buf() for _ in range(DC)]
        b_acc = [Buf() for _ in range(TPQ * 4)]
        b_hraw = [Buf() for _ in range(DC)]
        b_hscl = [[Buf() for _ in range(DC)], [Buf() for _ in range(DC)]]
        b_hD = [Buf(), Buf(), Buf()]
        b_qkbuf = [Buf(), Buf()]
        b_dx = [Buf(), Buf()]
        b_xsb = Buf()
        b_xin1 = Buf()
        b_xsq = Buf()
        b_sqj = Buf()
        b_vecs = Buf()
        b_stats = Buf()
        b_stmp = [Buf() for _ in range(4)]
        b_stgl = Buf()
        b_ab = Buf()
        b_abt = Buf()
        b_lnt = Buf()
        b_rows = Buf()
        b_bias = Buf()
        b_rbrow = Buf()
        b_rbbc = Buf()
        b_eb2 = Buf()
        b_routW = Buf()
        b_bdvv = Buf()
        b_vvsb = Buf()
        b_wtmall = Buf()
        b_wfm = [Buf(), Buf(), Buf()]
        b_wtm = Buf()
        b_wbc = Buf()
        b_lsb = Buf()
        b_sma = Buf()
        b_smb = Buf()
        b_smc = Buf()
        b_smt = Buf()
        b_ident = Buf()
        b_eps = Buf()
        b_c125 = Buf()
        b_ones = Buf()
        b_rowhot = Buf()
        b_xnsc = [Buf() for _ in range(NT)]
        b_hsc = [[Buf() for _ in range(NT)] for _ in range(3)]
        b_qksc = [Buf() for _ in range(NT)]
        b_vsc = [Buf() for _ in range(NT)]
        b_ysc = [Buf() for _ in range(NT)]
        b_ccin = Buf()
        b_ccout = Buf()

        P.emit("sync", [lambda e: e.dma_start(out=vecs_sb[:], in_=vecs_in)],
               writes=[b_vecs])
        rows_sb = x_sb[0:1, 0:2064]
        P.emit("sync", [lambda e: e.dma_start(out=rows_sb, in_=rows_in)],
               writes=[b_rows])
        P.emit("sync", [lambda e: e.dma_start(out=ident_sb[:], in_=ident_c)],
               writes=[b_ident])
        P.emit("sync", [lambda e: e.dma_start(out=ones_sb[:], in_=ones_c)],
               writes=[b_ones])
        P.emit("sync", [lambda e: e.dma_start(out=rowhot_sb[:], in_=rowhot_c)],
               writes=[b_rowhot])
        P.emit("gps", [lambda e: e.dma_start(out=eb2_sb[:], in_=eb2_in)],
               writes=[b_eb2])
        P.emit("gps", [lambda e: e.dma_start(
            out=routW_sb[:].rearrange("p (dc e) -> p dc e", dc=DC),
            in_=routW.rearrange("(dc p) e -> p dc e", p=128))],
            writes=[b_routW])

        def load_slot(slot, src2, wbuf):
            P.emit("gps", [lambda e, slot=slot, s=src2: e.dma_start(
                out=w_ring[:, slot * 8192:(slot + 1) * 8192], in_=s)],
                writes=[wbuf])

        def load_pair(base, srccat, wbuf):
            P.emit("gps", [lambda e, base=base, s=srccat: e.dma_start(
                out=w_ring[:, base * 16384:(base + 1) * 16384], in_=s)],
                writes=[wbuf])

        b_wpair = [Buf(), Buf()]
        load_slot(0, wqTp, b_wpair[0])
        load_slot(1, wkTp, b_wpair[0])
        load_slot(2, wvTp, b_wpair[1])

        P.emit("dve", [lambda e: e.tensor_add(
            xsq_junk[0:1, 0:1024], rows_sb[:, 0:1024],
            rows_sb[:, 1024:2048])],
            reads=[b_rows], writes=[b_xsq])
        P.emit("act", [lambda e: e.activation(
            out=bias_bf[0:1, :], in_=xsq_junk[0:1, 0:1024], func=AF.Copy)],
            reads=[b_xsq], writes=[b_bias])
        P.emit("act", [lambda e: e.activation(
            out=rbrow_bf[0:1, 0:15], in_=rows_sb[:, 2048:2063],
            func=AF.Copy)],
            reads=[b_rows], writes=[b_rbrow])
        P.emit("pe", [lambda e: e.matmul(
            ps_wbc[:, 0:15], ones_sb[0:1, :], rbrow_bf[0:1, 0:15],
            start=True, stop=True)],
            reads=[b_ones, b_rbrow], writes=[b_pswbc])
        P.emit("dve", [lambda e: e.tensor_copy(rb_bc[:], ps_wbc[:, 0:15])],
               reads=[b_pswbc], writes=[b_rbbc])
        P.emit("dve", [lambda e: e.memset(stats_acc[:], 0.0)],
               writes=[b_stats])
        P.emit("dve", [lambda e: e.memset(w_tm_t[:, 15:16], 0.0)],
               writes=[b_wtm])
        P.emit("dve", [lambda e: e.memset(vv_sb[:], 0.0)],
               writes=[b_vvsb])
        P.emit("dve", [lambda e: e.memset(eps_t[:], EPS)], writes=[b_eps])
        P.emit("dve", [lambda e: e.memset(c125_t[:], 0.125)], writes=[b_c125])

        seed([b_xsb], [b_xsb, b_rows])
        x_bufs = [x_sb[:, 0:4096], acc[:, 0:4096]]
        bx_bufs = [b_xsb, b_xin1]
        xn_tm = d_extra[:, 0:4096]
        b_xntm = b_dx[0]
        hbufs = [d_extra[:, 4096:8192], qk_bufs[:, 0:4096]]
        b_hbufs = [b_dx[1], b_qkbuf[0]]

        def ab_ln_tile(t):
            bi = t % 2
            xb, bxb = x_bufs[bi], bx_bufs[bi]
            src = x_in[t * TILE:(t + 1) * TILE, :].rearrange(
                "(c p) d -> p c d", p=128)
            P.emit("sync", [lambda e, d=xb, s=src: e.dma_start(
                out=d.rearrange("p (c d) -> p c d", c=4), in_=s)],
                writes=[bxb])
            for c in range(4):
                xc = xb[:, c * 1024:(c + 1) * 1024]
                P.emit("dve", [lambda e, xc=xc, c=c: e.reduce_sum(
                    ln_t[:, c:c + 1], xc, axis=AX.X)],
                    reads=[bxb], writes=[b_lnt])
                P.emit("act", [lambda e, xc=xc, c=c: e.activation(
                    out=xsq_junk[:, 0:1024], in_=xc, func=AF.Square,
                    accum_out=ln_t[:, 4 + c:5 + c])],
                    reads=[bxb], writes=[b_xsq, b_lnt])
            P.emit("dve", [lambda e: e.tensor_scalar_mul(
                ln_t[:, 8:12], ln_t[:, 0:4], 1.0 / 1024.0)],
                reads=[b_lnt], writes=[b_lnt])
            P.emit("dve", [lambda e: e.tensor_scalar_mul(
                ln_t[:, 4:8], ln_t[:, 4:8], 1.0 / 1024.0)],
                reads=[b_lnt], writes=[b_lnt])
            P.emit("dve", [lambda e: e.tensor_mul(
                ln_t[:, 12:16], ln_t[:, 8:12], ln_t[:, 8:12])],
                reads=[b_lnt], writes=[b_lnt])
            P.emit("dve", [lambda e: e.tensor_sub(
                ln_t[:, 4:8], ln_t[:, 4:8], ln_t[:, 12:16])],
                reads=[b_lnt], writes=[b_lnt])
            P.emit("act", [lambda e: e.activation(
                out=ln_t[:, 4:8], in_=ln_t[:, 4:8], func=AF.Sqrt,
                bias=eps_t[:, 0:1])],
                reads=[b_lnt, b_eps], writes=[b_lnt])
            P.emit("dve", [lambda e: e.reciprocal(
                ln_t[:, 12:16], ln_t[:, 4:8])],
                reads=[b_lnt], writes=[b_lnt])
            P.emit("dve", [lambda e: e.scalar_tensor_tensor(
                ln_t[:, 8:12], ln_t[:, 8:12], -1.0, ln_t[:, 12:16],
                ALU.mult, ALU.mult)],
                reads=[b_lnt], writes=[b_lnt])
            for c in range(4):
                xc = xb[:, c * 1024:(c + 1) * 1024]
                P.emit("dve", [lambda e, xc=xc, c=c: e.tensor_scalar(
                    xn_tm[:, c * 1024:(c + 1) * 1024], xc,
                    ln_t[:, 12 + c:13 + c], ln_t[:, 8 + c:9 + c],
                    ALU.mult, ALU.add)],
                    reads=[bxb, b_lnt], writes=[b_xntm])
            dst = xn_sc[t * TILE:(t + 1) * TILE, :].rearrange(
                "(c p) d -> p c d", p=128)
            P.emit("sync", [lambda e, d=dst: e.dma_start(
                out=d, in_=xn_tm.rearrange("p (c d) -> p c d", c=4))],
                reads=[b_xntm], writes=[b_xnsc[t]])

        def xn_quarter_load(q):
            qreads = [b_xnsc[q * TPQ + i] for i in range(TPQ)]
            for dc in range(DC):
                P.emit("actdma", [lambda e, q=q, dc=dc:
                                  e.dma_start_transpose(
                    xn_q[:, dc * QT:(dc + 1) * QT],
                    xn_sc[q * QT:(q + 1) * QT, dc * 128:(dc + 1) * 128])],
                    reads=qreads, writes=[b_xnq[dc]])

        psh_cnt = [0]

        def ab_proj_quarter(q):
            for ti in range(TPQ):
                t = q * TPQ + ti
                toff = ti * TILE
                for br in range(3):
                    hb = (t * 3 + br) % 2
                    hbuf, b_hbuf = hbufs[hb], b_hbufs[hb]
                    wslot = w_ring[:, br * 8192:(br + 1) * 8192]
                    for ec in range(DC):
                        par = psh_cnt[0] % 2
                        psh_cnt[0] += 1
                        fns = []
                        for dc in range(DC):
                            fns.append(
                                lambda e, w=wslot, dc=dc, ec=ec, toff=toff,
                                par=par:
                                e.matmul(
                                    ps_h[par][:, 0:512],
                                    w[:, dc * 1024 + ec * 128:
                                      dc * 1024 + ec * 128 + 128],
                                    xn_q[:, dc * QT + toff:
                                         dc * QT + toff + 512],
                                    start=(dc == 0), stop=(dc == DC - 1)))
                        P.emit("pe", fns,
                               reads=[b_wpair[0] if br < 2 else b_wpair[1]]
                               + b_xnq,
                               writes=[b_psh[par]])
                        st_r = (br * DC + ec) % 4
                        P.emit("act", [lambda e, ec=ec, par=par, st_r=st_r,
                                       hbuf=hbuf: e.activation(
                            out=hbuf[:, ec * 512:(ec + 1) * 512],
                            in_=ps_h[par][:, 0:512], func=AF.Copy,
                            accum_out=stats_tmp[:, st_r * 2:st_r * 2 + 1])],
                            reads=[b_psh[par]],
                            writes=[b_hbuf, b_stmp[st_r]])
                        P.emit("dve", [lambda e, ec=ec, st_r=st_r,
                                       hbuf=hbuf: e.scalar_tensor_tensor(
                            sq_junk[:, 0:512],
                            hbuf[:, ec * 512:(ec + 1) * 512], 1.0,
                            hbuf[:, ec * 512:(ec + 1) * 512],
                            ALU.mult, ALU.mult,
                            accum_out=stats_tmp[:, st_r * 2 + 1:
                                                st_r * 2 + 2])],
                            reads=[b_hbuf], writes=[b_sqj, b_stmp[st_r]])
                        cs = br * 16 + ec
                        P.emit("dve", [lambda e, cs=cs, st_r=st_r:
                                       e.tensor_add(
                            stats_acc[:, cs:cs + 16].rearrange(
                                "p (two ec) -> p (ec two)", two=2
                            )[:, 0:2] if False else
                            stats_acc[:, cs:cs + 9].rearrange(
                                "p (a b) -> p a b", a=1)[:, 0, 0:9:8],
                            stats_acc[:, cs:cs + 9].rearrange(
                                "p (a b) -> p a b", a=1)[:, 0, 0:9:8],
                            stats_tmp[:, st_r * 2:st_r * 2 + 2])],
                            reads=[b_stmp[st_r], b_stats], writes=[b_stats])
                    dsth = h_sc[br * D:(br + 1) * D, :].rearrange(
                        "(dc p) n -> p dc n", p=128)
                    for half in range(2):
                        P.emit("sync", [lambda e, d=dsth, half=half, t=t,
                                        hbuf=hbuf: e.dma_start(
                            out=d[:, half * 4:half * 4 + 4,
                                  t * TILE:(t + 1) * TILE],
                            in_=hbuf[:, half * 2048:half * 2048 + 2048]
                            .rearrange("p (dc n) -> p dc n", dc=4))],
                            reads=[b_hbuf], writes=[b_hsc[br][t]])

        def f_router(tq):
            ti = tq % TPQ
            toff = ti * TILE
            for c in range(4):
                fns = []
                for dc in range(DC):
                    fns.append(lambda e, dc=dc, c=c, toff=toff:
                               e.matmul(
                        ps_wbc[:, c * 128:c * 128 + 15],
                        xn_q[:, dc * QT + toff + c * 128:
                             dc * QT + toff + c * 128 + 128],
                        routW_sb[:, dc * 15:(dc + 1) * 15],
                        start=(dc == 0), stop=(dc == DC - 1)))
                P.emit("pe", fns, reads=b_xnq + [b_routW], writes=[b_pswbc])
                lc = l_sb[:, c * 16:c * 16 + 15]
                P.emit("dve", [lambda e, lc=lc, c=c: e.tensor_add(
                    lc, ps_wbc[:, c * 128:c * 128 + 15], rb_bc[:])],
                    reads=[b_pswbc, b_rbbc], writes=[b_lsb])
            for c in range(4):
                lc = l_sb[:, c * 16:c * 16 + 15]
                P.emit("dve", [lambda e, lc=lc: e.reduce_max(
                    sm_t[:, 0:1], lc, axis=AX.X)],
                    reads=[b_lsb], writes=[b_smt])
                P.emit("dve", [lambda e, lc=lc: e.tensor_scalar(
                    sm_a[:, 0:15], lc, sm_t[:, 0:1], None, ALU.is_ge)],
                    reads=[b_lsb, b_smt], writes=[b_sma])
                P.emit("dve", [lambda e, lc=lc: e.scalar_tensor_tensor(
                    sm_b[:, 0:15], sm_a[:, 0:15], -30000.0, lc,
                    ALU.mult, ALU.add)],
                    reads=[b_sma, b_lsb], writes=[b_smb])
                P.emit("dve", [lambda e: e.reduce_max(
                    sm_t[:, 1:2], sm_b[:, 0:15], axis=AX.X)],
                    reads=[b_smb], writes=[b_smt])
                P.emit("dve", [lambda e: e.tensor_scalar(
                    sm_a[:, 0:15], sm_b[:, 0:15], sm_t[:, 1:2], None,
                    ALU.is_ge)],
                    reads=[b_smb, b_smt], writes=[b_sma])
                P.emit("dve", [lambda e: e.scalar_tensor_tensor(
                    sm_b[:, 0:15], sm_a[:, 0:15], -30000.0, sm_b[:, 0:15],
                    ALU.mult, ALU.add)],
                    reads=[b_sma, b_smb], writes=[b_smb])
                P.emit("dve", [lambda e: e.reduce_max(
                    sm_t[:, 2:3], sm_b[:, 0:15], axis=AX.X)],
                    reads=[b_smb], writes=[b_smt])
                P.emit("dve", [lambda e: e.tensor_scalar_mul(
                    sm_t[:, 3:4], sm_t[:, 0:1], -1.0)],
                    reads=[b_smt], writes=[b_smt])
                P.emit("act", [lambda e, lc=lc: e.activation(
                    out=sm_c[:, 0:15], in_=lc, func=AF.Exp,
                    bias=sm_t[:, 3:4], scale=1.0)],
                    reads=[b_lsb, b_smt], writes=[b_smc])
                P.emit("dve", [lambda e: e.reduce_sum(
                    sm_t[:, 4:5], sm_c[:, 0:15], axis=AX.X)],
                    reads=[b_smc], writes=[b_smt])
                P.emit("dve", [lambda e: e.reciprocal(
                    sm_t[:, 5:6], sm_t[:, 4:5])],
                    reads=[b_smt], writes=[b_smt])
                P.emit("dve", [lambda e, lc=lc: e.tensor_scalar(
                    sm_a[:, 0:15], lc, sm_t[:, 2:3], None, ALU.is_ge)],
                    reads=[b_lsb, b_smt], writes=[b_sma])
                P.emit("dve", [lambda e: e.tensor_mul(
                    sm_c[:, 0:15], sm_c[:, 0:15], sm_a[:, 0:15])],
                    reads=[b_sma, b_smc], writes=[b_smc])
                P.emit("dve", [lambda e: e.tensor_scalar_mul(
                    w_tm_t[:, 0:15], sm_c[:, 0:15], sm_t[:, 5:6])],
                    reads=[b_smc, b_smt], writes=[b_wtm])
                P.emit("dve", [lambda e, tq=tq, c=c: e.tensor_copy(
                    w_tm_all[:, (tq * 4 + c) * 16:(tq * 4 + c) * 16 + 16],
                    w_tm_t[:, 0:16])],
                    reads=[b_wtm], writes=[b_wtmall])

        for q in range(NQ + 1):
            if q < NQ:
                for ti in range(TPQ):
                    ab_ln_tile(q * TPQ + ti)
            if q >= 1:
                ab_proj_quarter(q - 1)
                for ti in range(TPQ):
                    f_router((q - 1) * TPQ + ti)
            if q < NQ:
                xn_quarter_load(q)

        def emit_phase_c():
            P.emit("sync", [lambda e: e.dma_start(out=cc_in, in_=stats_acc[:])],
                   reads=[b_stats], writes=[b_ccin])
            P.emit("cc", [lambda e: e.collective_compute(
                "AllReduce", ALU.add,
                replica_groups=[list(range(NCORES))],
                ins=[cc_in.opt()], outs=[cc_out.opt()])],
                reads=[b_ccin], writes=[b_ccout])
            P.emit("gps", [lambda e: e.dma_start(out=stats_gl[:], in_=cc_out)],
                   reads=[b_ccout], writes=[b_stgl])
            sgl = stats_gl[:].rearrange("p (br k c) -> p br k c", br=3, k=2)
            vec3 = vecs_sb[:, 0:48].rearrange("p (br k c) -> p br k c", br=3, k=2)
            abc = ab_coef[:].rearrange("p (br k c) -> p br k c", br=3, k=2)
            abt = ab_tmp[:].rearrange("p (g c) -> p g c", g=3)
            sums3 = sgl[:, :, 0, :]
            sqs3 = sgl[:, :, 1, :]
            gam3 = vec3[:, :, 0, :]
            bet3 = vec3[:, :, 1, :]
            A3 = abc[:, :, 0, :]
            B3 = abc[:, :, 1, :]
            mean3 = abt[:, 0, :].rearrange("p (br c) -> p br c", br=1)[:, 0, :]
            mean3 = ab_tmp[:, 0:8]
            mean_t = ab_tmp[:, 0:24].rearrange("p (br c) -> p br c", br=3)
            ee_t = stats_tmp[:, 0:8]
            P.emit("dve", [lambda e: e.tensor_scalar_mul(
                mean_t, sums3, 1.0 / NTOK_GLOBAL)],
                reads=[b_stgl], writes=[b_abt])
            P.emit("dve", [lambda e: e.tensor_scalar_mul(
                B3, sqs3, 1.0 / NTOK_GLOBAL)],
                reads=[b_stgl], writes=[b_ab])
            P.emit("dve", [lambda e: e.scalar_tensor_tensor(
                A3, mean_t, 1.0, mean_t, ALU.mult, ALU.mult)],
                reads=[b_abt], writes=[b_ab])
            P.emit("dve", [lambda e: e.tensor_sub(B3, B3, A3)],
                   reads=[b_ab], writes=[b_ab])
            P.emit("act", [lambda e: e.activation(
                out=ab_coef[:].rearrange("p (br k c) -> p br k c",
                                         br=3, k=2)[:, :, 1, :],
                in_=ab_coef[:].rearrange("p (br k c) -> p br k c",
                                         br=3, k=2)[:, :, 1, :],
                func=AF.Sqrt, bias=eps_t[:, 0:1])],
                reads=[b_ab, b_eps], writes=[b_ab])
            P.emit("dve", [lambda e: e.reciprocal(A3, B3)],
                   reads=[b_ab], writes=[b_ab])
            P.emit("dve", [lambda e: e.tensor_mul(A3, A3, gam3)],
                   reads=[b_ab, b_vecs], writes=[b_ab])
            P.emit("dve", [lambda e: e.tensor_mul(mean_t, mean_t, A3)],
                   reads=[b_ab, b_abt], writes=[b_abt])
            P.emit("dve", [lambda e: e.tensor_scalar(
                B3, bet3, 4.0, -4.0, ALU.mult, ALU.add)],
                reads=[b_vecs, b_ab], writes=[b_ab])
            P.emit("dve", [lambda e: e.scalar_tensor_tensor(
                B3, mean_t, -4.0, B3, ALU.mult, ALU.add)],
                reads=[b_abt, b_ab], writes=[b_ab])
            P.emit("dve", [lambda e: e.tensor_scalar_mul(A3, A3, 4.0)],
                   reads=[b_ab], writes=[b_ab])


        d_stage = qk_bufs[:, 0:4096]
        d_qv = qk_bufs[:, 4096:8192]
        d_k = d_extra[:, 4096:8192]
        sig_dst = [d_qv, d_k, d_qv]
        sig_dbuf = [b_qkbuf[1], b_dx[1], b_qkbuf[1]]

        def d_load(t, br):
            srch = h_sc[br * D:(br + 1) * D, :].rearrange(
                "(dc p) n -> p dc n", p=128)
            for half in range(2):
                P.emit("sync", [lambda e, s=srch, half=half, t=t, br=br:
                                e.dma_start(
                    out=d_stage[:, half * 2048:half * 2048 + 2048]
                    .rearrange("p (dc n) -> p dc n", dc=4),
                    in_=s[:, half * 4:half * 4 + 4,
                          t * TILE:(t + 1) * TILE])],
                    reads=[b_hsc[br][t]], writes=[b_qkbuf[0]])

        def d_sigs(t, br):
            for dc in range(DC):
                P.emit("act", [lambda e, br=br, dc=dc: e.activation(
                    out=sig_dst[br][:, dc * 512:(dc + 1) * 512],
                    in_=d_stage[:, dc * 512:(dc + 1) * 512],
                    func=AF.Sigmoid,
                    scale=ab_coef[:, br * 16 + dc:br * 16 + dc + 1],
                    bias=ab_coef[:, br * 16 + 8 + dc:
                                 br * 16 + 8 + dc + 1])],
                    reads=[b_qkbuf[0], b_ab], writes=[sig_dbuf[br]])

        def d_qkmul(t):
            P.emit("dve", [lambda e: e.tensor_mul(
                d_stage[:], d_qv[:], d_k[:])],
                reads=[b_qkbuf[1], b_dx[1]], writes=[b_qkbuf[0]])

        def d_stq(t, which):
            srcb, dstd, btok, bsrc = (
                (d_stage, qk_sc, b_qksc[t], b_qkbuf[0]) if which == 0 else
                (d_qv, v_sc, b_vsc[t], b_qkbuf[1]))
            dd = dstd.rearrange("(dc p) n -> p dc n", p=128)
            for half in range(2):
                P.emit("sync", [lambda e, s=srcb, d=dd, half=half, t=t:
                                e.dma_start(
                    out=d[:, half * 4:half * 4 + 4,
                          t * TILE:(t + 1) * TILE],
                    in_=s[:, half * 2048:half * 2048 + 2048].rearrange(
                        "p (dc n) -> p dc n", dc=4))],
                    reads=[bsrc], writes=[btok])

        def d_vtm(t):
            for c in range(4):
                reg = d_extra[:, c * 1024:c * 1024 + 1024]
                P.emit("actdma", [lambda e, reg=reg, t=t, c=c:
                                  e.dma_start_transpose(
                    reg, v_sc[:, t * TILE + c * 128:
                              t * TILE + (c + 1) * 128])],
                    reads=[b_vsc[t]], writes=[b_dx[0]])

        def d_vv(t):
            paro = pso_cnt[0] % 2
            pso_cnt[0] += 1
            fns = []
            for c in range(4):
                reg = d_extra[:, c * 1024:c * 1024 + 1024]
                for hp in range(8):
                    fns.append(lambda e, reg=reg, hp=hp, c=c, paro=paro:
                               e.matmul(
                        ps_out[paro][:, hp * 128:(hp + 1) * 128],
                        reg[:, hp * 128:(hp + 1) * 128],
                        reg[:, hp * 128:(hp + 1) * 128],
                        start=(c == 0 and hp in (0, 4)),
                        stop=(c == 3 and hp in (3, 7)),
                        skip_group_check=True))
            P.emit("pe", fns, reads=[b_dx[0]], writes=[b_psout[paro]])
            P.emit("dve", [lambda e, paro=paro: e.tensor_add(
                vv_sb[:], vv_sb[:], ps_out[paro][:])],
                reads=[b_psout[paro], b_vvsb], writes=[b_vvsb])

        def d_bdvv():
            P.emit("dve", [lambda e: e.memset(bd_vv[:], 0.0)],
                   reads=[b_dx[1]], writes=[b_bdvv])
            for dc in range(DC):
                for half in range(2):
                    P.emit("dve", [lambda e, dc=dc, half=half:
                                   e.tensor_scalar_mul(
                        bd_vv[half * 64:half * 64 + 64,
                              dc * 128 + half * 64:
                              dc * 128 + half * 64 + 64],
                        vv_sb[half * 64:half * 64 + 64,
                              dc * 128 + half * 64:
                              dc * 128 + half * 64 + 64], 0.125)],
                        reads=[b_vvsb], writes=[b_bdvv])

        def d_sched(gp):
            t, step = gp // 5, gp % 5
            if t >= NT:
                if gp == NT * 5:
                    d_bdvv()
                return
            if step == 0:
                d_load(t, 0)
                d_sigs(t, 0)
                d_load(t, 1)
            elif step == 1:
                d_sigs(t, 1)
                d_qkmul(t)
                d_stq(t, 0)
            elif step == 2:
                d_load(t, 2)
                d_sigs(t, 2)
                d_stq(t, 1)
            elif step == 3:
                d_vtm(t)
            else:
                d_vv(t)

        seed(b_hraw, [b_hD[0]])
        seed(b_hscl[0], [b_hD[1]])
        seed(b_hscl[1], [b_hD[2]])
        seed(b_acc[:4], [b_xin1])

        h_raw = h_arena[:, 0:4096]
        h_scl = [h_arena[:, 4096:8192], h_arena[:, 8192:12288]]
        attn_fm = d_extra[:, 0:4096]
        b_attnfm = b_dx[0]
        pso_cnt = [0]
        hsl_cnt = [0]

        pair_defs = []
        for q in range(NQ):
            pair_defs.append(("pair", sWcat))
            for e_i in range(NEXP):
                pair_defs.append(
                    ("pair", eWcat[e_i:e_i + 1, :, :].rearrange(
                        "one p m -> p (one m)")))
        pair_slots = []

        def emit_pair_load(g):
            base = g % 2
            kind, ap = pair_defs[g]
            if kind == "pair":
                load_pair(base, ap, b_wpair[base])
            else:
                load_slot(base * 2, ap, b_wpair[base])
            pair_slots.append((base * 2, base * 2 + 1, b_wpair[base]))

        emit_pair_load(0)
        emit_pair_load(1)

        def f_first_and_h(p, tq, slot_a, wbuf, shared, hsl):
            ti = tq % TPQ
            toff = ti * TILE
            wslot = w_ring[:, slot_a * 8192:(slot_a + 1) * 8192]
            if not shared:
                P.emit("pe", [lambda e, p=p, ti=ti: e.matmul(
                    ps_wbc[:, 0:512],
                    rowhot_sb[:, (p - 1) * 128:p * 128],
                    w_fm_sb[0:15, ti * 512:(ti + 1) * 512],
                    start=True, stop=True)],
                    reads=[b_rowhot, b_wfm[ti]], writes=[b_pswbc])
                P.emit("dve", [lambda e: e.tensor_copy(
                    wbc_sb[:], ps_wbc[:, 0:512])],
                    reads=[b_pswbc], writes=[b_wbc])
            dstbufs = []
            for ec in range(DC):
                par = psh_cnt[0] % 2
                psh_cnt[0] += 1
                fns = []
                for dc in range(DC):
                    fns.append(lambda e, w=wslot, dc=dc, ec=ec, toff=toff,
                               par=par:
                               e.matmul(
                        ps_h[par][:, 0:512],
                        w[:, dc * 1024 + ec * 128:dc * 1024 + ec * 128 + 128],
                        xn_q[:, dc * QT + toff:dc * QT + toff + 512],
                        start=(dc == 0), stop=(dc == DC - 1)))
                P.emit("pe", fns, reads=[wbuf] + b_xnq,
                       writes=[b_psh[par]])
                bcol = 48 if shared else (56 + (p - 1) * 8)
                P.emit("act", [lambda e, ec=ec, par=par, bcol=bcol:
                               e.activation(
                    out=h_raw[:, ec * 512:(ec + 1) * 512],
                    in_=ps_h[par][:, 0:512], func=AF.Relu,
                    bias=vecs_sb[:, bcol + ec:bcol + ec + 1])],
                    reads=[b_psh[par], b_vecs], writes=[b_hraw[ec]])
                if shared:
                    dstbufs.append(b_hraw[ec])
                else:
                    P.emit("dve", [lambda e, ec=ec, hsl=hsl: e.tensor_mul(
                        h_scl[hsl][:, ec * 512:(ec + 1) * 512],
                        h_raw[:, ec * 512:(ec + 1) * 512], wbc_sb[:])],
                        reads=[b_hraw[ec], b_wbc], writes=[b_hscl[hsl][ec]])
                    dstbufs.append(b_hscl[hsl][ec])
            return dstbufs

        def f_second(lhs_t, lhs_bufs, slot_b, wbuf, tq, first_pass):
            ti = tq % TPQ
            wslot = w_ring[:, slot_b * 8192:(slot_b + 1) * 8192]
            for c in range(4):
                paro = pso_cnt[0] % 2
                pso_cnt[0] += 1
                fns = []
                for hc in range(DC):
                    for half in range(2):
                        fns.append(lambda e, w=wslot, hc=hc, half=half, c=c,
                                   paro=paro:
                                   e.matmul(
                            ps_out[paro][:, half * 512:half * 512 + 512],
                            lhs_t[:, hc * 512 + c * 128:
                                  hc * 512 + c * 128 + 128],
                            w[:, hc * 1024 + half * 512:
                              hc * 1024 + half * 512 + 512],
                            start=(hc == 0), stop=(hc == DC - 1)))
                P.emit("pe", fns, reads=lhs_bufs + [wbuf],
                       writes=[b_psout[paro]])
                aslab = acc[:, (ti * 4 + c) * 1024:(ti * 4 + c + 1) * 1024]
                ab = b_acc[ti * 4 + c]
                if first_pass:
                    P.emit("dve", [lambda e, a=aslab, paro=paro:
                                   e.tensor_copy(a, ps_out[paro][:])],
                           reads=[b_psout[paro]], writes=[ab])
                else:
                    P.emit("dve", [lambda e, a=aslab, paro=paro:
                                   e.tensor_add(a, a, ps_out[paro][:])],
                           reads=[b_psout[paro], ab], writes=[ab])

        wslot_wp = qk_bufs[:, 0:8192]
        b_wp = Buf()

        def load_wpt():
            P.emit("gps", [lambda e: e.dma_start(out=wslot_wp, in_=wpTp)],
                   reads=[b_qkbuf[0], b_qkbuf[1]], writes=[b_wp])

        srcqk_g = qk_sc.rearrange("(dc p) n -> p dc n", p=128)

        def attn_slot_a(t):
            qkb = d_extra[:, 0:4096]
            for half in range(2):
                P.emit("sync", [lambda e, half=half, t=t, qkb=qkb:
                                e.dma_start(
                    out=qkb[:, half * 2048:half * 2048 + 2048]
                    .rearrange("p (dc n) -> p dc n", dc=4),
                    in_=srcqk_g[:, half * 4:half * 4 + 4,
                                t * TILE:(t + 1) * TILE])],
                    reads=[b_qksc[t]], writes=[b_dx[0]])
            for c in range(4):
                P.emit("pe", [lambda e, t=t, c=c: e.transpose(
                    ps_tr[0:16, :],
                    w_tm_all[:, (t * 4 + c) * 16:(t * 4 + c) * 16 + 16],
                    ident_sb[:])],
                    reads=[b_wtmall, b_ident], writes=[b_pstr])
                P.emit("dve", [lambda e, c=c: e.tensor_copy(
                    w_fm_sb[0:16, 1024 + c * 128:1024 + (c + 1) * 128],
                    ps_tr[0:16, :])],
                    reads=[b_pstr], writes=[b_wfm[2]])
            for dc in range(DC):
                par = psh_cnt[0] % 2
                psh_cnt[0] += 1
                P.emit("pe", [lambda e, dc=dc, par=par, qkb=qkb: e.matmul(
                    ps_h[par][:, 0:512], bd_vv[:, dc * 128:(dc + 1) * 128],
                    qkb[:, dc * 512:(dc + 1) * 512], start=True, stop=True)],
                    reads=[b_bdvv, b_dx[0]], writes=[b_psh[par]])
                P.emit("act", [lambda e, dc=dc, par=par: e.activation(
                    out=attn_fm[:, dc * 512:(dc + 1) * 512],
                    in_=ps_h[par][:, 0:512], func=AF.Copy)],
                    reads=[b_psh[par]], writes=[b_attnfm])

        def attn_slot_b(t):
            srcy = y_out[t * TILE:(t + 1) * TILE, :].rearrange(
                "(c p) d -> p c d", p=128)
            P.emit("sync", [lambda e, s=srcy: e.dma_start(
                out=x_sb[:].rearrange("p (c d) -> p c d", c=4), in_=s)],
                reads=[b_ysc[t]], writes=[b_xsb])
            for c in range(4):
                paro = pso_cnt[0] % 2
                pso_cnt[0] += 1
                fns = []
                for hc in range(DC):
                    for half in range(2):
                        fns.append(lambda e, hc=hc, half=half, c=c,
                                   paro=paro:
                                   e.matmul(
                            ps_out[paro][:, half * 512:half * 512 + 512],
                            attn_fm[:, hc * 512 + c * 128:
                                    hc * 512 + c * 128 + 128],
                            wslot_wp[:, hc * 1024 + half * 512:
                                     hc * 1024 + half * 512 + 512],
                            start=(hc == 0), stop=False))
                for half in range(2):
                    fns.append(lambda e, half=half, c=c, paro=paro:
                               e.matmul(
                        ps_out[paro][:, half * 512:half * 512 + 512],
                        w_fm_sb[0:15, 1024 + c * 128:1024 + (c + 1) * 128],
                        eb2_sb[0:15, half * 512:half * 512 + 512],
                        start=False, stop=False))
                for half in range(2):
                    fns.append(lambda e, half=half, paro=paro:
                               e.matmul(
                        ps_out[paro][:, half * 512:half * 512 + 512],
                        ones_sb[0:1, 0:128],
                        bias_bf[0:1, half * 512:half * 512 + 512],
                        start=False, stop=True))
                P.emit("pe", fns,
                       reads=[b_attnfm, b_wp, b_wfm[2], b_eb2,
                              b_ones, b_bias],
                       writes=[b_psout[paro]])
                P.emit("dve", [lambda e, c=c, paro=paro: e.tensor_add(
                    x_sb[:, c * 1024:(c + 1) * 1024],
                    x_sb[:, c * 1024:(c + 1) * 1024], ps_out[paro][:])],
                    reads=[b_psout[paro], b_xsb], writes=[b_xsb])
            dsty = y_out[t * TILE:(t + 1) * TILE, :].rearrange(
                "(c p) d -> p c d", p=128)
            P.emit("sync", [lambda e, d=dsty: e.dma_start(
                out=d, in_=x_sb[:].rearrange("p (c d) -> p c d", c=4))],
                reads=[b_xsb], writes=[b_xsb, b_ysc[t]])

        can_inter = (NQ > 1) and (NT * 5 + 1 <= (NQ - 1) * 16)
        N_INTER = (NQ - 1) * TPQ if can_inter else 0

        def xn_next_load(q):
            qreads = [b_xnsc[q * TPQ + i] for i in range(TPQ)]
            for dc in range(DC):
                P.emit("actdma", [lambda e, q=q, dc=dc:
                                  e.dma_start_transpose(
                    xn_q[:, dc * QT:(dc + 1) * QT],
                    xn_sc[q * QT:(q + 1) * QT, dc * 128:(dc + 1) * 128])],
                    reads=qreads, writes=[b_xnq[dc]])
            for ti in range(TPQ):
                tq0 = q * TPQ + ti
                for c in range(4):
                    P.emit("pe", [lambda e, tq0=tq0, c=c: e.transpose(
                        ps_tr[0:16, :],
                        w_tm_all[:, (tq0 * 4 + c) * 16:
                                 (tq0 * 4 + c) * 16 + 16],
                        ident_sb[:])],
                        reads=[b_wtmall, b_ident], writes=[b_pstr])
                    P.emit("dve", [lambda e, c=c, ti=ti: e.tensor_copy(
                        w_fm_sb[0:16, ti * 512 + c * 128:
                                ti * 512 + (c + 1) * 128],
                        ps_tr[0:16, :])],
                        reads=[b_pstr], writes=[b_wfm[ti]])

        for q in range(NQ):
            if q == 0:
                xn_next_load(0)
                emit_phase_c()
            if q == NQ - 1 and can_inter:
                load_wpt()
            for p in range(16):
                g = q * 16 + p
                for ti in range(TPQ):
                    tq = q * TPQ + ti
                    sa, sb, wb = pair_slots[g]
                    shared = (p == 0)
                    hsl = hsl_cnt[0] % 2
                    hsl_cnt[0] += 1
                    lhs_bufs = f_first_and_h(p, tq, sa, wb, shared, hsl)
                    lhs_t = h_raw if shared else h_scl[hsl]
                    f_second(lhs_t, lhs_bufs, sb, wb, tq,
                             first_pass=shared)
                d_sched(g)
                if q == NQ - 1 and can_inter:
                    if p % 2 == 0 and p // 2 < N_INTER:
                        attn_slot_a(p // 2)
                    elif p % 2 == 1 and p // 2 < N_INTER:
                        attn_slot_b(p // 2)
                if g + 2 < len(pair_defs):
                    emit_pair_load(g + 2)
                if p == 15 and q + 1 < NQ:
                    xn_next_load(q + 1)
            for ti in range(TPQ):
                tq = q * TPQ + ti
                src = x_in[tq * TILE:(tq + 1) * TILE, :].rearrange(
                    "(c p) d -> p c d", p=128)
                P.emit("sync", [lambda e, s=src: e.dma_start(
                    out=x_sb[:].rearrange("p (c d) -> p c d", c=4), in_=s)],
                    writes=[b_xsb])
                P.emit("dve", [lambda e, ti=ti: e.tensor_add(
                    x_sb[:], acc[:, ti * 4096:(ti + 1) * 4096], x_sb[:])],
                    reads=[b_acc[ti * 4], b_acc[ti * 4 + 1],
                           b_acc[ti * 4 + 2], b_acc[ti * 4 + 3], b_xsb],
                    writes=[b_xsb])
                dsty = y_out[tq * TILE:(tq + 1) * TILE, :].rearrange(
                    "(c p) d -> p c d", p=128)
                P.emit("sync", [lambda e, d=dsty: e.dma_start(
                    out=d, in_=x_sb[:].rearrange("p (c d) -> p c d", c=4))],
                    reads=[b_xsb], writes=[b_xsb, b_ysc[tq]])

        if not can_inter:
            load_wpt()
        for t in range(N_INTER, NT):
            attn_slot_a(t)
            attn_slot_b(t)

        P.final_wait("sync", [("qsp%d" % i, P.cnt["qsp%d" % i])
                              for i in range(NQSEM)
                              if P.cnt["qsp%d" % i] > 0])
        P.flush()
    return nc


def kernel(L0_attn_x, L0_attn_Wq, L0_attn_gq, L0_attn_bq, L0_attn_Wk,
           L0_attn_gk, L0_attn_bk, L0_attn_Wv, L0_attn_gv, L0_attn_bv,
           L0_attn_Wp, L0_attn_bp, L0_ffn_router_W, L0_ffn_router_b,
           L0_ffn_sW1, L0_ffn_sb1, L0_ffn_sW2, L0_ffn_sb2, L0_ffn_eW1,
           L0_ffn_eb1, L0_ffn_eW2, L0_ffn_eb2):
    f = lambda a: np.ascontiguousarray(np.asarray(a, dtype=np.float32))
    x = f(L0_attn_x)
    B, T, _ = x.shape

    def pc(v):
        return np.ascontiguousarray(f(v).reshape(8, 128).T)

    vecs = np.concatenate(
        [pc(L0_attn_gq), pc(L0_attn_bq), pc(L0_attn_gk), pc(L0_attn_bk),
         pc(L0_attn_gv), pc(L0_attn_bv), pc(L0_ffn_sb1),
         np.ascontiguousarray(
             np.transpose(f(L0_ffn_eb1).reshape(15, 8, 128), (2, 0, 1))
             .reshape(128, 120))],
        axis=1)
    rows = np.zeros((1, 2064), dtype=np.float32)
    rows[0, 0:1024] = f(L0_attn_bp)
    rows[0, 1024:2048] = f(L0_ffn_sb2)
    rows[0, 2048:2063] = f(L0_ffn_router_b)

    def pm(w):
        w = f(w)
        return np.ascontiguousarray(
            w.reshape(8, 128, w.shape[1]).transpose(1, 0, 2).reshape(128, -1))

    eW1f, eW2f = f(L0_ffn_eW1), f(L0_ffn_eW2)
    eWcat = np.ascontiguousarray(np.stack(
        [np.concatenate([pm(eW1f[e]), pm(eW2f[e])], axis=1)
         for e in range(15)], axis=0))
    common = {
        "wqTp": pm(f(L0_attn_Wq).T),
        "wkTp": pm(f(L0_attn_Wk).T),
        "wvTp": pm(f(L0_attn_Wv).T),
        "wpTp": pm(f(L0_attn_Wp).T),
        "sWcat": np.concatenate([pm(L0_ffn_sW1), pm(L0_ffn_sW2)], axis=1),
        "eWcat": eWcat,
        "routW": f(L0_ffn_router_W),
        "eb2": f(L0_ffn_eb2),
        "vecs": vecs,
        "rows": rows,
    }
    in_maps = []
    for b in range(B):
        m = dict(common)
        m["x"] = np.ascontiguousarray(x[b])
        in_maps.append(m)

    nc = build_nc(T)
    res = run_bass_kernel_spmd(nc, in_maps, core_ids=list(range(B)))
    y = np.stack([res.results[i]["y"] for i in range(B)], axis=0)
    return y


# version 21
# speedup vs baseline: 1.0255x; 1.0131x over previous
import numpy as np
import ml_dtypes

import concourse.bass as bass
import concourse.mybir as mybir
from concourse.bass_utils import run_bass_kernel_spmd

F32 = mybir.dt.float32
BF16 = mybir.dt.bfloat16
AF = mybir.ActivationFunctionType
ALU = mybir.AluOpType
AX = mybir.AxisListType

D = 1024
DC = 8
NEXP = 15
NCORES = 8
TILE = 512
EPS = 1e-5


class Buf:
    __slots__ = ("ready", "readers")

    def __init__(self):
        self.ready = None
        self.readers = []


def seed(dst_bufs, src_bufs):
    toks = []
    for s in src_bufs:
        if s.ready is not None:
            toks.append(s.ready)
        toks.extend(s.readers)
    for d in dst_bufs:
        d.ready = None
        d.readers = list(toks)


NQSEM = 8


class Prog:
    ENG = {
        "pe": ("tensor", "pe", 1, 1),
        "act": ("scalar", "act", 1, 1),
        "dve": ("vector", "dve", 1, 1),
        "sync": ("sync", "qsp", 16, NQSEM),
        "actdma": ("scalar", "qact", 16, NQSEM),
        "gps": ("gpsimd", "qgps", 16, NQSEM),
        "cc": ("gpsimd", "cc", 1, 1),
    }

    def __init__(self, block, sems):
        self.block = block
        self.sems = sems
        self.cnt = {k: 0 for k in sems}
        self.wm = {}
        self.qidx = {}
        self.streams = {s: [] for s in
                        ("tensor", "scalar", "vector", "sync", "gpsimd")}

    def emit(self, eng, fns, reads=(), writes=(), waits=()):
        sec_name, semkey, inc, ring = self.ENG[eng]
        if ring > 1:
            i = self.qidx.get(eng, 0)
            self.qidx[eng] = i + 1
            semkey = "%s%d" % (semkey, i % ring)
        req = {}

        def add(tok):
            if tok is None:
                return
            sk, v = tok
            if req.get(sk, -1) < v:
                req[sk] = v

        for b in reads:
            add(b.ready)
        for b in writes:
            add(b.ready)
            for t in b.readers:
                add(t)
        for t in waits:
            add(t)
        if inc == 16 and self.cnt[semkey] > 0:
            add((semkey, self.cnt[semkey]))
        wm = self.wm.setdefault(sec_name, {})
        todo = [(sk, v) for sk, v in req.items() if wm.get(sk, 0) < v]
        self.cnt[semkey] += inc
        tok = (semkey, self.cnt[semkey])
        self.streams[sec_name].append((todo, fns, semkey, inc))
        for sk, v in todo:
            if wm.get(sk, 0) < v:
                wm[sk] = v
        for b in reads:
            b.readers.append(tok)
        for b in writes:
            b.ready = tok
            b.readers = []
        return tok

    def final_wait(self, eng, waits):
        sec_name = self.ENG[eng][0]
        self.streams[sec_name].append((list(waits), [], None, 0))

    def flush(self):
        sems = self.sems
        for sec_name, items in self.streams.items():
            if not items:
                continue
            sec = getattr(self.block, sec_name)

            @sec
            def _(eng_obj, items=items):
                for todo, fns, semkey, inc in items:
                    for sk, v in todo:
                        eng_obj.wait_ge(sems[sk], v)
                    inst = None
                    for fn in fns:
                        inst = fn(eng_obj)
                    if semkey is not None:
                        inst.then_inc(sems[semkey], inc)


def build_nc(T):
    assert T % TILE == 0
    NT = T // TILE
    QT = min(1024, T)
    NQ = T // QT
    TPQ = QT // TILE
    NTOK_GLOBAL = float(NCORES * T)

    nc = bass.Bass("TRN2", target_bir_lowering=False)

    x_in = nc.dram_tensor("x", [T, D], F32, kind="ExternalInput").ap()
    wqTp = nc.dram_tensor("wqTp", [128, 8192], F32, kind="ExternalInput").ap()
    wkTp = nc.dram_tensor("wkTp", [128, 8192], F32, kind="ExternalInput").ap()
    wvTp = nc.dram_tensor("wvTp", [128, 8192], F32, kind="ExternalInput").ap()
    wpTp = nc.dram_tensor("wpTp", [128, 8192], F32, kind="ExternalInput").ap()
    sWcat = nc.dram_tensor("sWcat", [128, 16384], F32, kind="ExternalInput").ap()
    eWcat = nc.dram_tensor("eWcat", [NEXP, 128, 16384], F32,
                           kind="ExternalInput").ap()
    routW = nc.dram_tensor("routW", [D, NEXP], F32, kind="ExternalInput").ap()
    eb2_in = nc.dram_tensor("eb2", [NEXP, D], F32, kind="ExternalInput").ap()
    vecs_in = nc.dram_tensor("vecs", [128, 176], F32, kind="ExternalInput").ap()
    rows_in = nc.dram_tensor("rows", [1, 2064], F32, kind="ExternalInput").ap()
    y_out = nc.dram_tensor("y", [T, D], F32, kind="ExternalOutput").ap()

    xn_sc = nc.dram_tensor("xn_sc", [T, D], BF16, kind="Internal").ap()
    h_sc = nc.dram_tensor("h_sc", [3 * D, T], BF16, kind="Internal").ap()
    qk_sc = nc.dram_tensor("qk_sc", [D, T], BF16, kind="Internal").ap()
    v_sc = nc.dram_tensor("v_sc", [D, T], BF16, kind="Internal").ap()
    cc_in = nc.dram_tensor("cc_in", [128, 48], F32, kind="Internal").ap()
    cc_out = nc.dram_tensor("cc_out", [128, 48], F32, kind="Internal").ap()

    ident_c = nc.inline_tensor(np.eye(128, dtype=ml_dtypes.bfloat16),
                               name="identc").ap()
    ones_c = nc.inline_tensor(np.ones((1, 128), dtype=ml_dtypes.bfloat16),
                              name="onesc").ap()
    rowhot_np = np.zeros((NEXP, NEXP * 128), dtype=ml_dtypes.bfloat16)
    for e in range(NEXP):
        rowhot_np[e, e * 128:(e + 1) * 128] = 1.0
    rowhot_c = nc.inline_tensor(rowhot_np, name="rowhotc").ap()

    from contextlib import ExitStack
    with ExitStack() as ctx:
        en = ctx.enter_context
        w_ring = en(nc.sbuf_tensor("w_ring", [128, 4 * 8192], BF16))
        acc = en(nc.sbuf_tensor("acc", [128, max(TPQ * 4096, 5632)], F32))
        xn_q = en(nc.sbuf_tensor("xn_q", [128, max(DC * QT, 8192)], BF16))
        h_arena = en(nc.sbuf_tensor("h_arena", [128, 12288], BF16))
        qk_bufs = en(nc.sbuf_tensor("qk_bufs", [128, 8192], BF16))
        d_extra = en(nc.sbuf_tensor("d_extra", [128, 8192], BF16))
        x_sb = en(nc.sbuf_tensor("x_sb", [128, 4096], F32))
        sq_junk = en(nc.sbuf_tensor("sq_junk", [128, 512], BF16))
        vecs_sb = en(nc.sbuf_tensor("vecs_sb", [128, 176], F32))
        stats_acc = en(nc.sbuf_tensor("stats_acc", [128, 48], F32))
        stats_tmp = en(nc.sbuf_tensor("stats_tmp", [128, 8], F32))
        stats_gl = en(nc.sbuf_tensor("stats_gl", [128, 48], F32))
        ab_coef = en(nc.sbuf_tensor("ab_coef", [128, 48], F32))
        ab_tmp = en(nc.sbuf_tensor("ab_tmp", [128, 24], F32))
        ln_t = en(nc.sbuf_tensor("ln_t", [128, 16], F32))
        bias_bf = en(nc.sbuf_tensor("bias_bf", [1, 1024], BF16))
        rbrow_bf = en(nc.sbuf_tensor("rbrow_bf", [1, 16], BF16))
        rb_bc = en(nc.sbuf_tensor("rb_bc", [128, 15], F32))
        eb2_sb = en(nc.sbuf_tensor("eb2_sb", [15, 1024], BF16))
        routW_sb = en(nc.sbuf_tensor("routW_sb", [128, 8 * 15], BF16))
        vv_sb = en(nc.sbuf_tensor("vv_sb", [128, 1024], F32))
        w_tm_all = en(nc.sbuf_tensor("w_tm_all", [128, 512], BF16))
        w_fm_sb = en(nc.sbuf_tensor("w_fm_sb", [16, 1536], BF16))
        bd_vv = en(nc.sbuf_tensor("bd_vv", [128, 1024], BF16))
        w_tm_t = en(nc.sbuf_tensor("w_tm_t", [128, 16], BF16))
        wbc_sb = en(nc.sbuf_tensor("wbc_sb", [128, 512], BF16))
        l_sb = en(nc.sbuf_tensor("l_sb", [128, 64], F32))
        sm_a = en(nc.sbuf_tensor("sm_a", [128, 16], F32))
        sm_b = en(nc.sbuf_tensor("sm_b", [128, 16], F32))
        sm_c = en(nc.sbuf_tensor("sm_c", [128, 16], F32))
        sm_t = en(nc.sbuf_tensor("sm_t", [128, 8], F32))
        eps_t = en(nc.sbuf_tensor("eps_t", [128, 1], F32))
        c125_t = en(nc.sbuf_tensor("c125_t", [128, 1], F32))
        ident_sb = en(nc.sbuf_tensor("ident_sb", [128, 128], BF16))
        ones_sb = en(nc.sbuf_tensor("ones_sb", [1, 128], BF16))
        rowhot_sb = en(nc.sbuf_tensor("rowhot_sb", [15, 15 * 128], BF16))

        ps_out0 = en(nc.psum_tensor("ps_out0", [128, 1024], F32))
        ps_out1 = en(nc.psum_tensor("ps_out1", [128, 1024], F32))
        ps_h0 = en(nc.psum_tensor("ps_h0", [128, 512], F32))
        ps_h1 = en(nc.psum_tensor("ps_h1", [128, 512], F32))
        ps_wbc = en(nc.psum_tensor("ps_wbc", [128, 512], F32))
        ps_tr = en(nc.psum_tensor("ps_tr", [128, 128], BF16))

        xsq_junk = acc[:, 4096:5120]
        sq_junk = acc[:, 5120:5632]
        sems = {}
        for k in ("pe", "act", "dve", "cc"):
            sems[k] = en(nc.semaphore(k))
        for base in ("qsp", "qact", "qgps"):
            for i in range(NQSEM):
                k = "%s%d" % (base, i)
                sems[k] = en(nc.semaphore(k))
        block = en(nc.Block())
        P = Prog(block, sems)

        ps_out = [ps_out0, ps_out1]
        ps_h = [ps_h0, ps_h1]
        b_psout = [Buf(), Buf()]
        b_psh = [Buf(), Buf()]
        b_pswbc = Buf()
        b_pstr = Buf()
        b_xnq = [Buf() for _ in range(DC)]
        b_acc = [Buf() for _ in range(TPQ * 4)]
        b_hraw = [Buf() for _ in range(DC)]
        b_hscl = [[Buf() for _ in range(DC)], [Buf() for _ in range(DC)]]
        b_hD = [Buf(), Buf(), Buf()]
        b_qkbuf = [Buf(), Buf()]
        b_dx = [Buf(), Buf()]
        b_xsb = Buf()
        b_xin1 = Buf()
        b_xsq = Buf()
        b_sqj = Buf()
        b_vecs = Buf()
        b_stats = Buf()
        b_stmp = [Buf() for _ in range(4)]
        b_stgl = Buf()
        b_ab = Buf()
        b_abt = Buf()
        b_lnt = Buf()
        b_rows = Buf()
        b_bias = Buf()
        b_rbrow = Buf()
        b_rbbc = Buf()
        b_eb2 = Buf()
        b_routW = Buf()
        b_bdvv = Buf()
        b_vvsb = Buf()
        b_wtmall = Buf()
        b_wfm = [Buf(), Buf(), Buf()]
        b_wtm = Buf()
        b_wbc = Buf()
        b_lsb = Buf()
        b_sma = Buf()
        b_smb = Buf()
        b_smc = Buf()
        b_smt = Buf()
        b_ident = Buf()
        b_eps = Buf()
        b_c125 = Buf()
        b_ones = Buf()
        b_rowhot = Buf()
        b_xnsc = [Buf() for _ in range(NT)]
        b_hsc = [[Buf() for _ in range(NT)] for _ in range(3)]
        b_qksc = [Buf() for _ in range(NT)]
        b_vsc = [Buf() for _ in range(NT)]
        b_ysc = [Buf() for _ in range(NT)]
        b_ccin = Buf()
        b_ccout = Buf()

        P.emit("sync", [lambda e: e.dma_start(out=vecs_sb[:], in_=vecs_in)],
               writes=[b_vecs])
        rows_sb = x_sb[0:1, 0:2064]
        P.emit("sync", [lambda e: e.dma_start(out=rows_sb, in_=rows_in)],
               writes=[b_rows])
        P.emit("sync", [lambda e: e.dma_start(out=ident_sb[:], in_=ident_c)],
               writes=[b_ident])
        P.emit("sync", [lambda e: e.dma_start(out=ones_sb[:], in_=ones_c)],
               writes=[b_ones])
        P.emit("sync", [lambda e: e.dma_start(out=rowhot_sb[:], in_=rowhot_c)],
               writes=[b_rowhot])
        P.emit("gps", [lambda e: e.dma_start(out=eb2_sb[:], in_=eb2_in)],
               writes=[b_eb2])
        P.emit("gps", [lambda e: e.dma_start(
            out=routW_sb[:].rearrange("p (dc e) -> p dc e", dc=DC),
            in_=routW.rearrange("(dc p) e -> p dc e", p=128))],
            writes=[b_routW])

        def load_slot(slot, src2, wbuf):
            P.emit("gps", [lambda e, slot=slot, s=src2: e.dma_start(
                out=w_ring[:, slot * 8192:(slot + 1) * 8192], in_=s)],
                writes=[wbuf])

        def load_pair(base, srccat, wbuf):
            P.emit("gps", [lambda e, base=base, s=srccat: e.dma_start(
                out=w_ring[:, base * 16384:(base + 1) * 16384], in_=s)],
                writes=[wbuf])

        b_wpair = [Buf(), Buf()]
        load_slot(0, wqTp, b_wpair[0])
        load_slot(1, wkTp, b_wpair[0])
        load_slot(2, wvTp, b_wpair[1])

        P.emit("dve", [lambda e: e.tensor_add(
            xsq_junk[0:1, 0:1024], rows_sb[:, 0:1024],
            rows_sb[:, 1024:2048])],
            reads=[b_rows], writes=[b_xsq])
        P.emit("act", [lambda e: e.activation(
            out=bias_bf[0:1, :], in_=xsq_junk[0:1, 0:1024], func=AF.Copy)],
            reads=[b_xsq], writes=[b_bias])
        P.emit("act", [lambda e: e.activation(
            out=rbrow_bf[0:1, 0:15], in_=rows_sb[:, 2048:2063],
            func=AF.Copy)],
            reads=[b_rows], writes=[b_rbrow])
        P.emit("pe", [lambda e: e.matmul(
            ps_wbc[:, 0:15], ones_sb[0:1, :], rbrow_bf[0:1, 0:15],
            start=True, stop=True)],
            reads=[b_ones, b_rbrow], writes=[b_pswbc])
        P.emit("dve", [lambda e: e.tensor_copy(rb_bc[:], ps_wbc[:, 0:15])],
               reads=[b_pswbc], writes=[b_rbbc])
        P.emit("dve", [lambda e: e.memset(stats_acc[:], 0.0)],
               writes=[b_stats])
        P.emit("dve", [lambda e: e.memset(w_tm_t[:, 15:16], 0.0)],
               writes=[b_wtm])
        P.emit("dve", [lambda e: e.memset(vv_sb[:], 0.0)],
               writes=[b_vvsb])
        P.emit("dve", [lambda e: e.memset(eps_t[:], EPS)], writes=[b_eps])
        P.emit("dve", [lambda e: e.memset(c125_t[:], 0.125)], writes=[b_c125])

        seed([b_xsb], [b_xsb, b_rows])
        x_bufs = [x_sb[:, 0:4096], acc[:, 0:4096]]
        bx_bufs = [b_xsb, b_xin1]
        xn_tm = d_extra[:, 0:4096]
        b_xntm = b_dx[0]
        hbufs = [d_extra[:, 4096:8192], qk_bufs[:, 0:4096]]
        b_hbufs = [b_dx[1], b_qkbuf[0]]

        def ab_ln_tile(t):
            bi = t % 2
            xb, bxb = x_bufs[bi], bx_bufs[bi]
            src = x_in[t * TILE:(t + 1) * TILE, :].rearrange(
                "(c p) d -> p c d", p=128)
            P.emit("sync", [lambda e, d=xb, s=src: e.dma_start(
                out=d.rearrange("p (c d) -> p c d", c=4), in_=s)],
                writes=[bxb])
            for c in range(4):
                xc = xb[:, c * 1024:(c + 1) * 1024]
                P.emit("dve", [lambda e, xc=xc, c=c: e.reduce_sum(
                    ln_t[:, c:c + 1], xc, axis=AX.X)],
                    reads=[bxb], writes=[b_lnt])
                P.emit("act", [lambda e, xc=xc, c=c: e.activation(
                    out=xsq_junk[:, 0:1024], in_=xc, func=AF.Square,
                    accum_out=ln_t[:, 4 + c:5 + c])],
                    reads=[bxb], writes=[b_xsq, b_lnt])
            P.emit("dve", [lambda e: e.tensor_scalar_mul(
                ln_t[:, 8:12], ln_t[:, 0:4], 1.0 / 1024.0)],
                reads=[b_lnt], writes=[b_lnt])
            P.emit("dve", [lambda e: e.tensor_scalar_mul(
                ln_t[:, 4:8], ln_t[:, 4:8], 1.0 / 1024.0)],
                reads=[b_lnt], writes=[b_lnt])
            P.emit("dve", [lambda e: e.tensor_mul(
                ln_t[:, 12:16], ln_t[:, 8:12], ln_t[:, 8:12])],
                reads=[b_lnt], writes=[b_lnt])
            P.emit("dve", [lambda e: e.tensor_sub(
                ln_t[:, 4:8], ln_t[:, 4:8], ln_t[:, 12:16])],
                reads=[b_lnt], writes=[b_lnt])
            P.emit("act", [lambda e: e.activation(
                out=ln_t[:, 4:8], in_=ln_t[:, 4:8], func=AF.Sqrt,
                bias=eps_t[:, 0:1])],
                reads=[b_lnt, b_eps], writes=[b_lnt])
            P.emit("dve", [lambda e: e.reciprocal(
                ln_t[:, 12:16], ln_t[:, 4:8])],
                reads=[b_lnt], writes=[b_lnt])
            P.emit("dve", [lambda e: e.scalar_tensor_tensor(
                ln_t[:, 8:12], ln_t[:, 8:12], -1.0, ln_t[:, 12:16],
                ALU.mult, ALU.mult)],
                reads=[b_lnt], writes=[b_lnt])
            for c in range(4):
                xc = xb[:, c * 1024:(c + 1) * 1024]
                P.emit("dve", [lambda e, xc=xc, c=c: e.tensor_scalar(
                    xn_tm[:, c * 1024:(c + 1) * 1024], xc,
                    ln_t[:, 12 + c:13 + c], ln_t[:, 8 + c:9 + c],
                    ALU.mult, ALU.add)],
                    reads=[bxb, b_lnt], writes=[b_xntm])
            dst = xn_sc[t * TILE:(t + 1) * TILE, :].rearrange(
                "(c p) d -> p c d", p=128)
            P.emit("sync", [lambda e, d=dst: e.dma_start(
                out=d, in_=xn_tm.rearrange("p (c d) -> p c d", c=4))],
                reads=[b_xntm], writes=[b_xnsc[t]])

        def xn_quarter_load(q):
            qreads = [b_xnsc[q * TPQ + i] for i in range(TPQ)]
            for dc in range(DC):
                P.emit("actdma", [lambda e, q=q, dc=dc:
                                  e.dma_start_transpose(
                    xn_q[:, dc * QT:(dc + 1) * QT],
                    xn_sc[q * QT:(q + 1) * QT, dc * 128:(dc + 1) * 128])],
                    reads=qreads, writes=[b_xnq[dc]])

        psh_cnt = [0]

        def ab_proj_quarter(q):
            for ti in range(TPQ):
                t = q * TPQ + ti
                toff = ti * TILE
                for br in range(3):
                    hb = (t * 3 + br) % 2
                    hbuf, b_hbuf = hbufs[hb], b_hbufs[hb]
                    wslot = w_ring[:, br * 8192:(br + 1) * 8192]
                    for ec in range(DC):
                        par = psh_cnt[0] % 2
                        psh_cnt[0] += 1
                        fns = []
                        for dc in range(DC):
                            fns.append(
                                lambda e, w=wslot, dc=dc, ec=ec, toff=toff,
                                par=par:
                                e.matmul(
                                    ps_h[par][:, 0:512],
                                    w[:, dc * 1024 + ec * 128:
                                      dc * 1024 + ec * 128 + 128],
                                    xn_q[:, dc * QT + toff:
                                         dc * QT + toff + 512],
                                    start=(dc == 0), stop=(dc == DC - 1)))
                        P.emit("pe", fns,
                               reads=[b_wpair[0] if br < 2 else b_wpair[1]]
                               + b_xnq,
                               writes=[b_psh[par]])
                        st_r = (br * DC + ec) % 4
                        P.emit("act", [lambda e, ec=ec, par=par, st_r=st_r,
                                       hbuf=hbuf: e.activation(
                            out=hbuf[:, ec * 512:(ec + 1) * 512],
                            in_=ps_h[par][:, 0:512], func=AF.Copy,
                            accum_out=stats_tmp[:, st_r * 2:st_r * 2 + 1])],
                            reads=[b_psh[par]],
                            writes=[b_hbuf, b_stmp[st_r]])
                        P.emit("dve", [lambda e, ec=ec, st_r=st_r,
                                       hbuf=hbuf: e.scalar_tensor_tensor(
                            sq_junk[:, 0:512],
                            hbuf[:, ec * 512:(ec + 1) * 512], 1.0,
                            hbuf[:, ec * 512:(ec + 1) * 512],
                            ALU.mult, ALU.mult,
                            accum_out=stats_tmp[:, st_r * 2 + 1:
                                                st_r * 2 + 2])],
                            reads=[b_hbuf], writes=[b_sqj, b_stmp[st_r]])
                        cs = br * 16 + ec
                        P.emit("dve", [lambda e, cs=cs, st_r=st_r:
                                       e.tensor_add(
                            stats_acc[:, cs:cs + 16].rearrange(
                                "p (two ec) -> p (ec two)", two=2
                            )[:, 0:2] if False else
                            stats_acc[:, cs:cs + 9].rearrange(
                                "p (a b) -> p a b", a=1)[:, 0, 0:9:8],
                            stats_acc[:, cs:cs + 9].rearrange(
                                "p (a b) -> p a b", a=1)[:, 0, 0:9:8],
                            stats_tmp[:, st_r * 2:st_r * 2 + 2])],
                            reads=[b_stmp[st_r], b_stats], writes=[b_stats])
                    dsth = h_sc[br * D:(br + 1) * D, :].rearrange(
                        "(dc p) n -> p dc n", p=128)
                    for half in range(2):
                        P.emit("sync", [lambda e, d=dsth, half=half, t=t,
                                        hbuf=hbuf: e.dma_start(
                            out=d[:, half * 4:half * 4 + 4,
                                  t * TILE:(t + 1) * TILE],
                            in_=hbuf[:, half * 2048:half * 2048 + 2048]
                            .rearrange("p (dc n) -> p dc n", dc=4))],
                            reads=[b_hbuf], writes=[b_hsc[br][t]])

        def f_router(tq):
            ti = tq % TPQ
            toff = ti * TILE
            for c in range(4):
                fns = []
                for dc in range(DC):
                    fns.append(lambda e, dc=dc, c=c, toff=toff:
                               e.matmul(
                        ps_wbc[:, c * 128:c * 128 + 15],
                        xn_q[:, dc * QT + toff + c * 128:
                             dc * QT + toff + c * 128 + 128],
                        routW_sb[:, dc * 15:(dc + 1) * 15],
                        start=(dc == 0), stop=(dc == DC - 1)))
                P.emit("pe", fns, reads=b_xnq + [b_routW], writes=[b_pswbc])
                lc = l_sb[:, c * 16:c * 16 + 15]
                P.emit("dve", [lambda e, lc=lc, c=c: e.tensor_add(
                    lc, ps_wbc[:, c * 128:c * 128 + 15], rb_bc[:])],
                    reads=[b_pswbc, b_rbbc], writes=[b_lsb])
            for c in range(4):
                lc = l_sb[:, c * 16:c * 16 + 15]
                P.emit("dve", [lambda e, lc=lc: e.reduce_max(
                    sm_t[:, 0:1], lc, axis=AX.X)],
                    reads=[b_lsb], writes=[b_smt])
                P.emit("dve", [lambda e, lc=lc: e.tensor_scalar(
                    sm_a[:, 0:15], lc, sm_t[:, 0:1], None, ALU.is_ge)],
                    reads=[b_lsb, b_smt], writes=[b_sma])
                P.emit("dve", [lambda e, lc=lc: e.scalar_tensor_tensor(
                    sm_b[:, 0:15], sm_a[:, 0:15], -30000.0, lc,
                    ALU.mult, ALU.add)],
                    reads=[b_sma, b_lsb], writes=[b_smb])
                P.emit("dve", [lambda e: e.reduce_max(
                    sm_t[:, 1:2], sm_b[:, 0:15], axis=AX.X)],
                    reads=[b_smb], writes=[b_smt])
                P.emit("dve", [lambda e: e.tensor_scalar(
                    sm_a[:, 0:15], sm_b[:, 0:15], sm_t[:, 1:2], None,
                    ALU.is_ge)],
                    reads=[b_smb, b_smt], writes=[b_sma])
                P.emit("dve", [lambda e: e.scalar_tensor_tensor(
                    sm_b[:, 0:15], sm_a[:, 0:15], -30000.0, sm_b[:, 0:15],
                    ALU.mult, ALU.add)],
                    reads=[b_sma, b_smb], writes=[b_smb])
                P.emit("dve", [lambda e: e.reduce_max(
                    sm_t[:, 2:3], sm_b[:, 0:15], axis=AX.X)],
                    reads=[b_smb], writes=[b_smt])
                P.emit("dve", [lambda e: e.tensor_scalar_mul(
                    sm_t[:, 3:4], sm_t[:, 0:1], -1.0)],
                    reads=[b_smt], writes=[b_smt])
                P.emit("act", [lambda e, lc=lc: e.activation(
                    out=sm_c[:, 0:15], in_=lc, func=AF.Exp,
                    bias=sm_t[:, 3:4], scale=1.0)],
                    reads=[b_lsb, b_smt], writes=[b_smc])
                P.emit("dve", [lambda e: e.reduce_sum(
                    sm_t[:, 4:5], sm_c[:, 0:15], axis=AX.X)],
                    reads=[b_smc], writes=[b_smt])
                P.emit("dve", [lambda e: e.reciprocal(
                    sm_t[:, 5:6], sm_t[:, 4:5])],
                    reads=[b_smt], writes=[b_smt])
                P.emit("dve", [lambda e, lc=lc: e.tensor_scalar(
                    sm_a[:, 0:15], lc, sm_t[:, 2:3], None, ALU.is_ge)],
                    reads=[b_lsb, b_smt], writes=[b_sma])
                P.emit("dve", [lambda e: e.tensor_mul(
                    sm_c[:, 0:15], sm_c[:, 0:15], sm_a[:, 0:15])],
                    reads=[b_sma, b_smc], writes=[b_smc])
                P.emit("dve", [lambda e: e.tensor_scalar_mul(
                    w_tm_t[:, 0:15], sm_c[:, 0:15], sm_t[:, 5:6])],
                    reads=[b_smc, b_smt], writes=[b_wtm])
                P.emit("dve", [lambda e, tq=tq, c=c: e.tensor_copy(
                    w_tm_all[:, (tq * 4 + c) * 16:(tq * 4 + c) * 16 + 16],
                    w_tm_t[:, 0:16])],
                    reads=[b_wtm], writes=[b_wtmall])

        ab_order = list(range(1, NQ)) + [0]
        for i in range(NQ + 1):
            if i < NQ:
                q = ab_order[i]
                for ti in range(TPQ):
                    ab_ln_tile(q * TPQ + ti)
            if i >= 1:
                qp = ab_order[i - 1]
                ab_proj_quarter(qp)
                for ti in range(TPQ):
                    f_router(qp * TPQ + ti)
            if i < NQ:
                xn_quarter_load(ab_order[i])

        def emit_phase_c():
            P.emit("sync", [lambda e: e.dma_start(out=cc_in, in_=stats_acc[:])],
                   reads=[b_stats], writes=[b_ccin])
            P.emit("cc", [lambda e: e.collective_compute(
                "AllReduce", ALU.add,
                replica_groups=[list(range(NCORES))],
                ins=[cc_in.opt()], outs=[cc_out.opt()])],
                reads=[b_ccin], writes=[b_ccout])
            P.emit("gps", [lambda e: e.dma_start(out=stats_gl[:], in_=cc_out)],
                   reads=[b_ccout], writes=[b_stgl])
            sgl = stats_gl[:].rearrange("p (br k c) -> p br k c", br=3, k=2)
            vec3 = vecs_sb[:, 0:48].rearrange("p (br k c) -> p br k c", br=3, k=2)
            abc = ab_coef[:].rearrange("p (br k c) -> p br k c", br=3, k=2)
            abt = ab_tmp[:].rearrange("p (g c) -> p g c", g=3)
            sums3 = sgl[:, :, 0, :]
            sqs3 = sgl[:, :, 1, :]
            gam3 = vec3[:, :, 0, :]
            bet3 = vec3[:, :, 1, :]
            A3 = abc[:, :, 0, :]
            B3 = abc[:, :, 1, :]
            mean3 = abt[:, 0, :].rearrange("p (br c) -> p br c", br=1)[:, 0, :]
            mean3 = ab_tmp[:, 0:8]
            mean_t = ab_tmp[:, 0:24].rearrange("p (br c) -> p br c", br=3)
            ee_t = stats_tmp[:, 0:8]
            P.emit("dve", [lambda e: e.tensor_scalar_mul(
                mean_t, sums3, 1.0 / NTOK_GLOBAL)],
                reads=[b_stgl], writes=[b_abt])
            P.emit("dve", [lambda e: e.tensor_scalar_mul(
                B3, sqs3, 1.0 / NTOK_GLOBAL)],
                reads=[b_stgl], writes=[b_ab])
            P.emit("dve", [lambda e: e.scalar_tensor_tensor(
                A3, mean_t, 1.0, mean_t, ALU.mult, ALU.mult)],
                reads=[b_abt], writes=[b_ab])
            P.emit("dve", [lambda e: e.tensor_sub(B3, B3, A3)],
                   reads=[b_ab], writes=[b_ab])
            P.emit("act", [lambda e: e.activation(
                out=ab_coef[:].rearrange("p (br k c) -> p br k c",
                                         br=3, k=2)[:, :, 1, :],
                in_=ab_coef[:].rearrange("p (br k c) -> p br k c",
                                         br=3, k=2)[:, :, 1, :],
                func=AF.Sqrt, bias=eps_t[:, 0:1])],
                reads=[b_ab, b_eps], writes=[b_ab])
            P.emit("dve", [lambda e: e.reciprocal(A3, B3)],
                   reads=[b_ab], writes=[b_ab])
            P.emit("dve", [lambda e: e.tensor_mul(A3, A3, gam3)],
                   reads=[b_ab, b_vecs], writes=[b_ab])
            P.emit("dve", [lambda e: e.tensor_mul(mean_t, mean_t, A3)],
                   reads=[b_ab, b_abt], writes=[b_abt])
            P.emit("dve", [lambda e: e.tensor_scalar(
                B3, bet3, 4.0, -4.0, ALU.mult, ALU.add)],
                reads=[b_vecs, b_ab], writes=[b_ab])
            P.emit("dve", [lambda e: e.scalar_tensor_tensor(
                B3, mean_t, -4.0, B3, ALU.mult, ALU.add)],
                reads=[b_abt, b_ab], writes=[b_ab])
            P.emit("dve", [lambda e: e.tensor_scalar_mul(A3, A3, 4.0)],
                   reads=[b_ab], writes=[b_ab])


        d_stage = qk_bufs[:, 0:4096]
        d_qv = qk_bufs[:, 4096:8192]
        d_k = d_extra[:, 4096:8192]
        sig_dst = [d_qv, d_k, d_qv]
        sig_dbuf = [b_qkbuf[1], b_dx[1], b_qkbuf[1]]

        def d_load(t, br):
            srch = h_sc[br * D:(br + 1) * D, :].rearrange(
                "(dc p) n -> p dc n", p=128)
            for half in range(2):
                P.emit("sync", [lambda e, s=srch, half=half, t=t, br=br:
                                e.dma_start(
                    out=d_stage[:, half * 2048:half * 2048 + 2048]
                    .rearrange("p (dc n) -> p dc n", dc=4),
                    in_=s[:, half * 4:half * 4 + 4,
                          t * TILE:(t + 1) * TILE])],
                    reads=[b_hsc[br][t]], writes=[b_qkbuf[0]])

        def d_sigs(t, br):
            for dc in range(DC):
                P.emit("act", [lambda e, br=br, dc=dc: e.activation(
                    out=sig_dst[br][:, dc * 512:(dc + 1) * 512],
                    in_=d_stage[:, dc * 512:(dc + 1) * 512],
                    func=AF.Sigmoid,
                    scale=ab_coef[:, br * 16 + dc:br * 16 + dc + 1],
                    bias=ab_coef[:, br * 16 + 8 + dc:
                                 br * 16 + 8 + dc + 1])],
                    reads=[b_qkbuf[0], b_ab], writes=[sig_dbuf[br]])

        def d_qkmul(t):
            P.emit("dve", [lambda e: e.tensor_mul(
                d_stage[:], d_qv[:], d_k[:])],
                reads=[b_qkbuf[1], b_dx[1]], writes=[b_qkbuf[0]])

        def d_stq(t, which):
            srcb, dstd, btok, bsrc = (
                (d_stage, qk_sc, b_qksc[t], b_qkbuf[0]) if which == 0 else
                (d_qv, v_sc, b_vsc[t], b_qkbuf[1]))
            dd = dstd.rearrange("(dc p) n -> p dc n", p=128)
            for half in range(2):
                P.emit("sync", [lambda e, s=srcb, d=dd, half=half, t=t:
                                e.dma_start(
                    out=d[:, half * 4:half * 4 + 4,
                          t * TILE:(t + 1) * TILE],
                    in_=s[:, half * 2048:half * 2048 + 2048].rearrange(
                        "p (dc n) -> p dc n", dc=4))],
                    reads=[bsrc], writes=[btok])

        def d_vtm(t):
            for c in range(4):
                reg = d_extra[:, c * 1024:c * 1024 + 1024]
                P.emit("actdma", [lambda e, reg=reg, t=t, c=c:
                                  e.dma_start_transpose(
                    reg, v_sc[:, t * TILE + c * 128:
                              t * TILE + (c + 1) * 128])],
                    reads=[b_vsc[t]], writes=[b_dx[0]])

        def d_vv(t):
            paro = pso_cnt[0] % 2
            pso_cnt[0] += 1
            fns = []
            for c in range(4):
                reg = d_extra[:, c * 1024:c * 1024 + 1024]
                for hp in range(8):
                    fns.append(lambda e, reg=reg, hp=hp, c=c, paro=paro:
                               e.matmul(
                        ps_out[paro][:, hp * 128:(hp + 1) * 128],
                        reg[:, hp * 128:(hp + 1) * 128],
                        reg[:, hp * 128:(hp + 1) * 128],
                        start=(c == 0 and hp in (0, 4)),
                        stop=(c == 3 and hp in (3, 7)),
                        skip_group_check=True))
            P.emit("pe", fns, reads=[b_dx[0]], writes=[b_psout[paro]])
            P.emit("dve", [lambda e, paro=paro: e.tensor_add(
                vv_sb[:], vv_sb[:], ps_out[paro][:])],
                reads=[b_psout[paro], b_vvsb], writes=[b_vvsb])

        def d_bdvv():
            P.emit("dve", [lambda e: e.memset(bd_vv[:], 0.0)],
                   reads=[b_dx[1]], writes=[b_bdvv])
            for dc in range(DC):
                for half in range(2):
                    P.emit("dve", [lambda e, dc=dc, half=half:
                                   e.tensor_scalar_mul(
                        bd_vv[half * 64:half * 64 + 64,
                              dc * 128 + half * 64:
                              dc * 128 + half * 64 + 64],
                        vv_sb[half * 64:half * 64 + 64,
                              dc * 128 + half * 64:
                              dc * 128 + half * 64 + 64], 0.125)],
                        reads=[b_vvsb], writes=[b_bdvv])

        def d_sched(gp):
            t, step = gp // 5, gp % 5
            if t >= NT:
                if gp == NT * 5:
                    d_bdvv()
                return
            if step == 0:
                d_load(t, 0)
                d_sigs(t, 0)
                d_load(t, 1)
            elif step == 1:
                d_sigs(t, 1)
                d_qkmul(t)
                d_stq(t, 0)
            elif step == 2:
                d_load(t, 2)
                d_sigs(t, 2)
                d_stq(t, 1)
            elif step == 3:
                d_vtm(t)
            else:
                d_vv(t)

        seed(b_hraw, [b_hD[0]])
        seed(b_hscl[0], [b_hD[1]])
        seed(b_hscl[1], [b_hD[2]])
        seed(b_acc[:4], [b_xin1])

        h_raw = h_arena[:, 0:4096]
        h_scl = [h_arena[:, 4096:8192], h_arena[:, 8192:12288]]
        attn_fm = d_extra[:, 0:4096]
        b_attnfm = b_dx[0]
        pso_cnt = [0]
        hsl_cnt = [0]

        pair_defs = []
        for q in range(NQ):
            pair_defs.append(("pair", sWcat))
            for e_i in range(NEXP):
                pair_defs.append(
                    ("pair", eWcat[e_i:e_i + 1, :, :].rearrange(
                        "one p m -> p (one m)")))
        pair_slots = []

        def emit_pair_load(g):
            base = g % 2
            kind, ap = pair_defs[g]
            if kind == "pair":
                load_pair(base, ap, b_wpair[base])
            else:
                load_slot(base * 2, ap, b_wpair[base])
            pair_slots.append((base * 2, base * 2 + 1, b_wpair[base]))

        emit_pair_load(0)
        emit_pair_load(1)

        def f_first_and_h(p, tq, slot_a, wbuf, shared, hsl):
            ti = tq % TPQ
            toff = ti * TILE
            wslot = w_ring[:, slot_a * 8192:(slot_a + 1) * 8192]
            if not shared:
                P.emit("pe", [lambda e, p=p, ti=ti: e.matmul(
                    ps_wbc[:, 0:512],
                    rowhot_sb[:, (p - 1) * 128:p * 128],
                    w_fm_sb[0:15, ti * 512:(ti + 1) * 512],
                    start=True, stop=True)],
                    reads=[b_rowhot, b_wfm[ti]], writes=[b_pswbc])
                P.emit("dve", [lambda e: e.tensor_copy(
                    wbc_sb[:], ps_wbc[:, 0:512])],
                    reads=[b_pswbc], writes=[b_wbc])
            dstbufs = []
            for ec in range(DC):
                par = psh_cnt[0] % 2
                psh_cnt[0] += 1
                fns = []
                for dc in range(DC):
                    fns.append(lambda e, w=wslot, dc=dc, ec=ec, toff=toff,
                               par=par:
                               e.matmul(
                        ps_h[par][:, 0:512],
                        w[:, dc * 1024 + ec * 128:dc * 1024 + ec * 128 + 128],
                        xn_q[:, dc * QT + toff:dc * QT + toff + 512],
                        start=(dc == 0), stop=(dc == DC - 1)))
                P.emit("pe", fns, reads=[wbuf] + b_xnq,
                       writes=[b_psh[par]])
                bcol = 48 if shared else (56 + (p - 1) * 8)
                P.emit("act", [lambda e, ec=ec, par=par, bcol=bcol:
                               e.activation(
                    out=h_raw[:, ec * 512:(ec + 1) * 512],
                    in_=ps_h[par][:, 0:512], func=AF.Relu,
                    bias=vecs_sb[:, bcol + ec:bcol + ec + 1])],
                    reads=[b_psh[par], b_vecs], writes=[b_hraw[ec]])
                if shared:
                    dstbufs.append(b_hraw[ec])
                else:
                    P.emit("dve", [lambda e, ec=ec, hsl=hsl: e.tensor_mul(
                        h_scl[hsl][:, ec * 512:(ec + 1) * 512],
                        h_raw[:, ec * 512:(ec + 1) * 512], wbc_sb[:])],
                        reads=[b_hraw[ec], b_wbc], writes=[b_hscl[hsl][ec]])
                    dstbufs.append(b_hscl[hsl][ec])
            return dstbufs

        def f_second(lhs_t, lhs_bufs, slot_b, wbuf, tq, first_pass):
            ti = tq % TPQ
            wslot = w_ring[:, slot_b * 8192:(slot_b + 1) * 8192]
            for c in range(4):
                paro = pso_cnt[0] % 2
                pso_cnt[0] += 1
                fns = []
                for hc in range(DC):
                    for half in range(2):
                        fns.append(lambda e, w=wslot, hc=hc, half=half, c=c,
                                   paro=paro:
                                   e.matmul(
                            ps_out[paro][:, half * 512:half * 512 + 512],
                            lhs_t[:, hc * 512 + c * 128:
                                  hc * 512 + c * 128 + 128],
                            w[:, hc * 1024 + half * 512:
                              hc * 1024 + half * 512 + 512],
                            start=(hc == 0), stop=(hc == DC - 1)))
                P.emit("pe", fns, reads=lhs_bufs + [wbuf],
                       writes=[b_psout[paro]])
                aslab = acc[:, (ti * 4 + c) * 1024:(ti * 4 + c + 1) * 1024]
                ab = b_acc[ti * 4 + c]
                if first_pass:
                    P.emit("dve", [lambda e, a=aslab, paro=paro:
                                   e.tensor_copy(a, ps_out[paro][:])],
                           reads=[b_psout[paro]], writes=[ab])
                else:
                    P.emit("dve", [lambda e, a=aslab, paro=paro:
                                   e.tensor_add(a, a, ps_out[paro][:])],
                           reads=[b_psout[paro], ab], writes=[ab])

        wslot_wp = qk_bufs[:, 0:8192]
        b_wp = Buf()

        def load_wpt():
            P.emit("gps", [lambda e: e.dma_start(out=wslot_wp, in_=wpTp)],
                   reads=[b_qkbuf[0], b_qkbuf[1]], writes=[b_wp])

        srcqk_g = qk_sc.rearrange("(dc p) n -> p dc n", p=128)

        def attn_slot_a(t):
            qkb = d_extra[:, 0:4096]
            for half in range(2):
                P.emit("sync", [lambda e, half=half, t=t, qkb=qkb:
                                e.dma_start(
                    out=qkb[:, half * 2048:half * 2048 + 2048]
                    .rearrange("p (dc n) -> p dc n", dc=4),
                    in_=srcqk_g[:, half * 4:half * 4 + 4,
                                t * TILE:(t + 1) * TILE])],
                    reads=[b_qksc[t]], writes=[b_dx[0]])
            for c in range(4):
                P.emit("pe", [lambda e, t=t, c=c: e.transpose(
                    ps_tr[0:16, :],
                    w_tm_all[:, (t * 4 + c) * 16:(t * 4 + c) * 16 + 16],
                    ident_sb[:])],
                    reads=[b_wtmall, b_ident], writes=[b_pstr])
                P.emit("dve", [lambda e, c=c: e.tensor_copy(
                    w_fm_sb[0:16, 1024 + c * 128:1024 + (c + 1) * 128],
                    ps_tr[0:16, :])],
                    reads=[b_pstr], writes=[b_wfm[2]])
            for dc in range(DC):
                par = psh_cnt[0] % 2
                psh_cnt[0] += 1
                P.emit("pe", [lambda e, dc=dc, par=par, qkb=qkb: e.matmul(
                    ps_h[par][:, 0:512], bd_vv[:, dc * 128:(dc + 1) * 128],
                    qkb[:, dc * 512:(dc + 1) * 512], start=True, stop=True)],
                    reads=[b_bdvv, b_dx[0]], writes=[b_psh[par]])
                P.emit("act", [lambda e, dc=dc, par=par: e.activation(
                    out=attn_fm[:, dc * 512:(dc + 1) * 512],
                    in_=ps_h[par][:, 0:512], func=AF.Copy)],
                    reads=[b_psh[par]], writes=[b_attnfm])

        def attn_slot_b(t):
            srcy = y_out[t * TILE:(t + 1) * TILE, :].rearrange(
                "(c p) d -> p c d", p=128)
            P.emit("sync", [lambda e, s=srcy: e.dma_start(
                out=x_sb[:].rearrange("p (c d) -> p c d", c=4), in_=s)],
                reads=[b_ysc[t]], writes=[b_xsb])
            for c in range(4):
                paro = pso_cnt[0] % 2
                pso_cnt[0] += 1
                fns = []
                for hc in range(DC):
                    for half in range(2):
                        fns.append(lambda e, hc=hc, half=half, c=c,
                                   paro=paro:
                                   e.matmul(
                            ps_out[paro][:, half * 512:half * 512 + 512],
                            attn_fm[:, hc * 512 + c * 128:
                                    hc * 512 + c * 128 + 128],
                            wslot_wp[:, hc * 1024 + half * 512:
                                     hc * 1024 + half * 512 + 512],
                            start=(hc == 0), stop=False))
                for half in range(2):
                    fns.append(lambda e, half=half, c=c, paro=paro:
                               e.matmul(
                        ps_out[paro][:, half * 512:half * 512 + 512],
                        w_fm_sb[0:15, 1024 + c * 128:1024 + (c + 1) * 128],
                        eb2_sb[0:15, half * 512:half * 512 + 512],
                        start=False, stop=False))
                for half in range(2):
                    fns.append(lambda e, half=half, paro=paro:
                               e.matmul(
                        ps_out[paro][:, half * 512:half * 512 + 512],
                        ones_sb[0:1, 0:128],
                        bias_bf[0:1, half * 512:half * 512 + 512],
                        start=False, stop=True))
                P.emit("pe", fns,
                       reads=[b_attnfm, b_wp, b_wfm[2], b_eb2,
                              b_ones, b_bias],
                       writes=[b_psout[paro]])
                P.emit("dve", [lambda e, c=c, paro=paro: e.tensor_add(
                    x_sb[:, c * 1024:(c + 1) * 1024],
                    x_sb[:, c * 1024:(c + 1) * 1024], ps_out[paro][:])],
                    reads=[b_psout[paro], b_xsb], writes=[b_xsb])
            dsty = y_out[t * TILE:(t + 1) * TILE, :].rearrange(
                "(c p) d -> p c d", p=128)
            P.emit("sync", [lambda e, d=dsty: e.dma_start(
                out=d, in_=x_sb[:].rearrange("p (c d) -> p c d", c=4))],
                reads=[b_xsb], writes=[b_xsb, b_ysc[t]])

        can_inter = (NQ > 1) and (NT * 5 + 1 <= (NQ - 1) * 16)
        N_INTER = (NQ - 1) * TPQ if can_inter else 0

        def xn_next_load(q):
            qreads = [b_xnsc[q * TPQ + i] for i in range(TPQ)]
            for dc in range(DC):
                P.emit("actdma", [lambda e, q=q, dc=dc:
                                  e.dma_start_transpose(
                    xn_q[:, dc * QT:(dc + 1) * QT],
                    xn_sc[q * QT:(q + 1) * QT, dc * 128:(dc + 1) * 128])],
                    reads=qreads, writes=[b_xnq[dc]])
            for ti in range(TPQ):
                tq0 = q * TPQ + ti
                for c in range(4):
                    P.emit("pe", [lambda e, tq0=tq0, c=c: e.transpose(
                        ps_tr[0:16, :],
                        w_tm_all[:, (tq0 * 4 + c) * 16:
                                 (tq0 * 4 + c) * 16 + 16],
                        ident_sb[:])],
                        reads=[b_wtmall, b_ident], writes=[b_pstr])
                    P.emit("dve", [lambda e, c=c, ti=ti: e.tensor_copy(
                        w_fm_sb[0:16, ti * 512 + c * 128:
                                ti * 512 + (c + 1) * 128],
                        ps_tr[0:16, :])],
                        reads=[b_pstr], writes=[b_wfm[ti]])

        def wfm_rebuild(q):
            for ti in range(TPQ):
                tq0 = q * TPQ + ti
                for c in range(4):
                    P.emit("pe", [lambda e, tq0=tq0, c=c: e.transpose(
                        ps_tr[0:16, :],
                        w_tm_all[:, (tq0 * 4 + c) * 16:
                                 (tq0 * 4 + c) * 16 + 16],
                        ident_sb[:])],
                        reads=[b_wtmall, b_ident], writes=[b_pstr])
                    P.emit("dve", [lambda e, c=c, ti=ti: e.tensor_copy(
                        w_fm_sb[0:16, ti * 512 + c * 128:
                                ti * 512 + (c + 1) * 128],
                        ps_tr[0:16, :])],
                        reads=[b_pstr], writes=[b_wfm[ti]])

        for q in range(NQ):
            if q == 0:
                wfm_rebuild(0)
                emit_phase_c()
            if q == NQ - 1 and can_inter:
                load_wpt()
            for p in range(16):
                g = q * 16 + p
                for ti in range(TPQ):
                    tq = q * TPQ + ti
                    sa, sb, wb = pair_slots[g]
                    shared = (p == 0)
                    hsl = hsl_cnt[0] % 2
                    hsl_cnt[0] += 1
                    lhs_bufs = f_first_and_h(p, tq, sa, wb, shared, hsl)
                    lhs_t = h_raw if shared else h_scl[hsl]
                    f_second(lhs_t, lhs_bufs, sb, wb, tq,
                             first_pass=shared)
                d_sched(g)
                if q == NQ - 1 and can_inter:
                    if p % 2 == 0 and p // 2 < N_INTER:
                        attn_slot_a(p // 2)
                    elif p % 2 == 1 and p // 2 < N_INTER:
                        attn_slot_b(p // 2)
                if g + 2 < len(pair_defs):
                    emit_pair_load(g + 2)
                if p == 15 and q + 1 < NQ:
                    xn_next_load(q + 1)
            for ti in range(TPQ):
                tq = q * TPQ + ti
                src = x_in[tq * TILE:(tq + 1) * TILE, :].rearrange(
                    "(c p) d -> p c d", p=128)
                P.emit("sync", [lambda e, s=src: e.dma_start(
                    out=x_sb[:].rearrange("p (c d) -> p c d", c=4), in_=s)],
                    writes=[b_xsb])
                P.emit("dve", [lambda e, ti=ti: e.tensor_add(
                    x_sb[:], acc[:, ti * 4096:(ti + 1) * 4096], x_sb[:])],
                    reads=[b_acc[ti * 4], b_acc[ti * 4 + 1],
                           b_acc[ti * 4 + 2], b_acc[ti * 4 + 3], b_xsb],
                    writes=[b_xsb])
                dsty = y_out[tq * TILE:(tq + 1) * TILE, :].rearrange(
                    "(c p) d -> p c d", p=128)
                P.emit("sync", [lambda e, d=dsty: e.dma_start(
                    out=d, in_=x_sb[:].rearrange("p (c d) -> p c d", c=4))],
                    reads=[b_xsb], writes=[b_xsb, b_ysc[tq]])

        if not can_inter:
            load_wpt()
        for t in range(N_INTER, NT):
            attn_slot_a(t)
            attn_slot_b(t)

        P.final_wait("sync", [("qsp%d" % i, P.cnt["qsp%d" % i])
                              for i in range(NQSEM)
                              if P.cnt["qsp%d" % i] > 0])
        P.flush()
    return nc


def kernel(L0_attn_x, L0_attn_Wq, L0_attn_gq, L0_attn_bq, L0_attn_Wk,
           L0_attn_gk, L0_attn_bk, L0_attn_Wv, L0_attn_gv, L0_attn_bv,
           L0_attn_Wp, L0_attn_bp, L0_ffn_router_W, L0_ffn_router_b,
           L0_ffn_sW1, L0_ffn_sb1, L0_ffn_sW2, L0_ffn_sb2, L0_ffn_eW1,
           L0_ffn_eb1, L0_ffn_eW2, L0_ffn_eb2):
    f = lambda a: np.ascontiguousarray(np.asarray(a, dtype=np.float32))
    x = f(L0_attn_x)
    B, T, _ = x.shape

    def pc(v):
        return np.ascontiguousarray(f(v).reshape(8, 128).T)

    vecs = np.concatenate(
        [pc(L0_attn_gq), pc(L0_attn_bq), pc(L0_attn_gk), pc(L0_attn_bk),
         pc(L0_attn_gv), pc(L0_attn_bv), pc(L0_ffn_sb1),
         np.ascontiguousarray(
             np.transpose(f(L0_ffn_eb1).reshape(15, 8, 128), (2, 0, 1))
             .reshape(128, 120))],
        axis=1)
    rows = np.zeros((1, 2064), dtype=np.float32)
    rows[0, 0:1024] = f(L0_attn_bp)
    rows[0, 1024:2048] = f(L0_ffn_sb2)
    rows[0, 2048:2063] = f(L0_ffn_router_b)

    def pm(w):
        w = f(w)
        return np.ascontiguousarray(
            w.reshape(8, 128, w.shape[1]).transpose(1, 0, 2).reshape(128, -1))

    eW1f, eW2f = f(L0_ffn_eW1), f(L0_ffn_eW2)
    eWcat = np.ascontiguousarray(np.stack(
        [np.concatenate([pm(eW1f[e]), pm(eW2f[e])], axis=1)
         for e in range(15)], axis=0))
    common = {
        "wqTp": pm(f(L0_attn_Wq).T),
        "wkTp": pm(f(L0_attn_Wk).T),
        "wvTp": pm(f(L0_attn_Wv).T),
        "wpTp": pm(f(L0_attn_Wp).T),
        "sWcat": np.concatenate([pm(L0_ffn_sW1), pm(L0_ffn_sW2)], axis=1),
        "eWcat": eWcat,
        "routW": f(L0_ffn_router_W),
        "eb2": f(L0_ffn_eb2),
        "vecs": vecs,
        "rows": rows,
    }
    in_maps = []
    for b in range(B):
        m = dict(common)
        m["x"] = np.ascontiguousarray(x[b])
        in_maps.append(m)

    nc = build_nc(T)
    res = run_bass_kernel_spmd(nc, in_maps, core_ids=list(range(B)))
    y = np.stack([res.results[i]["y"] for i in range(B)], axis=0)
    return y


# version 22
# speedup vs baseline: 1.0256x; 1.0000x over previous
import numpy as np
import ml_dtypes

import concourse.bass as bass
import concourse.mybir as mybir
from concourse.bass_utils import run_bass_kernel_spmd

F32 = mybir.dt.float32
BF16 = mybir.dt.bfloat16
AF = mybir.ActivationFunctionType
ALU = mybir.AluOpType
AX = mybir.AxisListType

D = 1024
DC = 8
NEXP = 15
NCORES = 8
TILE = 512
EPS = 1e-5


class Buf:
    __slots__ = ("ready", "readers")

    def __init__(self):
        self.ready = None
        self.readers = []


def seed(dst_bufs, src_bufs):
    toks = []
    for s in src_bufs:
        if s.ready is not None:
            toks.append(s.ready)
        toks.extend(s.readers)
    for d in dst_bufs:
        d.ready = None
        d.readers = list(toks)


NQSEM = 8


class Prog:
    ENG = {
        "pe": ("tensor", "pe", 1, 1),
        "act": ("scalar", "act", 1, 1),
        "dve": ("vector", "dve", 1, 1),
        "sync": ("sync", "qsp", 16, NQSEM),
        "actdma": ("scalar", "qact", 16, NQSEM),
        "gps": ("gpsimd", "qgps", 16, NQSEM),
        "cc": ("gpsimd", "cc", 1, 1),
    }

    def __init__(self, block, sems):
        self.block = block
        self.sems = sems
        self.cnt = {k: 0 for k in sems}
        self.wm = {}
        self.qidx = {}
        self.streams = {s: [] for s in
                        ("tensor", "scalar", "vector", "sync", "gpsimd")}

    def emit(self, eng, fns, reads=(), writes=(), waits=()):
        sec_name, semkey, inc, ring = self.ENG[eng]
        if ring > 1:
            i = self.qidx.get(eng, 0)
            self.qidx[eng] = i + 1
            semkey = "%s%d" % (semkey, i % ring)
        req = {}

        def add(tok):
            if tok is None:
                return
            sk, v = tok
            if req.get(sk, -1) < v:
                req[sk] = v

        for b in reads:
            add(b.ready)
        for b in writes:
            add(b.ready)
            for t in b.readers:
                add(t)
        for t in waits:
            add(t)
        if inc == 16 and self.cnt[semkey] > 0:
            add((semkey, self.cnt[semkey]))
        wm = self.wm.setdefault(sec_name, {})
        todo = [(sk, v) for sk, v in req.items() if wm.get(sk, 0) < v]
        self.cnt[semkey] += inc
        tok = (semkey, self.cnt[semkey])
        self.streams[sec_name].append((todo, fns, semkey, inc))
        for sk, v in todo:
            if wm.get(sk, 0) < v:
                wm[sk] = v
        for b in reads:
            b.readers.append(tok)
        for b in writes:
            b.ready = tok
            b.readers = []
        return tok

    def final_wait(self, eng, waits):
        sec_name = self.ENG[eng][0]
        self.streams[sec_name].append((list(waits), [], None, 0))

    def flush(self):
        sems = self.sems
        for sec_name, items in self.streams.items():
            if not items:
                continue
            sec = getattr(self.block, sec_name)

            @sec
            def _(eng_obj, items=items):
                for todo, fns, semkey, inc in items:
                    for sk, v in todo:
                        eng_obj.wait_ge(sems[sk], v)
                    inst = None
                    for fn in fns:
                        inst = fn(eng_obj)
                    if semkey is not None:
                        inst.then_inc(sems[semkey], inc)


def build_nc(T):
    assert T % TILE == 0
    NT = T // TILE
    QT = min(1024, T)
    NQ = T // QT
    TPQ = QT // TILE
    NTOK_GLOBAL = float(NCORES * T)

    nc = bass.Bass("TRN2", target_bir_lowering=False)

    x_in = nc.dram_tensor("x", [T, D], F32, kind="ExternalInput").ap()
    wqTp = nc.dram_tensor("wqTp", [128, 8192], F32, kind="ExternalInput").ap()
    wkTp = nc.dram_tensor("wkTp", [128, 8192], F32, kind="ExternalInput").ap()
    wvTp = nc.dram_tensor("wvTp", [128, 8192], F32, kind="ExternalInput").ap()
    wpTp = nc.dram_tensor("wpTp", [128, 8192], F32, kind="ExternalInput").ap()
    sWcat = nc.dram_tensor("sWcat", [128, 16384], F32, kind="ExternalInput").ap()
    eWcat = nc.dram_tensor("eWcat", [NEXP, 128, 16384], F32,
                           kind="ExternalInput").ap()
    routW = nc.dram_tensor("routW", [D, NEXP], F32, kind="ExternalInput").ap()
    eb2_in = nc.dram_tensor("eb2", [NEXP, D], F32, kind="ExternalInput").ap()
    vecs_in = nc.dram_tensor("vecs", [128, 176], F32, kind="ExternalInput").ap()
    rows_in = nc.dram_tensor("rows", [1, 2064], F32, kind="ExternalInput").ap()
    y_out = nc.dram_tensor("y", [T, D], F32, kind="ExternalOutput").ap()

    xn_sc = nc.dram_tensor("xn_sc", [T, D], BF16, kind="Internal").ap()
    h_sc = nc.dram_tensor("h_sc", [3 * D, T], BF16, kind="Internal").ap()
    qk_sc = nc.dram_tensor("qk_sc", [D, T], BF16, kind="Internal").ap()
    v_sc = nc.dram_tensor("v_sc", [D, T], BF16, kind="Internal").ap()
    cc_in = nc.dram_tensor("cc_in", [128, 48], F32, kind="Internal").ap()
    cc_out = nc.dram_tensor("cc_out", [128, 48], F32, kind="Internal").ap()

    ident_c = nc.inline_tensor(np.eye(128, dtype=ml_dtypes.bfloat16),
                               name="identc").ap()
    ones_c = nc.inline_tensor(np.ones((1, 128), dtype=ml_dtypes.bfloat16),
                              name="onesc").ap()
    rowhot_np = np.zeros((NEXP, NEXP * 128), dtype=ml_dtypes.bfloat16)
    for e in range(NEXP):
        rowhot_np[e, e * 128:(e + 1) * 128] = 1.0
    rowhot_c = nc.inline_tensor(rowhot_np, name="rowhotc").ap()

    from contextlib import ExitStack
    with ExitStack() as ctx:
        en = ctx.enter_context
        w_ring = en(nc.sbuf_tensor("w_ring", [128, 4 * 8192], BF16))
        acc = en(nc.sbuf_tensor("acc", [128, max(TPQ * 4096, 5632)], F32))
        xn_q = en(nc.sbuf_tensor("xn_q", [128, max(DC * QT, 8192)], BF16))
        h_arena = en(nc.sbuf_tensor("h_arena", [128, 12288], BF16))
        qk_bufs = en(nc.sbuf_tensor("qk_bufs", [128, 8192], BF16))
        d_extra = en(nc.sbuf_tensor("d_extra", [128, 8192], BF16))
        x_sb = en(nc.sbuf_tensor("x_sb", [128, 4096], F32))
        sq_junk = en(nc.sbuf_tensor("sq_junk", [128, 512], BF16))
        vecs_sb = en(nc.sbuf_tensor("vecs_sb", [128, 176], F32))
        stats_acc = en(nc.sbuf_tensor("stats_acc", [128, 48], F32))
        stats_tmp = en(nc.sbuf_tensor("stats_tmp", [128, 8], F32))
        stats_gl = en(nc.sbuf_tensor("stats_gl", [128, 48], F32))
        ab_coef = en(nc.sbuf_tensor("ab_coef", [128, 48], F32))
        ab_tmp = en(nc.sbuf_tensor("ab_tmp", [128, 24], F32))
        ln_t = en(nc.sbuf_tensor("ln_t", [128, 16], F32))
        bias_bf = en(nc.sbuf_tensor("bias_bf", [1, 1024], BF16))
        rbrow_bf = en(nc.sbuf_tensor("rbrow_bf", [1, 16], BF16))
        rb_bc = en(nc.sbuf_tensor("rb_bc", [128, 15], F32))
        eb2_sb = en(nc.sbuf_tensor("eb2_sb", [15, 1024], BF16))
        routW_sb = en(nc.sbuf_tensor("routW_sb", [128, 8 * 15], BF16))
        vv_sb = en(nc.sbuf_tensor("vv_sb", [128, 1024], F32))
        w_tm_all = en(nc.sbuf_tensor("w_tm_all", [128, 512], BF16))
        w_fm_sb = en(nc.sbuf_tensor("w_fm_sb", [16, 1536], BF16))
        bd_vv = en(nc.sbuf_tensor("bd_vv", [128, 1024], BF16))
        w_tm_t = en(nc.sbuf_tensor("w_tm_t", [128, 16], BF16))
        wbc_sb = en(nc.sbuf_tensor("wbc_sb", [128, 512], BF16))
        l_sb = en(nc.sbuf_tensor("l_sb", [128, 64], F32))
        sm_a = en(nc.sbuf_tensor("sm_a", [128, 16], F32))
        sm_b = en(nc.sbuf_tensor("sm_b", [128, 16], F32))
        sm_c = en(nc.sbuf_tensor("sm_c", [128, 16], F32))
        sm_t = en(nc.sbuf_tensor("sm_t", [128, 8], F32))
        eps_t = en(nc.sbuf_tensor("eps_t", [128, 1], F32))
        c125_t = en(nc.sbuf_tensor("c125_t", [128, 1], F32))
        ident_sb = en(nc.sbuf_tensor("ident_sb", [128, 128], BF16))
        ones_sb = en(nc.sbuf_tensor("ones_sb", [1, 128], BF16))
        rowhot_sb = en(nc.sbuf_tensor("rowhot_sb", [15, 15 * 128], BF16))

        ps_out0 = en(nc.psum_tensor("ps_out0", [128, 1024], F32))
        ps_out1 = en(nc.psum_tensor("ps_out1", [128, 1024], F32))
        ps_h0 = en(nc.psum_tensor("ps_h0", [128, 512], F32))
        ps_h1 = en(nc.psum_tensor("ps_h1", [128, 512], F32))
        ps_wbc = en(nc.psum_tensor("ps_wbc", [128, 512], F32))
        ps_tr = en(nc.psum_tensor("ps_tr", [128, 128], BF16))

        xsq_junk = acc[:, 4096:5120]
        sq_junk = acc[:, 5120:5632]
        sems = {}
        for k in ("pe", "act", "dve", "cc"):
            sems[k] = en(nc.semaphore(k))
        for base in ("qsp", "qact", "qgps"):
            for i in range(NQSEM):
                k = "%s%d" % (base, i)
                sems[k] = en(nc.semaphore(k))
        block = en(nc.Block())
        P = Prog(block, sems)

        ps_out = [ps_out0, ps_out1]
        ps_h = [ps_h0, ps_h1]
        b_psout = [Buf(), Buf()]
        b_psh = [Buf(), Buf()]
        b_pswbc = Buf()
        b_pstr = Buf()
        b_xnt = [Buf(), Buf()]
        b_acc = [Buf() for _ in range(TPQ * 4)]
        b_hraw = [Buf() for _ in range(DC)]
        b_hscl = [[Buf() for _ in range(DC)], [Buf() for _ in range(DC)]]
        b_hD = [Buf(), Buf(), Buf()]
        b_qkbuf = [Buf(), Buf()]
        b_dx = [Buf(), Buf()]
        b_xsb = Buf()
        b_xin1 = Buf()
        b_xsq = Buf()
        b_sqj = Buf()
        b_vecs = Buf()
        b_stats = Buf()
        b_stmp = [Buf() for _ in range(4)]
        b_stgl = Buf()
        b_ab = Buf()
        b_abt = Buf()
        b_lnt = Buf()
        b_rows = Buf()
        b_bias = Buf()
        b_rbrow = Buf()
        b_rbbc = Buf()
        b_eb2 = Buf()
        b_routW = Buf()
        b_bdvv = Buf()
        b_vvsb = Buf()
        b_wtmall = Buf()
        b_wfm = [Buf(), Buf(), Buf()]
        b_wtm = Buf()
        b_wbc = Buf()
        b_lsb = Buf()
        b_sma = Buf()
        b_smb = Buf()
        b_smc = Buf()
        b_smt = Buf()
        b_ident = Buf()
        b_eps = Buf()
        b_c125 = Buf()
        b_ones = Buf()
        b_rowhot = Buf()
        b_xnsc = [Buf() for _ in range(NT)]
        b_hsc = [[Buf() for _ in range(NT)] for _ in range(3)]
        b_qksc = [Buf() for _ in range(NT)]
        b_vsc = [Buf() for _ in range(NT)]
        b_ysc = [Buf() for _ in range(NT)]
        b_ccin = Buf()
        b_ccout = Buf()

        P.emit("sync", [lambda e: e.dma_start(out=vecs_sb[:], in_=vecs_in)],
               writes=[b_vecs])
        rows_sb = x_sb[0:1, 0:2064]
        P.emit("sync", [lambda e: e.dma_start(out=rows_sb, in_=rows_in)],
               writes=[b_rows])
        P.emit("sync", [lambda e: e.dma_start(out=ident_sb[:], in_=ident_c)],
               writes=[b_ident])
        P.emit("sync", [lambda e: e.dma_start(out=ones_sb[:], in_=ones_c)],
               writes=[b_ones])
        P.emit("sync", [lambda e: e.dma_start(out=rowhot_sb[:], in_=rowhot_c)],
               writes=[b_rowhot])
        P.emit("gps", [lambda e: e.dma_start(out=eb2_sb[:], in_=eb2_in)],
               writes=[b_eb2])
        P.emit("gps", [lambda e: e.dma_start(
            out=routW_sb[:].rearrange("p (dc e) -> p dc e", dc=DC),
            in_=routW.rearrange("(dc p) e -> p dc e", p=128))],
            writes=[b_routW])

        def load_slot(slot, src2, wbuf):
            P.emit("gps", [lambda e, slot=slot, s=src2: e.dma_start(
                out=w_ring[:, slot * 8192:(slot + 1) * 8192], in_=s)],
                writes=[wbuf])

        def load_pair(base, srccat, wbuf):
            P.emit("gps", [lambda e, base=base, s=srccat: e.dma_start(
                out=w_ring[:, base * 16384:(base + 1) * 16384], in_=s)],
                writes=[wbuf])

        b_wpair = [Buf(), Buf()]
        load_slot(0, wqTp, b_wpair[0])
        load_slot(1, wkTp, b_wpair[0])
        load_slot(2, wvTp, b_wpair[1])

        P.emit("dve", [lambda e: e.tensor_add(
            xsq_junk[0:1, 0:1024], rows_sb[:, 0:1024],
            rows_sb[:, 1024:2048])],
            reads=[b_rows], writes=[b_xsq])
        P.emit("act", [lambda e: e.activation(
            out=bias_bf[0:1, :], in_=xsq_junk[0:1, 0:1024], func=AF.Copy)],
            reads=[b_xsq], writes=[b_bias])
        P.emit("act", [lambda e: e.activation(
            out=rbrow_bf[0:1, 0:15], in_=rows_sb[:, 2048:2063],
            func=AF.Copy)],
            reads=[b_rows], writes=[b_rbrow])
        P.emit("pe", [lambda e: e.matmul(
            ps_wbc[:, 0:15], ones_sb[0:1, :], rbrow_bf[0:1, 0:15],
            start=True, stop=True)],
            reads=[b_ones, b_rbrow], writes=[b_pswbc])
        P.emit("dve", [lambda e: e.tensor_copy(rb_bc[:], ps_wbc[:, 0:15])],
               reads=[b_pswbc], writes=[b_rbbc])
        P.emit("dve", [lambda e: e.memset(stats_acc[:], 0.0)],
               writes=[b_stats])
        P.emit("dve", [lambda e: e.memset(w_tm_t[:, 15:16], 0.0)],
               writes=[b_wtm])
        P.emit("dve", [lambda e: e.memset(vv_sb[:], 0.0)],
               writes=[b_vvsb])
        P.emit("dve", [lambda e: e.memset(eps_t[:], EPS)], writes=[b_eps])
        P.emit("dve", [lambda e: e.memset(c125_t[:], 0.125)], writes=[b_c125])

        seed([b_xsb], [b_xsb, b_rows])
        x_bufs = [x_sb[:, 0:4096], acc[:, 0:4096]]
        bx_bufs = [b_xsb, b_xin1]
        xn_tm = d_extra[:, 0:4096]
        b_xntm = b_dx[0]
        hbufs = [d_extra[:, 4096:8192], qk_bufs[:, 0:4096]]
        b_hbufs = [b_dx[1], b_qkbuf[0]]

        def ab_ln_tile(t):
            bi = t % 2
            xb, bxb = x_bufs[bi], bx_bufs[bi]
            src = x_in[t * TILE:(t + 1) * TILE, :].rearrange(
                "(c p) d -> p c d", p=128)
            P.emit("sync", [lambda e, d=xb, s=src: e.dma_start(
                out=d.rearrange("p (c d) -> p c d", c=4), in_=s)],
                writes=[bxb])
            for c in range(4):
                xc = xb[:, c * 1024:(c + 1) * 1024]
                P.emit("dve", [lambda e, xc=xc, c=c: e.reduce_sum(
                    ln_t[:, c:c + 1], xc, axis=AX.X)],
                    reads=[bxb], writes=[b_lnt])
                P.emit("act", [lambda e, xc=xc, c=c: e.activation(
                    out=xsq_junk[:, 0:1024], in_=xc, func=AF.Square,
                    accum_out=ln_t[:, 4 + c:5 + c])],
                    reads=[bxb], writes=[b_xsq, b_lnt])
            P.emit("dve", [lambda e: e.tensor_scalar_mul(
                ln_t[:, 8:12], ln_t[:, 0:4], 1.0 / 1024.0)],
                reads=[b_lnt], writes=[b_lnt])
            P.emit("dve", [lambda e: e.tensor_scalar_mul(
                ln_t[:, 4:8], ln_t[:, 4:8], 1.0 / 1024.0)],
                reads=[b_lnt], writes=[b_lnt])
            P.emit("dve", [lambda e: e.tensor_mul(
                ln_t[:, 12:16], ln_t[:, 8:12], ln_t[:, 8:12])],
                reads=[b_lnt], writes=[b_lnt])
            P.emit("dve", [lambda e: e.tensor_sub(
                ln_t[:, 4:8], ln_t[:, 4:8], ln_t[:, 12:16])],
                reads=[b_lnt], writes=[b_lnt])
            P.emit("act", [lambda e: e.activation(
                out=ln_t[:, 4:8], in_=ln_t[:, 4:8], func=AF.Sqrt,
                bias=eps_t[:, 0:1])],
                reads=[b_lnt, b_eps], writes=[b_lnt])
            P.emit("dve", [lambda e: e.reciprocal(
                ln_t[:, 12:16], ln_t[:, 4:8])],
                reads=[b_lnt], writes=[b_lnt])
            P.emit("dve", [lambda e: e.scalar_tensor_tensor(
                ln_t[:, 8:12], ln_t[:, 8:12], -1.0, ln_t[:, 12:16],
                ALU.mult, ALU.mult)],
                reads=[b_lnt], writes=[b_lnt])
            for c in range(4):
                xc = xb[:, c * 1024:(c + 1) * 1024]
                P.emit("dve", [lambda e, xc=xc, c=c: e.tensor_scalar(
                    xn_tm[:, c * 1024:(c + 1) * 1024], xc,
                    ln_t[:, 12 + c:13 + c], ln_t[:, 8 + c:9 + c],
                    ALU.mult, ALU.add)],
                    reads=[bxb, b_lnt], writes=[b_xntm])
            dst = xn_sc[t * TILE:(t + 1) * TILE, :].rearrange(
                "(c p) d -> p c d", p=128)
            P.emit("sync", [lambda e, d=dst: e.dma_start(
                out=d, in_=xn_tm.rearrange("p (c d) -> p c d", c=4))],
                reads=[b_xntm], writes=[b_xnsc[t]])

        def xn_quarter_load(q):
            for ti in range(TPQ):
                t = q * TPQ + ti
                for dc in range(DC):
                    P.emit("actdma", [lambda e, t=t, ti=ti, dc=dc:
                                      e.dma_start_transpose(
                        xn_q[:, dc * QT + ti * TILE:
                             dc * QT + ti * TILE + TILE],
                        xn_sc[t * TILE:(t + 1) * TILE,
                              dc * 128:(dc + 1) * 128])],
                        reads=[b_xnsc[t]], writes=[b_xnt[ti]])

        psh_cnt = [0]

        def ab_proj_quarter(q):
            for ti in range(TPQ):
                t = q * TPQ + ti
                toff = ti * TILE
                for br in range(3):
                    hb = (t * 3 + br) % 2
                    hbuf, b_hbuf = hbufs[hb], b_hbufs[hb]
                    wslot = w_ring[:, br * 8192:(br + 1) * 8192]
                    for ec in range(DC):
                        par = psh_cnt[0] % 2
                        psh_cnt[0] += 1
                        fns = []
                        for dc in range(DC):
                            fns.append(
                                lambda e, w=wslot, dc=dc, ec=ec, toff=toff,
                                par=par:
                                e.matmul(
                                    ps_h[par][:, 0:512],
                                    w[:, dc * 1024 + ec * 128:
                                      dc * 1024 + ec * 128 + 128],
                                    xn_q[:, dc * QT + toff:
                                         dc * QT + toff + 512],
                                    start=(dc == 0), stop=(dc == DC - 1)))
                        P.emit("pe", fns,
                               reads=[b_wpair[0] if br < 2 else b_wpair[1],
                                      b_xnt[ti]],
                               writes=[b_psh[par]])
                        st_r = (br * DC + ec) % 4
                        P.emit("act", [lambda e, ec=ec, par=par, st_r=st_r,
                                       hbuf=hbuf: e.activation(
                            out=hbuf[:, ec * 512:(ec + 1) * 512],
                            in_=ps_h[par][:, 0:512], func=AF.Copy,
                            accum_out=stats_tmp[:, st_r * 2:st_r * 2 + 1])],
                            reads=[b_psh[par]],
                            writes=[b_hbuf, b_stmp[st_r]])
                        P.emit("dve", [lambda e, ec=ec, st_r=st_r,
                                       hbuf=hbuf: e.scalar_tensor_tensor(
                            sq_junk[:, 0:512],
                            hbuf[:, ec * 512:(ec + 1) * 512], 1.0,
                            hbuf[:, ec * 512:(ec + 1) * 512],
                            ALU.mult, ALU.mult,
                            accum_out=stats_tmp[:, st_r * 2 + 1:
                                                st_r * 2 + 2])],
                            reads=[b_hbuf], writes=[b_sqj, b_stmp[st_r]])
                        cs = br * 16 + ec
                        P.emit("dve", [lambda e, cs=cs, st_r=st_r:
                                       e.tensor_add(
                            stats_acc[:, cs:cs + 16].rearrange(
                                "p (two ec) -> p (ec two)", two=2
                            )[:, 0:2] if False else
                            stats_acc[:, cs:cs + 9].rearrange(
                                "p (a b) -> p a b", a=1)[:, 0, 0:9:8],
                            stats_acc[:, cs:cs + 9].rearrange(
                                "p (a b) -> p a b", a=1)[:, 0, 0:9:8],
                            stats_tmp[:, st_r * 2:st_r * 2 + 2])],
                            reads=[b_stmp[st_r], b_stats], writes=[b_stats])
                    dsth = h_sc[br * D:(br + 1) * D, :].rearrange(
                        "(dc p) n -> p dc n", p=128)
                    for half in range(2):
                        P.emit("sync", [lambda e, d=dsth, half=half, t=t,
                                        hbuf=hbuf: e.dma_start(
                            out=d[:, half * 4:half * 4 + 4,
                                  t * TILE:(t + 1) * TILE],
                            in_=hbuf[:, half * 2048:half * 2048 + 2048]
                            .rearrange("p (dc n) -> p dc n", dc=4))],
                            reads=[b_hbuf], writes=[b_hsc[br][t]])

        def f_router(tq):
            ti = tq % TPQ
            toff = ti * TILE
            for c in range(4):
                fns = []
                for dc in range(DC):
                    fns.append(lambda e, dc=dc, c=c, toff=toff:
                               e.matmul(
                        ps_wbc[:, c * 128:c * 128 + 15],
                        xn_q[:, dc * QT + toff + c * 128:
                             dc * QT + toff + c * 128 + 128],
                        routW_sb[:, dc * 15:(dc + 1) * 15],
                        start=(dc == 0), stop=(dc == DC - 1)))
                P.emit("pe", fns, reads=[b_xnt[ti], b_routW],
                       writes=[b_pswbc])
                lc = l_sb[:, c * 16:c * 16 + 15]
                P.emit("dve", [lambda e, lc=lc, c=c: e.tensor_add(
                    lc, ps_wbc[:, c * 128:c * 128 + 15], rb_bc[:])],
                    reads=[b_pswbc, b_rbbc], writes=[b_lsb])
            for c in range(4):
                lc = l_sb[:, c * 16:c * 16 + 15]
                P.emit("dve", [lambda e, lc=lc: e.reduce_max(
                    sm_t[:, 0:1], lc, axis=AX.X)],
                    reads=[b_lsb], writes=[b_smt])
                P.emit("dve", [lambda e, lc=lc: e.tensor_scalar(
                    sm_a[:, 0:15], lc, sm_t[:, 0:1], None, ALU.is_ge)],
                    reads=[b_lsb, b_smt], writes=[b_sma])
                P.emit("dve", [lambda e, lc=lc: e.scalar_tensor_tensor(
                    sm_b[:, 0:15], sm_a[:, 0:15], -30000.0, lc,
                    ALU.mult, ALU.add)],
                    reads=[b_sma, b_lsb], writes=[b_smb])
                P.emit("dve", [lambda e: e.reduce_max(
                    sm_t[:, 1:2], sm_b[:, 0:15], axis=AX.X)],
                    reads=[b_smb], writes=[b_smt])
                P.emit("dve", [lambda e: e.tensor_scalar(
                    sm_a[:, 0:15], sm_b[:, 0:15], sm_t[:, 1:2], None,
                    ALU.is_ge)],
                    reads=[b_smb, b_smt], writes=[b_sma])
                P.emit("dve", [lambda e: e.scalar_tensor_tensor(
                    sm_b[:, 0:15], sm_a[:, 0:15], -30000.0, sm_b[:, 0:15],
                    ALU.mult, ALU.add)],
                    reads=[b_sma, b_smb], writes=[b_smb])
                P.emit("dve", [lambda e: e.reduce_max(
                    sm_t[:, 2:3], sm_b[:, 0:15], axis=AX.X)],
                    reads=[b_smb], writes=[b_smt])
                P.emit("dve", [lambda e: e.tensor_scalar_mul(
                    sm_t[:, 3:4], sm_t[:, 0:1], -1.0)],
                    reads=[b_smt], writes=[b_smt])
                P.emit("act", [lambda e, lc=lc: e.activation(
                    out=sm_c[:, 0:15], in_=lc, func=AF.Exp,
                    bias=sm_t[:, 3:4], scale=1.0)],
                    reads=[b_lsb, b_smt], writes=[b_smc])
                P.emit("dve", [lambda e: e.reduce_sum(
                    sm_t[:, 4:5], sm_c[:, 0:15], axis=AX.X)],
                    reads=[b_smc], writes=[b_smt])
                P.emit("dve", [lambda e: e.reciprocal(
                    sm_t[:, 5:6], sm_t[:, 4:5])],
                    reads=[b_smt], writes=[b_smt])
                P.emit("dve", [lambda e, lc=lc: e.tensor_scalar(
                    sm_a[:, 0:15], lc, sm_t[:, 2:3], None, ALU.is_ge)],
                    reads=[b_lsb, b_smt], writes=[b_sma])
                P.emit("dve", [lambda e: e.tensor_mul(
                    sm_c[:, 0:15], sm_c[:, 0:15], sm_a[:, 0:15])],
                    reads=[b_sma, b_smc], writes=[b_smc])
                P.emit("dve", [lambda e: e.tensor_scalar_mul(
                    w_tm_t[:, 0:15], sm_c[:, 0:15], sm_t[:, 5:6])],
                    reads=[b_smc, b_smt], writes=[b_wtm])
                P.emit("dve", [lambda e, tq=tq, c=c: e.tensor_copy(
                    w_tm_all[:, (tq * 4 + c) * 16:(tq * 4 + c) * 16 + 16],
                    w_tm_t[:, 0:16])],
                    reads=[b_wtm], writes=[b_wtmall])

        ab_order = list(range(1, NQ)) + [0]
        for i in range(NQ + 1):
            if i < NQ:
                q = ab_order[i]
                for ti in range(TPQ):
                    ab_ln_tile(q * TPQ + ti)
            if i >= 1:
                qp = ab_order[i - 1]
                ab_proj_quarter(qp)
                for ti in range(TPQ):
                    f_router(qp * TPQ + ti)
            if i < NQ:
                xn_quarter_load(ab_order[i])

        def emit_phase_c():
            P.emit("sync", [lambda e: e.dma_start(out=cc_in, in_=stats_acc[:])],
                   reads=[b_stats], writes=[b_ccin])
            P.emit("cc", [lambda e: e.collective_compute(
                "AllReduce", ALU.add,
                replica_groups=[list(range(NCORES))],
                ins=[cc_in.opt()], outs=[cc_out.opt()])],
                reads=[b_ccin], writes=[b_ccout])
            P.emit("gps", [lambda e: e.dma_start(out=stats_gl[:], in_=cc_out)],
                   reads=[b_ccout], writes=[b_stgl])
            sgl = stats_gl[:].rearrange("p (br k c) -> p br k c", br=3, k=2)
            vec3 = vecs_sb[:, 0:48].rearrange("p (br k c) -> p br k c", br=3, k=2)
            abc = ab_coef[:].rearrange("p (br k c) -> p br k c", br=3, k=2)
            abt = ab_tmp[:].rearrange("p (g c) -> p g c", g=3)
            sums3 = sgl[:, :, 0, :]
            sqs3 = sgl[:, :, 1, :]
            gam3 = vec3[:, :, 0, :]
            bet3 = vec3[:, :, 1, :]
            A3 = abc[:, :, 0, :]
            B3 = abc[:, :, 1, :]
            mean3 = abt[:, 0, :].rearrange("p (br c) -> p br c", br=1)[:, 0, :]
            mean3 = ab_tmp[:, 0:8]
            mean_t = ab_tmp[:, 0:24].rearrange("p (br c) -> p br c", br=3)
            ee_t = stats_tmp[:, 0:8]
            P.emit("dve", [lambda e: e.tensor_scalar_mul(
                mean_t, sums3, 1.0 / NTOK_GLOBAL)],
                reads=[b_stgl], writes=[b_abt])
            P.emit("dve", [lambda e: e.tensor_scalar_mul(
                B3, sqs3, 1.0 / NTOK_GLOBAL)],
                reads=[b_stgl], writes=[b_ab])
            P.emit("dve", [lambda e: e.scalar_tensor_tensor(
                A3, mean_t, 1.0, mean_t, ALU.mult, ALU.mult)],
                reads=[b_abt], writes=[b_ab])
            P.emit("dve", [lambda e: e.tensor_sub(B3, B3, A3)],
                   reads=[b_ab], writes=[b_ab])
            P.emit("act", [lambda e: e.activation(
                out=ab_coef[:].rearrange("p (br k c) -> p br k c",
                                         br=3, k=2)[:, :, 1, :],
                in_=ab_coef[:].rearrange("p (br k c) -> p br k c",
                                         br=3, k=2)[:, :, 1, :],
                func=AF.Sqrt, bias=eps_t[:, 0:1])],
                reads=[b_ab, b_eps], writes=[b_ab])
            P.emit("dve", [lambda e: e.reciprocal(A3, B3)],
                   reads=[b_ab], writes=[b_ab])
            P.emit("dve", [lambda e: e.tensor_mul(A3, A3, gam3)],
                   reads=[b_ab, b_vecs], writes=[b_ab])
            P.emit("dve", [lambda e: e.tensor_mul(mean_t, mean_t, A3)],
                   reads=[b_ab, b_abt], writes=[b_abt])
            P.emit("dve", [lambda e: e.tensor_scalar(
                B3, bet3, 4.0, -4.0, ALU.mult, ALU.add)],
                reads=[b_vecs, b_ab], writes=[b_ab])
            P.emit("dve", [lambda e: e.scalar_tensor_tensor(
                B3, mean_t, -4.0, B3, ALU.mult, ALU.add)],
                reads=[b_abt, b_ab], writes=[b_ab])
            P.emit("dve", [lambda e: e.tensor_scalar_mul(A3, A3, 4.0)],
                   reads=[b_ab], writes=[b_ab])


        d_stage = qk_bufs[:, 0:4096]
        d_qv = qk_bufs[:, 4096:8192]
        d_k = d_extra[:, 4096:8192]
        sig_dst = [d_qv, d_k, d_qv]
        sig_dbuf = [b_qkbuf[1], b_dx[1], b_qkbuf[1]]

        def d_load(t, br):
            srch = h_sc[br * D:(br + 1) * D, :].rearrange(
                "(dc p) n -> p dc n", p=128)
            for half in range(2):
                P.emit("sync", [lambda e, s=srch, half=half, t=t, br=br:
                                e.dma_start(
                    out=d_stage[:, half * 2048:half * 2048 + 2048]
                    .rearrange("p (dc n) -> p dc n", dc=4),
                    in_=s[:, half * 4:half * 4 + 4,
                          t * TILE:(t + 1) * TILE])],
                    reads=[b_hsc[br][t]], writes=[b_qkbuf[0]])

        def d_sigs(t, br):
            for dc in range(DC):
                P.emit("act", [lambda e, br=br, dc=dc: e.activation(
                    out=sig_dst[br][:, dc * 512:(dc + 1) * 512],
                    in_=d_stage[:, dc * 512:(dc + 1) * 512],
                    func=AF.Sigmoid,
                    scale=ab_coef[:, br * 16 + dc:br * 16 + dc + 1],
                    bias=ab_coef[:, br * 16 + 8 + dc:
                                 br * 16 + 8 + dc + 1])],
                    reads=[b_qkbuf[0], b_ab], writes=[sig_dbuf[br]])

        def d_qkmul(t):
            P.emit("dve", [lambda e: e.tensor_mul(
                d_stage[:], d_qv[:], d_k[:])],
                reads=[b_qkbuf[1], b_dx[1]], writes=[b_qkbuf[0]])

        def d_stq(t, which):
            srcb, dstd, btok, bsrc = (
                (d_stage, qk_sc, b_qksc[t], b_qkbuf[0]) if which == 0 else
                (d_qv, v_sc, b_vsc[t], b_qkbuf[1]))
            dd = dstd.rearrange("(dc p) n -> p dc n", p=128)
            for half in range(2):
                P.emit("sync", [lambda e, s=srcb, d=dd, half=half, t=t:
                                e.dma_start(
                    out=d[:, half * 4:half * 4 + 4,
                          t * TILE:(t + 1) * TILE],
                    in_=s[:, half * 2048:half * 2048 + 2048].rearrange(
                        "p (dc n) -> p dc n", dc=4))],
                    reads=[bsrc], writes=[btok])

        def d_vtm(t):
            for c in range(4):
                reg = d_extra[:, c * 1024:c * 1024 + 1024]
                P.emit("actdma", [lambda e, reg=reg, t=t, c=c:
                                  e.dma_start_transpose(
                    reg, v_sc[:, t * TILE + c * 128:
                              t * TILE + (c + 1) * 128])],
                    reads=[b_vsc[t]], writes=[b_dx[0]])

        def d_vv(t):
            paro = pso_cnt[0] % 2
            pso_cnt[0] += 1
            fns = []
            for c in range(4):
                reg = d_extra[:, c * 1024:c * 1024 + 1024]
                for hp in range(8):
                    fns.append(lambda e, reg=reg, hp=hp, c=c, paro=paro:
                               e.matmul(
                        ps_out[paro][:, hp * 128:(hp + 1) * 128],
                        reg[:, hp * 128:(hp + 1) * 128],
                        reg[:, hp * 128:(hp + 1) * 128],
                        start=(c == 0 and hp in (0, 4)),
                        stop=(c == 3 and hp in (3, 7)),
                        skip_group_check=True))
            P.emit("pe", fns, reads=[b_dx[0]], writes=[b_psout[paro]])
            P.emit("dve", [lambda e, paro=paro: e.tensor_add(
                vv_sb[:], vv_sb[:], ps_out[paro][:])],
                reads=[b_psout[paro], b_vvsb], writes=[b_vvsb])

        def d_bdvv():
            P.emit("dve", [lambda e: e.memset(bd_vv[:], 0.0)],
                   reads=[b_dx[1]], writes=[b_bdvv])
            for dc in range(DC):
                for half in range(2):
                    P.emit("dve", [lambda e, dc=dc, half=half:
                                   e.tensor_scalar_mul(
                        bd_vv[half * 64:half * 64 + 64,
                              dc * 128 + half * 64:
                              dc * 128 + half * 64 + 64],
                        vv_sb[half * 64:half * 64 + 64,
                              dc * 128 + half * 64:
                              dc * 128 + half * 64 + 64], 0.125)],
                        reads=[b_vvsb], writes=[b_bdvv])

        def d_sched(gp):
            t, step = gp // 5, gp % 5
            if t >= NT:
                if gp == NT * 5:
                    d_bdvv()
                return
            if step == 0:
                d_load(t, 0)
                d_sigs(t, 0)
                d_load(t, 1)
            elif step == 1:
                d_sigs(t, 1)
                d_qkmul(t)
                d_stq(t, 0)
            elif step == 2:
                d_load(t, 2)
                d_sigs(t, 2)
                d_stq(t, 1)
            elif step == 3:
                d_vtm(t)
            else:
                d_vv(t)

        seed(b_hraw, [b_hD[0]])
        seed(b_hscl[0], [b_hD[1]])
        seed(b_hscl[1], [b_hD[2]])
        seed(b_acc[:4], [b_xin1])

        h_raw = h_arena[:, 0:4096]
        h_scl = [h_arena[:, 4096:8192], h_arena[:, 8192:12288]]
        attn_fm = d_extra[:, 0:4096]
        b_attnfm = b_dx[0]
        pso_cnt = [0]
        hsl_cnt = [0]

        pair_defs = []
        for q in range(NQ):
            pair_defs.append(("pair", sWcat))
            for e_i in range(NEXP):
                pair_defs.append(
                    ("pair", eWcat[e_i:e_i + 1, :, :].rearrange(
                        "one p m -> p (one m)")))
        pair_slots = []

        def emit_pair_load(g):
            base = g % 2
            kind, ap = pair_defs[g]
            if kind == "pair":
                load_pair(base, ap, b_wpair[base])
            else:
                load_slot(base * 2, ap, b_wpair[base])
            pair_slots.append((base * 2, base * 2 + 1, b_wpair[base]))

        emit_pair_load(0)
        emit_pair_load(1)

        def f_first_and_h(p, tq, slot_a, wbuf, shared, hsl):
            ti = tq % TPQ
            toff = ti * TILE
            wslot = w_ring[:, slot_a * 8192:(slot_a + 1) * 8192]
            if not shared:
                P.emit("pe", [lambda e, p=p, ti=ti: e.matmul(
                    ps_wbc[:, 0:512],
                    rowhot_sb[:, (p - 1) * 128:p * 128],
                    w_fm_sb[0:15, ti * 512:(ti + 1) * 512],
                    start=True, stop=True)],
                    reads=[b_rowhot, b_wfm[ti]], writes=[b_pswbc])
                P.emit("dve", [lambda e: e.tensor_copy(
                    wbc_sb[:], ps_wbc[:, 0:512])],
                    reads=[b_pswbc], writes=[b_wbc])
            dstbufs = []
            for ec in range(DC):
                par = psh_cnt[0] % 2
                psh_cnt[0] += 1
                fns = []
                for dc in range(DC):
                    fns.append(lambda e, w=wslot, dc=dc, ec=ec, toff=toff,
                               par=par:
                               e.matmul(
                        ps_h[par][:, 0:512],
                        w[:, dc * 1024 + ec * 128:dc * 1024 + ec * 128 + 128],
                        xn_q[:, dc * QT + toff:dc * QT + toff + 512],
                        start=(dc == 0), stop=(dc == DC - 1)))
                P.emit("pe", fns, reads=[wbuf, b_xnt[ti]],
                       writes=[b_psh[par]])
                bcol = 48 if shared else (56 + (p - 1) * 8)
                P.emit("act", [lambda e, ec=ec, par=par, bcol=bcol:
                               e.activation(
                    out=h_raw[:, ec * 512:(ec + 1) * 512],
                    in_=ps_h[par][:, 0:512], func=AF.Relu,
                    bias=vecs_sb[:, bcol + ec:bcol + ec + 1])],
                    reads=[b_psh[par], b_vecs], writes=[b_hraw[ec]])
                if shared:
                    dstbufs.append(b_hraw[ec])
                else:
                    P.emit("dve", [lambda e, ec=ec, hsl=hsl: e.tensor_mul(
                        h_scl[hsl][:, ec * 512:(ec + 1) * 512],
                        h_raw[:, ec * 512:(ec + 1) * 512], wbc_sb[:])],
                        reads=[b_hraw[ec], b_wbc], writes=[b_hscl[hsl][ec]])
                    dstbufs.append(b_hscl[hsl][ec])
            return dstbufs

        def f_second(lhs_t, lhs_bufs, slot_b, wbuf, tq, first_pass):
            ti = tq % TPQ
            wslot = w_ring[:, slot_b * 8192:(slot_b + 1) * 8192]
            for c in range(4):
                paro = pso_cnt[0] % 2
                pso_cnt[0] += 1
                fns = []
                for hc in range(DC):
                    for half in range(2):
                        fns.append(lambda e, w=wslot, hc=hc, half=half, c=c,
                                   paro=paro:
                                   e.matmul(
                            ps_out[paro][:, half * 512:half * 512 + 512],
                            lhs_t[:, hc * 512 + c * 128:
                                  hc * 512 + c * 128 + 128],
                            w[:, hc * 1024 + half * 512:
                              hc * 1024 + half * 512 + 512],
                            start=(hc == 0), stop=(hc == DC - 1)))
                P.emit("pe", fns, reads=lhs_bufs + [wbuf],
                       writes=[b_psout[paro]])
                aslab = acc[:, (ti * 4 + c) * 1024:(ti * 4 + c + 1) * 1024]
                ab = b_acc[ti * 4 + c]
                if first_pass:
                    P.emit("dve", [lambda e, a=aslab, paro=paro:
                                   e.tensor_copy(a, ps_out[paro][:])],
                           reads=[b_psout[paro]], writes=[ab])
                else:
                    P.emit("dve", [lambda e, a=aslab, paro=paro:
                                   e.tensor_add(a, a, ps_out[paro][:])],
                           reads=[b_psout[paro], ab], writes=[ab])

        wslot_wp = qk_bufs[:, 0:8192]
        b_wp = Buf()

        def load_wpt():
            P.emit("gps", [lambda e: e.dma_start(out=wslot_wp, in_=wpTp)],
                   reads=[b_qkbuf[0], b_qkbuf[1]], writes=[b_wp])

        srcqk_g = qk_sc.rearrange("(dc p) n -> p dc n", p=128)

        def attn_slot_a(t):
            qkb = d_extra[:, 0:4096]
            for half in range(2):
                P.emit("sync", [lambda e, half=half, t=t, qkb=qkb:
                                e.dma_start(
                    out=qkb[:, half * 2048:half * 2048 + 2048]
                    .rearrange("p (dc n) -> p dc n", dc=4),
                    in_=srcqk_g[:, half * 4:half * 4 + 4,
                                t * TILE:(t + 1) * TILE])],
                    reads=[b_qksc[t]], writes=[b_dx[0]])
            for c in range(4):
                P.emit("pe", [lambda e, t=t, c=c: e.transpose(
                    ps_tr[0:16, :],
                    w_tm_all[:, (t * 4 + c) * 16:(t * 4 + c) * 16 + 16],
                    ident_sb[:])],
                    reads=[b_wtmall, b_ident], writes=[b_pstr])
                P.emit("dve", [lambda e, c=c: e.tensor_copy(
                    w_fm_sb[0:16, 1024 + c * 128:1024 + (c + 1) * 128],
                    ps_tr[0:16, :])],
                    reads=[b_pstr], writes=[b_wfm[2]])
            for dc in range(DC):
                par = psh_cnt[0] % 2
                psh_cnt[0] += 1
                P.emit("pe", [lambda e, dc=dc, par=par, qkb=qkb: e.matmul(
                    ps_h[par][:, 0:512], bd_vv[:, dc * 128:(dc + 1) * 128],
                    qkb[:, dc * 512:(dc + 1) * 512], start=True, stop=True)],
                    reads=[b_bdvv, b_dx[0]], writes=[b_psh[par]])
                P.emit("act", [lambda e, dc=dc, par=par: e.activation(
                    out=attn_fm[:, dc * 512:(dc + 1) * 512],
                    in_=ps_h[par][:, 0:512], func=AF.Copy)],
                    reads=[b_psh[par]], writes=[b_attnfm])

        def attn_slot_b(t):
            srcy = y_out[t * TILE:(t + 1) * TILE, :].rearrange(
                "(c p) d -> p c d", p=128)
            P.emit("sync", [lambda e, s=srcy: e.dma_start(
                out=x_sb[:].rearrange("p (c d) -> p c d", c=4), in_=s)],
                reads=[b_ysc[t]], writes=[b_xsb])
            for c in range(4):
                paro = pso_cnt[0] % 2
                pso_cnt[0] += 1
                fns = []
                for hc in range(DC):
                    for half in range(2):
                        fns.append(lambda e, hc=hc, half=half, c=c,
                                   paro=paro:
                                   e.matmul(
                            ps_out[paro][:, half * 512:half * 512 + 512],
                            attn_fm[:, hc * 512 + c * 128:
                                    hc * 512 + c * 128 + 128],
                            wslot_wp[:, hc * 1024 + half * 512:
                                     hc * 1024 + half * 512 + 512],
                            start=(hc == 0), stop=False))
                for half in range(2):
                    fns.append(lambda e, half=half, c=c, paro=paro:
                               e.matmul(
                        ps_out[paro][:, half * 512:half * 512 + 512],
                        w_fm_sb[0:15, 1024 + c * 128:1024 + (c + 1) * 128],
                        eb2_sb[0:15, half * 512:half * 512 + 512],
                        start=False, stop=False))
                for half in range(2):
                    fns.append(lambda e, half=half, paro=paro:
                               e.matmul(
                        ps_out[paro][:, half * 512:half * 512 + 512],
                        ones_sb[0:1, 0:128],
                        bias_bf[0:1, half * 512:half * 512 + 512],
                        start=False, stop=True))
                P.emit("pe", fns,
                       reads=[b_attnfm, b_wp, b_wfm[2], b_eb2,
                              b_ones, b_bias],
                       writes=[b_psout[paro]])
                P.emit("dve", [lambda e, c=c, paro=paro: e.tensor_add(
                    x_sb[:, c * 1024:(c + 1) * 1024],
                    x_sb[:, c * 1024:(c + 1) * 1024], ps_out[paro][:])],
                    reads=[b_psout[paro], b_xsb], writes=[b_xsb])
            dsty = y_out[t * TILE:(t + 1) * TILE, :].rearrange(
                "(c p) d -> p c d", p=128)
            P.emit("sync", [lambda e, d=dsty: e.dma_start(
                out=d, in_=x_sb[:].rearrange("p (c d) -> p c d", c=4))],
                reads=[b_xsb], writes=[b_xsb, b_ysc[t]])

        can_inter = (NQ > 1) and (NT * 5 + 1 <= (NQ - 1) * 16)
        N_INTER = (NQ - 1) * TPQ if can_inter else 0

        def xn_next_load(q):
            for ti in range(TPQ):
                t = q * TPQ + ti
                for dc in range(DC):
                    P.emit("actdma", [lambda e, t=t, ti=ti, dc=dc:
                                      e.dma_start_transpose(
                        xn_q[:, dc * QT + ti * TILE:
                             dc * QT + ti * TILE + TILE],
                        xn_sc[t * TILE:(t + 1) * TILE,
                              dc * 128:(dc + 1) * 128])],
                        reads=[b_xnsc[t]], writes=[b_xnt[ti]])
            for ti in range(TPQ):
                tq0 = q * TPQ + ti
                for c in range(4):
                    P.emit("pe", [lambda e, tq0=tq0, c=c: e.transpose(
                        ps_tr[0:16, :],
                        w_tm_all[:, (tq0 * 4 + c) * 16:
                                 (tq0 * 4 + c) * 16 + 16],
                        ident_sb[:])],
                        reads=[b_wtmall, b_ident], writes=[b_pstr])
                    P.emit("dve", [lambda e, c=c, ti=ti: e.tensor_copy(
                        w_fm_sb[0:16, ti * 512 + c * 128:
                                ti * 512 + (c + 1) * 128],
                        ps_tr[0:16, :])],
                        reads=[b_pstr], writes=[b_wfm[ti]])

        def wfm_rebuild(q):
            for ti in range(TPQ):
                tq0 = q * TPQ + ti
                for c in range(4):
                    P.emit("pe", [lambda e, tq0=tq0, c=c: e.transpose(
                        ps_tr[0:16, :],
                        w_tm_all[:, (tq0 * 4 + c) * 16:
                                 (tq0 * 4 + c) * 16 + 16],
                        ident_sb[:])],
                        reads=[b_wtmall, b_ident], writes=[b_pstr])
                    P.emit("dve", [lambda e, c=c, ti=ti: e.tensor_copy(
                        w_fm_sb[0:16, ti * 512 + c * 128:
                                ti * 512 + (c + 1) * 128],
                        ps_tr[0:16, :])],
                        reads=[b_pstr], writes=[b_wfm[ti]])

        for q in range(NQ):
            if q == 0:
                wfm_rebuild(0)
                emit_phase_c()
            if q == NQ - 1 and can_inter:
                load_wpt()
            for p in range(16):
                g = q * 16 + p
                for ti in range(TPQ):
                    tq = q * TPQ + ti
                    sa, sb, wb = pair_slots[g]
                    shared = (p == 0)
                    hsl = hsl_cnt[0] % 2
                    hsl_cnt[0] += 1
                    lhs_bufs = f_first_and_h(p, tq, sa, wb, shared, hsl)
                    lhs_t = h_raw if shared else h_scl[hsl]
                    f_second(lhs_t, lhs_bufs, sb, wb, tq,
                             first_pass=shared)
                d_sched(g)
                if q == NQ - 1 and can_inter:
                    if p % 2 == 0 and p // 2 < N_INTER:
                        attn_slot_a(p // 2)
                    elif p % 2 == 1 and p // 2 < N_INTER:
                        attn_slot_b(p // 2)
                if g + 2 < len(pair_defs):
                    emit_pair_load(g + 2)
                if p == 15 and q + 1 < NQ:
                    xn_next_load(q + 1)
            for ti in range(TPQ):
                tq = q * TPQ + ti
                src = x_in[tq * TILE:(tq + 1) * TILE, :].rearrange(
                    "(c p) d -> p c d", p=128)
                P.emit("sync", [lambda e, s=src: e.dma_start(
                    out=x_sb[:].rearrange("p (c d) -> p c d", c=4), in_=s)],
                    writes=[b_xsb])
                P.emit("dve", [lambda e, ti=ti: e.tensor_add(
                    x_sb[:], acc[:, ti * 4096:(ti + 1) * 4096], x_sb[:])],
                    reads=[b_acc[ti * 4], b_acc[ti * 4 + 1],
                           b_acc[ti * 4 + 2], b_acc[ti * 4 + 3], b_xsb],
                    writes=[b_xsb])
                dsty = y_out[tq * TILE:(tq + 1) * TILE, :].rearrange(
                    "(c p) d -> p c d", p=128)
                P.emit("sync", [lambda e, d=dsty: e.dma_start(
                    out=d, in_=x_sb[:].rearrange("p (c d) -> p c d", c=4))],
                    reads=[b_xsb], writes=[b_xsb, b_ysc[tq]])

        if not can_inter:
            load_wpt()
        for t in range(N_INTER, NT):
            attn_slot_a(t)
            attn_slot_b(t)

        P.final_wait("sync", [("qsp%d" % i, P.cnt["qsp%d" % i])
                              for i in range(NQSEM)
                              if P.cnt["qsp%d" % i] > 0])
        P.flush()
    return nc


def kernel(L0_attn_x, L0_attn_Wq, L0_attn_gq, L0_attn_bq, L0_attn_Wk,
           L0_attn_gk, L0_attn_bk, L0_attn_Wv, L0_attn_gv, L0_attn_bv,
           L0_attn_Wp, L0_attn_bp, L0_ffn_router_W, L0_ffn_router_b,
           L0_ffn_sW1, L0_ffn_sb1, L0_ffn_sW2, L0_ffn_sb2, L0_ffn_eW1,
           L0_ffn_eb1, L0_ffn_eW2, L0_ffn_eb2):
    f = lambda a: np.ascontiguousarray(np.asarray(a, dtype=np.float32))
    x = f(L0_attn_x)
    B, T, _ = x.shape

    def pc(v):
        return np.ascontiguousarray(f(v).reshape(8, 128).T)

    vecs = np.concatenate(
        [pc(L0_attn_gq), pc(L0_attn_bq), pc(L0_attn_gk), pc(L0_attn_bk),
         pc(L0_attn_gv), pc(L0_attn_bv), pc(L0_ffn_sb1),
         np.ascontiguousarray(
             np.transpose(f(L0_ffn_eb1).reshape(15, 8, 128), (2, 0, 1))
             .reshape(128, 120))],
        axis=1)
    rows = np.zeros((1, 2064), dtype=np.float32)
    rows[0, 0:1024] = f(L0_attn_bp)
    rows[0, 1024:2048] = f(L0_ffn_sb2)
    rows[0, 2048:2063] = f(L0_ffn_router_b)

    def pm(w):
        w = f(w)
        return np.ascontiguousarray(
            w.reshape(8, 128, w.shape[1]).transpose(1, 0, 2).reshape(128, -1))

    eW1f, eW2f = f(L0_ffn_eW1), f(L0_ffn_eW2)
    eWcat = np.ascontiguousarray(np.stack(
        [np.concatenate([pm(eW1f[e]), pm(eW2f[e])], axis=1)
         for e in range(15)], axis=0))
    common = {
        "wqTp": pm(f(L0_attn_Wq).T),
        "wkTp": pm(f(L0_attn_Wk).T),
        "wvTp": pm(f(L0_attn_Wv).T),
        "wpTp": pm(f(L0_attn_Wp).T),
        "sWcat": np.concatenate([pm(L0_ffn_sW1), pm(L0_ffn_sW2)], axis=1),
        "eWcat": eWcat,
        "routW": f(L0_ffn_router_W),
        "eb2": f(L0_ffn_eb2),
        "vecs": vecs,
        "rows": rows,
    }
    in_maps = []
    for b in range(B):
        m = dict(common)
        m["x"] = np.ascontiguousarray(x[b])
        in_maps.append(m)

    nc = build_nc(T)
    res = run_bass_kernel_spmd(nc, in_maps, core_ids=list(range(B)))
    y = np.stack([res.results[i]["y"] for i in range(B)], axis=0)
    return y


# version 23
# speedup vs baseline: 1.0476x; 1.0215x over previous
import numpy as np
import ml_dtypes

import concourse.bass as bass
import concourse.mybir as mybir
from concourse.bass_utils import run_bass_kernel_spmd

F32 = mybir.dt.float32
BF16 = mybir.dt.bfloat16
AF = mybir.ActivationFunctionType
ALU = mybir.AluOpType
AX = mybir.AxisListType

D = 1024
DC = 8
NEXP = 15
NCORES = 8
TILE = 512
EPS = 1e-5


class Buf:
    __slots__ = ("ready", "readers")

    def __init__(self):
        self.ready = None
        self.readers = []


def seed(dst_bufs, src_bufs):
    toks = []
    for s in src_bufs:
        if s.ready is not None:
            toks.append(s.ready)
        toks.extend(s.readers)
    for d in dst_bufs:
        d.ready = None
        d.readers = list(toks)


NQSEM = 8


class Prog:
    ENG = {
        "pe": ("tensor", "pe", 1, 1),
        "act": ("scalar", "act", 1, 1),
        "dve": ("vector", "dve", 1, 1),
        "sync": ("sync", "qsp", 16, NQSEM),
        "actdma": ("scalar", "qact", 16, NQSEM),
        "gps": ("gpsimd", "qgps", 16, NQSEM),
        "cc": ("gpsimd", "cc", 1, 1),
    }

    def __init__(self, block, sems):
        self.block = block
        self.sems = sems
        self.cnt = {k: 0 for k in sems}
        self.wm = {}
        self.qidx = {}
        self.streams = {s: [] for s in
                        ("tensor", "scalar", "vector", "sync", "gpsimd")}

    def emit(self, eng, fns, reads=(), writes=(), waits=()):
        sec_name, semkey, inc, ring = self.ENG[eng]
        if ring > 1:
            i = self.qidx.get(eng, 0)
            self.qidx[eng] = i + 1
            semkey = "%s%d" % (semkey, i % ring)
        req = {}

        def add(tok):
            if tok is None:
                return
            sk, v = tok
            if req.get(sk, -1) < v:
                req[sk] = v

        for b in reads:
            add(b.ready)
        for b in writes:
            add(b.ready)
            for t in b.readers:
                add(t)
        for t in waits:
            add(t)
        if inc == 16 and self.cnt[semkey] > 0:
            add((semkey, self.cnt[semkey]))
        wm = self.wm.setdefault(sec_name, {})
        todo = [(sk, v) for sk, v in req.items() if wm.get(sk, 0) < v]
        self.cnt[semkey] += inc
        tok = (semkey, self.cnt[semkey])
        self.streams[sec_name].append((todo, fns, semkey, inc))
        for sk, v in todo:
            if wm.get(sk, 0) < v:
                wm[sk] = v
        for b in reads:
            b.readers.append(tok)
        for b in writes:
            b.ready = tok
            b.readers = []
        return tok

    def final_wait(self, eng, waits):
        sec_name = self.ENG[eng][0]
        self.streams[sec_name].append((list(waits), [], None, 0))

    def flush(self):
        sems = self.sems
        for sec_name, items in self.streams.items():
            if not items:
                continue
            sec = getattr(self.block, sec_name)

            @sec
            def _(eng_obj, items=items):
                for todo, fns, semkey, inc in items:
                    for sk, v in todo:
                        eng_obj.wait_ge(sems[sk], v)
                    inst = None
                    for fn in fns:
                        inst = fn(eng_obj)
                    if semkey is not None:
                        inst.then_inc(sems[semkey], inc)


def build_nc(T):
    assert T % TILE == 0
    NT = T // TILE
    QT = min(1024, T)
    NQ = T // QT
    TPQ = QT // TILE
    NTOK_GLOBAL = float(NCORES * T)

    nc = bass.Bass("TRN2", target_bir_lowering=False)

    x_in = nc.dram_tensor("x", [T, D], F32, kind="ExternalInput").ap()
    wqTp = nc.dram_tensor("wqTp", [128, 8192], F32, kind="ExternalInput").ap()
    wkTp = nc.dram_tensor("wkTp", [128, 8192], F32, kind="ExternalInput").ap()
    wvTp = nc.dram_tensor("wvTp", [128, 8192], F32, kind="ExternalInput").ap()
    wpTp = nc.dram_tensor("wpTp", [128, 8192], F32, kind="ExternalInput").ap()
    sWcat = nc.dram_tensor("sWcat", [128, 16384], F32, kind="ExternalInput").ap()
    eWcat = nc.dram_tensor("eWcat", [NEXP, 128, 16384], F32,
                           kind="ExternalInput").ap()
    routW = nc.dram_tensor("routW", [D, NEXP], F32, kind="ExternalInput").ap()
    eb2_in = nc.dram_tensor("eb2", [NEXP, D], F32, kind="ExternalInput").ap()
    vecs_in = nc.dram_tensor("vecs", [128, 176], F32, kind="ExternalInput").ap()
    rows_in = nc.dram_tensor("rows", [1, 2064], F32, kind="ExternalInput").ap()
    y_out = nc.dram_tensor("y", [T, D], F32, kind="ExternalOutput").ap()

    xn_sc = nc.dram_tensor("xn_sc", [T, D], BF16, kind="Internal").ap()
    h_sc = nc.dram_tensor("h_sc", [3 * D, T], BF16, kind="Internal").ap()
    qk_sc = nc.dram_tensor("qk_sc", [D, T], BF16, kind="Internal").ap()
    v_sc = nc.dram_tensor("v_sc", [D, T], BF16, kind="Internal").ap()
    cc_in = nc.dram_tensor("cc_in", [128, 48], F32, kind="Internal").ap()
    cc_out = nc.dram_tensor("cc_out", [128, 48], F32, kind="Internal").ap()

    ident_c = nc.inline_tensor(np.eye(128, dtype=ml_dtypes.bfloat16),
                               name="identc").ap()
    ones_c = nc.inline_tensor(np.ones((1, 128), dtype=ml_dtypes.bfloat16),
                              name="onesc").ap()
    rowhot_np = np.zeros((NEXP, NEXP * 128), dtype=ml_dtypes.bfloat16)
    for e in range(NEXP):
        rowhot_np[e, e * 128:(e + 1) * 128] = 1.0
    rowhot_c = nc.inline_tensor(rowhot_np, name="rowhotc").ap()

    from contextlib import ExitStack
    with ExitStack() as ctx:
        en = ctx.enter_context
        w_ring = en(nc.sbuf_tensor("w_ring", [128, 4 * 8192], BF16))
        acc = en(nc.sbuf_tensor("acc", [128, max(TPQ * 4096, 5632)], F32))
        xn_q = en(nc.sbuf_tensor("xn_q", [128, max(DC * QT, 8192)], BF16))
        h_arena = en(nc.sbuf_tensor("h_arena", [128, 12288], BF16))
        qk_bufs = en(nc.sbuf_tensor("qk_bufs", [128, 8192], BF16))
        d_extra = en(nc.sbuf_tensor("d_extra", [128, 8192], BF16))
        x_sb = en(nc.sbuf_tensor("x_sb", [128, 4096], F32))
        sq_junk = en(nc.sbuf_tensor("sq_junk", [128, 512], BF16))
        vecs_sb = en(nc.sbuf_tensor("vecs_sb", [128, 176], F32))
        stats_acc = en(nc.sbuf_tensor("stats_acc", [128, 48], F32))
        stats_tmp = en(nc.sbuf_tensor("stats_tmp", [128, 8], F32))
        stats_gl = en(nc.sbuf_tensor("stats_gl", [128, 48], F32))
        ab_coef = en(nc.sbuf_tensor("ab_coef", [128, 48], F32))
        ab_tmp = en(nc.sbuf_tensor("ab_tmp", [128, 24], F32))
        ln_t = en(nc.sbuf_tensor("ln_t", [128, 16], F32))
        bias_bf = en(nc.sbuf_tensor("bias_bf", [1, 1024], BF16))
        rbrow_bf = en(nc.sbuf_tensor("rbrow_bf", [1, 16], BF16))
        rb_bc = en(nc.sbuf_tensor("rb_bc", [128, 15], F32))
        eb2_sb = en(nc.sbuf_tensor("eb2_sb", [15, 1024], BF16))
        routW_sb = en(nc.sbuf_tensor("routW_sb", [128, 8 * 15], BF16))
        vv_sb = en(nc.sbuf_tensor("vv_sb", [128, 1024], F32))
        w_tm_all = en(nc.sbuf_tensor("w_tm_all", [128, 512], BF16))
        w_fm_sb = en(nc.sbuf_tensor("w_fm_sb", [16, 1536], BF16))
        bd_vv = en(nc.sbuf_tensor("bd_vv", [128, 1024], BF16))
        w_tm_t = en(nc.sbuf_tensor("w_tm_t", [128, 16], BF16))
        wbc_sb = en(nc.sbuf_tensor("wbc_sb", [128, 512], BF16))
        l_sb = en(nc.sbuf_tensor("l_sb", [128, 64], F32))
        sm_a = en(nc.sbuf_tensor("sm_a", [128, 16], F32))
        sm_b = en(nc.sbuf_tensor("sm_b", [128, 16], F32))
        sm_c = en(nc.sbuf_tensor("sm_c", [128, 16], F32))
        sm_t = en(nc.sbuf_tensor("sm_t", [128, 8], F32))
        eps_t = en(nc.sbuf_tensor("eps_t", [128, 1], F32))
        c125_t = en(nc.sbuf_tensor("c125_t", [128, 1], F32))
        ident_sb = en(nc.sbuf_tensor("ident_sb", [128, 128], BF16))
        ones_sb = en(nc.sbuf_tensor("ones_sb", [1, 128], BF16))
        rowhot_sb = en(nc.sbuf_tensor("rowhot_sb", [15, 15 * 128], BF16))

        ps_out0 = en(nc.psum_tensor("ps_out0", [128, 1024], F32))
        ps_out1 = en(nc.psum_tensor("ps_out1", [128, 1024], F32))
        ps_h0 = en(nc.psum_tensor("ps_h0", [128, 512], F32))
        ps_h1 = en(nc.psum_tensor("ps_h1", [128, 512], F32))
        ps_wbc = en(nc.psum_tensor("ps_wbc", [128, 512], F32))
        ps_tr = en(nc.psum_tensor("ps_tr", [128, 128], BF16))

        xsq_junk = acc[:, 4096:5120]
        sq_junk = acc[:, 5120:5632]
        sems = {}
        for k in ("pe", "act", "dve", "cc"):
            sems[k] = en(nc.semaphore(k))
        for base in ("qsp", "qact", "qgps"):
            for i in range(NQSEM):
                k = "%s%d" % (base, i)
                sems[k] = en(nc.semaphore(k))
        block = en(nc.Block())
        P = Prog(block, sems)

        ps_out = [ps_out0, ps_out1]
        ps_h = [ps_h0, ps_h1]
        b_psout = [Buf(), Buf()]
        b_psh = [Buf(), Buf()]
        b_pswbc = Buf()
        b_pstr = Buf()
        b_xnq = [Buf() for _ in range(DC)]
        b_acc = [Buf() for _ in range(TPQ * 4)]
        b_hraw = [Buf() for _ in range(DC)]
        b_hscl = [[Buf() for _ in range(DC)], [Buf() for _ in range(DC)]]
        b_hD = [Buf(), Buf(), Buf()]
        b_qkbuf = [Buf(), Buf()]
        b_dx = [Buf(), Buf()]
        b_xsb = Buf()
        b_xin1 = Buf()
        b_xsq = Buf()
        b_sqj = Buf()
        b_vecs = Buf()
        b_stats = Buf()
        b_stmp = [Buf() for _ in range(4)]
        b_stgl = Buf()
        b_ab = Buf()
        b_abt = Buf()
        b_lnt = Buf()
        b_rows = Buf()
        b_bias = Buf()
        b_rbrow = Buf()
        b_rbbc = Buf()
        b_eb2 = Buf()
        b_routW = Buf()
        b_bdvv = Buf()
        b_vvsb = Buf()
        b_wtmall = Buf()
        b_wfm = [Buf(), Buf(), Buf()]
        b_wtm = Buf()
        b_wbc = Buf()
        b_lsb = Buf()
        b_sma = Buf()
        b_smb = Buf()
        b_smc = Buf()
        b_smt = Buf()
        b_ident = Buf()
        b_eps = Buf()
        b_c125 = Buf()
        b_ones = Buf()
        b_rowhot = Buf()
        b_xnsc = [Buf() for _ in range(NT)]
        b_hsc = [[Buf() for _ in range(NT)] for _ in range(3)]
        b_qksc = [Buf() for _ in range(NT)]
        b_vsc = [Buf() for _ in range(NT)]
        b_ysc = [Buf() for _ in range(NT)]
        b_ccin = Buf()
        b_ccout = Buf()

        P.emit("sync", [lambda e: e.dma_start(out=vecs_sb[:], in_=vecs_in)],
               writes=[b_vecs])
        rows_sb = x_sb[0:1, 0:2064]
        P.emit("sync", [lambda e: e.dma_start(out=rows_sb, in_=rows_in)],
               writes=[b_rows])
        P.emit("sync", [lambda e: e.dma_start(out=ident_sb[:], in_=ident_c)],
               writes=[b_ident])
        P.emit("sync", [lambda e: e.dma_start(out=ones_sb[:], in_=ones_c)],
               writes=[b_ones])
        P.emit("sync", [lambda e: e.dma_start(out=rowhot_sb[:], in_=rowhot_c)],
               writes=[b_rowhot])
        P.emit("gps", [lambda e: e.dma_start(out=eb2_sb[:], in_=eb2_in)],
               writes=[b_eb2])
        P.emit("gps", [lambda e: e.dma_start(
            out=routW_sb[:].rearrange("p (dc e) -> p dc e", dc=DC),
            in_=routW.rearrange("(dc p) e -> p dc e", p=128))],
            writes=[b_routW])

        def load_slot(slot, src2, wbuf):
            P.emit("gps", [lambda e, slot=slot, s=src2: e.dma_start(
                out=w_ring[:, slot * 8192:(slot + 1) * 8192], in_=s)],
                writes=[wbuf])

        def load_pair(base, srccat, wbuf):
            P.emit("gps", [lambda e, base=base, s=srccat: e.dma_start(
                out=w_ring[:, base * 16384:(base + 1) * 16384], in_=s)],
                writes=[wbuf])

        b_wpair = [Buf(), Buf()]
        load_slot(0, wqTp, b_wpair[0])
        load_slot(1, wkTp, b_wpair[0])
        load_slot(2, wvTp, b_wpair[1])

        P.emit("dve", [lambda e: e.tensor_add(
            xsq_junk[0:1, 0:1024], rows_sb[:, 0:1024],
            rows_sb[:, 1024:2048])],
            reads=[b_rows], writes=[b_xsq])
        P.emit("act", [lambda e: e.activation(
            out=bias_bf[0:1, :], in_=xsq_junk[0:1, 0:1024], func=AF.Copy)],
            reads=[b_xsq], writes=[b_bias])
        P.emit("act", [lambda e: e.activation(
            out=rbrow_bf[0:1, 0:15], in_=rows_sb[:, 2048:2063],
            func=AF.Copy)],
            reads=[b_rows], writes=[b_rbrow])
        P.emit("pe", [lambda e: e.matmul(
            ps_wbc[:, 0:15], ones_sb[0:1, :], rbrow_bf[0:1, 0:15],
            start=True, stop=True)],
            reads=[b_ones, b_rbrow], writes=[b_pswbc])
        P.emit("dve", [lambda e: e.tensor_copy(rb_bc[:], ps_wbc[:, 0:15])],
               reads=[b_pswbc], writes=[b_rbbc])
        P.emit("dve", [lambda e: e.memset(stats_acc[:], 0.0)],
               writes=[b_stats])
        P.emit("dve", [lambda e: e.memset(w_tm_t[:, 15:16], 0.0)],
               writes=[b_wtm])
        P.emit("dve", [lambda e: e.memset(vv_sb[:], 0.0)],
               writes=[b_vvsb])
        P.emit("dve", [lambda e: e.memset(eps_t[:], EPS)], writes=[b_eps])
        P.emit("dve", [lambda e: e.memset(c125_t[:], 0.125)], writes=[b_c125])

        seed([b_xsb], [b_xsb, b_rows])
        x_bufs = [x_sb[:, 0:4096], acc[:, 0:4096]]
        bx_bufs = [b_xsb, b_xin1]
        xn_tm = d_extra[:, 0:4096]
        b_xntm = b_dx[0]
        hbufs = [d_extra[:, 4096:8192], qk_bufs[:, 0:4096]]
        b_hbufs = [b_dx[1], b_qkbuf[0]]

        def ab_ln_tile(t):
            bi = t % 2
            xb, bxb = x_bufs[bi], bx_bufs[bi]
            src = x_in[t * TILE:(t + 1) * TILE, :].rearrange(
                "(c p) d -> p c d", p=128)
            P.emit("sync", [lambda e, d=xb, s=src: e.dma_start(
                out=d.rearrange("p (c d) -> p c d", c=4), in_=s)],
                writes=[bxb])
            for c in range(4):
                xc = xb[:, c * 1024:(c + 1) * 1024]
                P.emit("dve", [lambda e, xc=xc, c=c: e.reduce_sum(
                    ln_t[:, c:c + 1], xc, axis=AX.X)],
                    reads=[bxb], writes=[b_lnt])
                P.emit("act", [lambda e, xc=xc, c=c: e.activation(
                    out=xsq_junk[:, 0:1024], in_=xc, func=AF.Square,
                    accum_out=ln_t[:, 4 + c:5 + c])],
                    reads=[bxb], writes=[b_xsq, b_lnt])
            P.emit("dve", [lambda e: e.tensor_scalar_mul(
                ln_t[:, 8:12], ln_t[:, 0:4], 1.0 / 1024.0)],
                reads=[b_lnt], writes=[b_lnt])
            P.emit("dve", [lambda e: e.tensor_scalar_mul(
                ln_t[:, 4:8], ln_t[:, 4:8], 1.0 / 1024.0)],
                reads=[b_lnt], writes=[b_lnt])
            P.emit("dve", [lambda e: e.tensor_mul(
                ln_t[:, 12:16], ln_t[:, 8:12], ln_t[:, 8:12])],
                reads=[b_lnt], writes=[b_lnt])
            P.emit("dve", [lambda e: e.tensor_sub(
                ln_t[:, 4:8], ln_t[:, 4:8], ln_t[:, 12:16])],
                reads=[b_lnt], writes=[b_lnt])
            P.emit("act", [lambda e: e.activation(
                out=ln_t[:, 4:8], in_=ln_t[:, 4:8], func=AF.Sqrt,
                bias=eps_t[:, 0:1])],
                reads=[b_lnt, b_eps], writes=[b_lnt])
            P.emit("dve", [lambda e: e.reciprocal(
                ln_t[:, 12:16], ln_t[:, 4:8])],
                reads=[b_lnt], writes=[b_lnt])
            P.emit("dve", [lambda e: e.scalar_tensor_tensor(
                ln_t[:, 8:12], ln_t[:, 8:12], -1.0, ln_t[:, 12:16],
                ALU.mult, ALU.mult)],
                reads=[b_lnt], writes=[b_lnt])
            for c in range(4):
                xc = xb[:, c * 1024:(c + 1) * 1024]
                P.emit("dve", [lambda e, xc=xc, c=c: e.tensor_scalar(
                    xn_tm[:, c * 1024:(c + 1) * 1024], xc,
                    ln_t[:, 12 + c:13 + c], ln_t[:, 8 + c:9 + c],
                    ALU.mult, ALU.add)],
                    reads=[bxb, b_lnt], writes=[b_xntm])
            dst = xn_sc[t * TILE:(t + 1) * TILE, :].rearrange(
                "(c p) d -> p c d", p=128)
            P.emit("sync", [lambda e, d=dst: e.dma_start(
                out=d, in_=xn_tm.rearrange("p (c d) -> p c d", c=4))],
                reads=[b_xntm], writes=[b_xnsc[t]])

        def xn_quarter_load(q):
            qreads = [b_xnsc[q * TPQ + i] for i in range(TPQ)]
            for dc in range(DC):
                P.emit("actdma", [lambda e, q=q, dc=dc:
                                  e.dma_start_transpose(
                    xn_q[:, dc * QT:(dc + 1) * QT],
                    xn_sc[q * QT:(q + 1) * QT, dc * 128:(dc + 1) * 128])],
                    reads=qreads, writes=[b_xnq[dc]])

        psh_cnt = [0]

        def ab_proj_quarter(q):
            for ti in range(TPQ):
                t = q * TPQ + ti
                toff = ti * TILE
                for br in range(3):
                    hb = (t * 3 + br) % 2
                    hbuf, b_hbuf = hbufs[hb], b_hbufs[hb]
                    wslot = w_ring[:, br * 8192:(br + 1) * 8192]
                    for ec in range(DC):
                        par = psh_cnt[0] % 2
                        psh_cnt[0] += 1
                        fns = []
                        for dc in range(DC):
                            fns.append(
                                lambda e, w=wslot, dc=dc, ec=ec, toff=toff,
                                par=par:
                                e.matmul(
                                    ps_h[par][:, 0:512],
                                    w[:, dc * 1024 + ec * 128:
                                      dc * 1024 + ec * 128 + 128],
                                    xn_q[:, dc * QT + toff:
                                         dc * QT + toff + 512],
                                    start=(dc == 0), stop=(dc == DC - 1)))
                        P.emit("pe", fns,
                               reads=[b_wpair[0] if br < 2 else b_wpair[1]]
                               + b_xnq,
                               writes=[b_psh[par]])
                        st_r = (br * DC + ec) % 4
                        P.emit("act", [lambda e, ec=ec, par=par, st_r=st_r,
                                       hbuf=hbuf: e.activation(
                            out=hbuf[:, ec * 512:(ec + 1) * 512],
                            in_=ps_h[par][:, 0:512], func=AF.Copy,
                            accum_out=stats_tmp[:, st_r * 2:st_r * 2 + 1])],
                            reads=[b_psh[par]],
                            writes=[b_hbuf, b_stmp[st_r]])
                        P.emit("dve", [lambda e, ec=ec, st_r=st_r,
                                       hbuf=hbuf: e.scalar_tensor_tensor(
                            sq_junk[:, 0:512],
                            hbuf[:, ec * 512:(ec + 1) * 512], 1.0,
                            hbuf[:, ec * 512:(ec + 1) * 512],
                            ALU.mult, ALU.mult,
                            accum_out=stats_tmp[:, st_r * 2 + 1:
                                                st_r * 2 + 2])],
                            reads=[b_hbuf], writes=[b_sqj, b_stmp[st_r]])
                        cs = br * 16 + ec
                        P.emit("dve", [lambda e, cs=cs, st_r=st_r:
                                       e.tensor_add(
                            stats_acc[:, cs:cs + 16].rearrange(
                                "p (two ec) -> p (ec two)", two=2
                            )[:, 0:2] if False else
                            stats_acc[:, cs:cs + 9].rearrange(
                                "p (a b) -> p a b", a=1)[:, 0, 0:9:8],
                            stats_acc[:, cs:cs + 9].rearrange(
                                "p (a b) -> p a b", a=1)[:, 0, 0:9:8],
                            stats_tmp[:, st_r * 2:st_r * 2 + 2])],
                            reads=[b_stmp[st_r], b_stats], writes=[b_stats])
                    dsth = h_sc[br * D:(br + 1) * D, :].rearrange(
                        "(dc p) n -> p dc n", p=128)
                    for half in range(2):
                        P.emit("sync", [lambda e, d=dsth, half=half, t=t,
                                        hbuf=hbuf: e.dma_start(
                            out=d[:, half * 4:half * 4 + 4,
                                  t * TILE:(t + 1) * TILE],
                            in_=hbuf[:, half * 2048:half * 2048 + 2048]
                            .rearrange("p (dc n) -> p dc n", dc=4))],
                            reads=[b_hbuf], writes=[b_hsc[br][t]])

        def f_router(tq):
            ti = tq % TPQ
            toff = ti * TILE
            for c in range(4):
                fns = []
                for dc in range(DC):
                    fns.append(lambda e, dc=dc, c=c, toff=toff:
                               e.matmul(
                        ps_wbc[:, c * 128:c * 128 + 15],
                        xn_q[:, dc * QT + toff + c * 128:
                             dc * QT + toff + c * 128 + 128],
                        routW_sb[:, dc * 15:(dc + 1) * 15],
                        start=(dc == 0), stop=(dc == DC - 1)))
                P.emit("pe", fns, reads=b_xnq + [b_routW], writes=[b_pswbc])
                lc = l_sb[:, c * 16:c * 16 + 15]
                P.emit("dve", [lambda e, lc=lc, c=c: e.tensor_add(
                    lc, ps_wbc[:, c * 128:c * 128 + 15], rb_bc[:])],
                    reads=[b_pswbc, b_rbbc], writes=[b_lsb])
            for c in range(4):
                lc = l_sb[:, c * 16:c * 16 + 15]
                P.emit("dve", [lambda e, lc=lc: e.reduce_max(
                    sm_t[:, 0:1], lc, axis=AX.X)],
                    reads=[b_lsb], writes=[b_smt])
                P.emit("dve", [lambda e, lc=lc: e.tensor_scalar(
                    sm_a[:, 0:15], lc, sm_t[:, 0:1], None, ALU.is_ge)],
                    reads=[b_lsb, b_smt], writes=[b_sma])
                P.emit("dve", [lambda e, lc=lc: e.scalar_tensor_tensor(
                    sm_b[:, 0:15], sm_a[:, 0:15], -30000.0, lc,
                    ALU.mult, ALU.add)],
                    reads=[b_sma, b_lsb], writes=[b_smb])
                P.emit("dve", [lambda e: e.reduce_max(
                    sm_t[:, 1:2], sm_b[:, 0:15], axis=AX.X)],
                    reads=[b_smb], writes=[b_smt])
                P.emit("dve", [lambda e: e.tensor_scalar(
                    sm_a[:, 0:15], sm_b[:, 0:15], sm_t[:, 1:2], None,
                    ALU.is_ge)],
                    reads=[b_smb, b_smt], writes=[b_sma])
                P.emit("dve", [lambda e: e.scalar_tensor_tensor(
                    sm_b[:, 0:15], sm_a[:, 0:15], -30000.0, sm_b[:, 0:15],
                    ALU.mult, ALU.add)],
                    reads=[b_sma, b_smb], writes=[b_smb])
                P.emit("dve", [lambda e: e.reduce_max(
                    sm_t[:, 2:3], sm_b[:, 0:15], axis=AX.X)],
                    reads=[b_smb], writes=[b_smt])
                P.emit("dve", [lambda e: e.tensor_scalar_mul(
                    sm_t[:, 3:4], sm_t[:, 0:1], -1.0)],
                    reads=[b_smt], writes=[b_smt])
                P.emit("act", [lambda e, lc=lc: e.activation(
                    out=sm_c[:, 0:15], in_=lc, func=AF.Exp,
                    bias=sm_t[:, 3:4], scale=1.0)],
                    reads=[b_lsb, b_smt], writes=[b_smc])
                P.emit("dve", [lambda e: e.reduce_sum(
                    sm_t[:, 4:5], sm_c[:, 0:15], axis=AX.X)],
                    reads=[b_smc], writes=[b_smt])
                P.emit("dve", [lambda e: e.reciprocal(
                    sm_t[:, 5:6], sm_t[:, 4:5])],
                    reads=[b_smt], writes=[b_smt])
                P.emit("dve", [lambda e, lc=lc: e.tensor_scalar(
                    sm_a[:, 0:15], lc, sm_t[:, 2:3], None, ALU.is_ge)],
                    reads=[b_lsb, b_smt], writes=[b_sma])
                P.emit("dve", [lambda e: e.tensor_mul(
                    sm_c[:, 0:15], sm_c[:, 0:15], sm_a[:, 0:15])],
                    reads=[b_sma, b_smc], writes=[b_smc])
                P.emit("dve", [lambda e: e.tensor_scalar_mul(
                    w_tm_t[:, 0:15], sm_c[:, 0:15], sm_t[:, 5:6])],
                    reads=[b_smc, b_smt], writes=[b_wtm])
                P.emit("dve", [lambda e, tq=tq, c=c: e.tensor_copy(
                    w_tm_all[:, (tq * 4 + c) * 16:(tq * 4 + c) * 16 + 16],
                    w_tm_t[:, 0:16])],
                    reads=[b_wtm], writes=[b_wtmall])

        ab_order = list(range(1, NQ)) + [0]
        for i in range(NQ + 1):
            if i >= 1:
                qp = ab_order[i - 1]
                ab_proj_quarter(qp)
                for ti in range(TPQ):
                    f_router(qp * TPQ + ti)
            if i < NQ:
                q = ab_order[i]
                for ti in range(TPQ):
                    ab_ln_tile(q * TPQ + ti)
                xn_quarter_load(q)

        def emit_phase_c():
            P.emit("sync", [lambda e: e.dma_start(out=cc_in, in_=stats_acc[:])],
                   reads=[b_stats], writes=[b_ccin])
            P.emit("cc", [lambda e: e.collective_compute(
                "AllReduce", ALU.add,
                replica_groups=[list(range(NCORES))],
                ins=[cc_in.opt()], outs=[cc_out.opt()])],
                reads=[b_ccin], writes=[b_ccout])
            P.emit("gps", [lambda e: e.dma_start(out=stats_gl[:], in_=cc_out)],
                   reads=[b_ccout], writes=[b_stgl])
            sgl = stats_gl[:].rearrange("p (br k c) -> p br k c", br=3, k=2)
            vec3 = vecs_sb[:, 0:48].rearrange("p (br k c) -> p br k c", br=3, k=2)
            abc = ab_coef[:].rearrange("p (br k c) -> p br k c", br=3, k=2)
            abt = ab_tmp[:].rearrange("p (g c) -> p g c", g=3)
            sums3 = sgl[:, :, 0, :]
            sqs3 = sgl[:, :, 1, :]
            gam3 = vec3[:, :, 0, :]
            bet3 = vec3[:, :, 1, :]
            A3 = abc[:, :, 0, :]
            B3 = abc[:, :, 1, :]
            mean3 = abt[:, 0, :].rearrange("p (br c) -> p br c", br=1)[:, 0, :]
            mean3 = ab_tmp[:, 0:8]
            mean_t = ab_tmp[:, 0:24].rearrange("p (br c) -> p br c", br=3)
            ee_t = stats_tmp[:, 0:8]
            P.emit("dve", [lambda e: e.tensor_scalar_mul(
                mean_t, sums3, 1.0 / NTOK_GLOBAL)],
                reads=[b_stgl], writes=[b_abt])
            P.emit("dve", [lambda e: e.tensor_scalar_mul(
                B3, sqs3, 1.0 / NTOK_GLOBAL)],
                reads=[b_stgl], writes=[b_ab])
            P.emit("dve", [lambda e: e.scalar_tensor_tensor(
                A3, mean_t, 1.0, mean_t, ALU.mult, ALU.mult)],
                reads=[b_abt], writes=[b_ab])
            P.emit("dve", [lambda e: e.tensor_sub(B3, B3, A3)],
                   reads=[b_ab], writes=[b_ab])
            P.emit("act", [lambda e: e.activation(
                out=ab_coef[:].rearrange("p (br k c) -> p br k c",
                                         br=3, k=2)[:, :, 1, :],
                in_=ab_coef[:].rearrange("p (br k c) -> p br k c",
                                         br=3, k=2)[:, :, 1, :],
                func=AF.Sqrt, bias=eps_t[:, 0:1])],
                reads=[b_ab, b_eps], writes=[b_ab])
            P.emit("dve", [lambda e: e.reciprocal(A3, B3)],
                   reads=[b_ab], writes=[b_ab])
            P.emit("dve", [lambda e: e.tensor_mul(A3, A3, gam3)],
                   reads=[b_ab, b_vecs], writes=[b_ab])
            P.emit("dve", [lambda e: e.tensor_mul(mean_t, mean_t, A3)],
                   reads=[b_ab, b_abt], writes=[b_abt])
            P.emit("dve", [lambda e: e.tensor_scalar(
                B3, bet3, 4.0, -4.0, ALU.mult, ALU.add)],
                reads=[b_vecs, b_ab], writes=[b_ab])
            P.emit("dve", [lambda e: e.scalar_tensor_tensor(
                B3, mean_t, -4.0, B3, ALU.mult, ALU.add)],
                reads=[b_abt, b_ab], writes=[b_ab])
            P.emit("dve", [lambda e: e.tensor_scalar_mul(A3, A3, 4.0)],
                   reads=[b_ab], writes=[b_ab])


        d_stage = qk_bufs[:, 0:4096]
        d_qv = qk_bufs[:, 4096:8192]
        d_k = d_extra[:, 4096:8192]
        sig_dst = [d_qv, d_k, d_qv]
        sig_dbuf = [b_qkbuf[1], b_dx[1], b_qkbuf[1]]

        def d_load(t, br):
            srch = h_sc[br * D:(br + 1) * D, :].rearrange(
                "(dc p) n -> p dc n", p=128)
            for half in range(2):
                P.emit("sync", [lambda e, s=srch, half=half, t=t, br=br:
                                e.dma_start(
                    out=d_stage[:, half * 2048:half * 2048 + 2048]
                    .rearrange("p (dc n) -> p dc n", dc=4),
                    in_=s[:, half * 4:half * 4 + 4,
                          t * TILE:(t + 1) * TILE])],
                    reads=[b_hsc[br][t]], writes=[b_qkbuf[0]])

        def d_sigs(t, br):
            for dc in range(DC):
                P.emit("act", [lambda e, br=br, dc=dc: e.activation(
                    out=sig_dst[br][:, dc * 512:(dc + 1) * 512],
                    in_=d_stage[:, dc * 512:(dc + 1) * 512],
                    func=AF.Sigmoid,
                    scale=ab_coef[:, br * 16 + dc:br * 16 + dc + 1],
                    bias=ab_coef[:, br * 16 + 8 + dc:
                                 br * 16 + 8 + dc + 1])],
                    reads=[b_qkbuf[0], b_ab], writes=[sig_dbuf[br]])

        def d_qkmul(t):
            P.emit("dve", [lambda e: e.tensor_mul(
                d_stage[:], d_qv[:], d_k[:])],
                reads=[b_qkbuf[1], b_dx[1]], writes=[b_qkbuf[0]])

        def d_stq(t, which):
            srcb, dstd, btok, bsrc = (
                (d_stage, qk_sc, b_qksc[t], b_qkbuf[0]) if which == 0 else
                (d_qv, v_sc, b_vsc[t], b_qkbuf[1]))
            dd = dstd.rearrange("(dc p) n -> p dc n", p=128)
            for half in range(2):
                P.emit("sync", [lambda e, s=srcb, d=dd, half=half, t=t:
                                e.dma_start(
                    out=d[:, half * 4:half * 4 + 4,
                          t * TILE:(t + 1) * TILE],
                    in_=s[:, half * 2048:half * 2048 + 2048].rearrange(
                        "p (dc n) -> p dc n", dc=4))],
                    reads=[bsrc], writes=[btok])

        def d_vtm(t):
            for c in range(4):
                reg = d_extra[:, c * 1024:c * 1024 + 1024]
                P.emit("actdma", [lambda e, reg=reg, t=t, c=c:
                                  e.dma_start_transpose(
                    reg, v_sc[:, t * TILE + c * 128:
                              t * TILE + (c + 1) * 128])],
                    reads=[b_vsc[t]], writes=[b_dx[0]])

        def d_vv(t):
            paro = pso_cnt[0] % 2
            pso_cnt[0] += 1
            fns = []
            for c in range(4):
                reg = d_extra[:, c * 1024:c * 1024 + 1024]
                for hp in range(8):
                    fns.append(lambda e, reg=reg, hp=hp, c=c, paro=paro:
                               e.matmul(
                        ps_out[paro][:, hp * 128:(hp + 1) * 128],
                        reg[:, hp * 128:(hp + 1) * 128],
                        reg[:, hp * 128:(hp + 1) * 128],
                        start=(c == 0 and hp in (0, 4)),
                        stop=(c == 3 and hp in (3, 7)),
                        skip_group_check=True))
            P.emit("pe", fns, reads=[b_dx[0]], writes=[b_psout[paro]])
            P.emit("dve", [lambda e, paro=paro: e.tensor_add(
                vv_sb[:], vv_sb[:], ps_out[paro][:])],
                reads=[b_psout[paro], b_vvsb], writes=[b_vvsb])

        def d_bdvv():
            P.emit("dve", [lambda e: e.memset(bd_vv[:], 0.0)],
                   reads=[b_dx[1]], writes=[b_bdvv])
            for dc in range(DC):
                for half in range(2):
                    P.emit("dve", [lambda e, dc=dc, half=half:
                                   e.tensor_scalar_mul(
                        bd_vv[half * 64:half * 64 + 64,
                              dc * 128 + half * 64:
                              dc * 128 + half * 64 + 64],
                        vv_sb[half * 64:half * 64 + 64,
                              dc * 128 + half * 64:
                              dc * 128 + half * 64 + 64], 0.125)],
                        reads=[b_vvsb], writes=[b_bdvv])

        def d_sched(gp):
            t, step = gp // 5, gp % 5
            if t >= NT:
                if gp == NT * 5:
                    d_bdvv()
                return
            if step == 0:
                d_load(t, 0)
                d_sigs(t, 0)
                d_load(t, 1)
            elif step == 1:
                d_sigs(t, 1)
                d_qkmul(t)
                d_stq(t, 0)
            elif step == 2:
                d_load(t, 2)
                d_sigs(t, 2)
                d_stq(t, 1)
            elif step == 3:
                d_vtm(t)
            else:
                d_vv(t)

        seed(b_hraw, [b_hD[0]])
        seed(b_hscl[0], [b_hD[1]])
        seed(b_hscl[1], [b_hD[2]])
        seed(b_acc[:4], [b_xin1])

        h_raw = h_arena[:, 0:4096]
        h_scl = [h_arena[:, 4096:8192], h_arena[:, 8192:12288]]
        attn_fm = d_extra[:, 0:4096]
        b_attnfm = b_dx[0]
        pso_cnt = [0]
        hsl_cnt = [0]

        pair_defs = []
        for q in range(NQ):
            pair_defs.append(("pair", sWcat))
            for e_i in range(NEXP):
                pair_defs.append(
                    ("pair", eWcat[e_i:e_i + 1, :, :].rearrange(
                        "one p m -> p (one m)")))
        pair_slots = []

        def emit_pair_load(g):
            base = g % 2
            kind, ap = pair_defs[g]
            if kind == "pair":
                load_pair(base, ap, b_wpair[base])
            else:
                load_slot(base * 2, ap, b_wpair[base])
            pair_slots.append((base * 2, base * 2 + 1, b_wpair[base]))

        emit_pair_load(0)
        emit_pair_load(1)

        def f_first_and_h(p, tq, slot_a, wbuf, shared, hsl):
            ti = tq % TPQ
            toff = ti * TILE
            wslot = w_ring[:, slot_a * 8192:(slot_a + 1) * 8192]
            if not shared:
                P.emit("pe", [lambda e, p=p, ti=ti: e.matmul(
                    ps_wbc[:, 0:512],
                    rowhot_sb[:, (p - 1) * 128:p * 128],
                    w_fm_sb[0:15, ti * 512:(ti + 1) * 512],
                    start=True, stop=True)],
                    reads=[b_rowhot, b_wfm[ti]], writes=[b_pswbc])
                P.emit("dve", [lambda e: e.tensor_copy(
                    wbc_sb[:], ps_wbc[:, 0:512])],
                    reads=[b_pswbc], writes=[b_wbc])
            dstbufs = []
            for ec in range(DC):
                par = psh_cnt[0] % 2
                psh_cnt[0] += 1
                fns = []
                for dc in range(DC):
                    fns.append(lambda e, w=wslot, dc=dc, ec=ec, toff=toff,
                               par=par:
                               e.matmul(
                        ps_h[par][:, 0:512],
                        w[:, dc * 1024 + ec * 128:dc * 1024 + ec * 128 + 128],
                        xn_q[:, dc * QT + toff:dc * QT + toff + 512],
                        start=(dc == 0), stop=(dc == DC - 1)))
                P.emit("pe", fns, reads=[wbuf] + b_xnq,
                       writes=[b_psh[par]])
                bcol = 48 if shared else (56 + (p - 1) * 8)
                P.emit("act", [lambda e, ec=ec, par=par, bcol=bcol:
                               e.activation(
                    out=h_raw[:, ec * 512:(ec + 1) * 512],
                    in_=ps_h[par][:, 0:512], func=AF.Relu,
                    bias=vecs_sb[:, bcol + ec:bcol + ec + 1])],
                    reads=[b_psh[par], b_vecs], writes=[b_hraw[ec]])
                if shared:
                    dstbufs.append(b_hraw[ec])
                else:
                    P.emit("dve", [lambda e, ec=ec, hsl=hsl: e.tensor_mul(
                        h_scl[hsl][:, ec * 512:(ec + 1) * 512],
                        h_raw[:, ec * 512:(ec + 1) * 512], wbc_sb[:])],
                        reads=[b_hraw[ec], b_wbc], writes=[b_hscl[hsl][ec]])
                    dstbufs.append(b_hscl[hsl][ec])
            return dstbufs

        def f_second(lhs_t, lhs_bufs, slot_b, wbuf, tq, first_pass):
            ti = tq % TPQ
            wslot = w_ring[:, slot_b * 8192:(slot_b + 1) * 8192]
            for c in range(4):
                paro = pso_cnt[0] % 2
                pso_cnt[0] += 1
                fns = []
                for hc in range(DC):
                    for half in range(2):
                        fns.append(lambda e, w=wslot, hc=hc, half=half, c=c,
                                   paro=paro:
                                   e.matmul(
                            ps_out[paro][:, half * 512:half * 512 + 512],
                            lhs_t[:, hc * 512 + c * 128:
                                  hc * 512 + c * 128 + 128],
                            w[:, hc * 1024 + half * 512:
                              hc * 1024 + half * 512 + 512],
                            start=(hc == 0), stop=(hc == DC - 1)))
                P.emit("pe", fns, reads=lhs_bufs + [wbuf],
                       writes=[b_psout[paro]])
                aslab = acc[:, (ti * 4 + c) * 1024:(ti * 4 + c + 1) * 1024]
                ab = b_acc[ti * 4 + c]
                if first_pass:
                    P.emit("dve", [lambda e, a=aslab, paro=paro:
                                   e.tensor_copy(a, ps_out[paro][:])],
                           reads=[b_psout[paro]], writes=[ab])
                else:
                    P.emit("dve", [lambda e, a=aslab, paro=paro:
                                   e.tensor_add(a, a, ps_out[paro][:])],
                           reads=[b_psout[paro], ab], writes=[ab])

        wslot_wp = qk_bufs[:, 0:8192]
        b_wp = Buf()

        def load_wpt():
            P.emit("gps", [lambda e: e.dma_start(out=wslot_wp, in_=wpTp)],
                   reads=[b_qkbuf[0], b_qkbuf[1]], writes=[b_wp])

        srcqk_g = qk_sc.rearrange("(dc p) n -> p dc n", p=128)

        def attn_slot_a(t):
            qkb = d_extra[:, 0:4096]
            for half in range(2):
                P.emit("sync", [lambda e, half=half, t=t, qkb=qkb:
                                e.dma_start(
                    out=qkb[:, half * 2048:half * 2048 + 2048]
                    .rearrange("p (dc n) -> p dc n", dc=4),
                    in_=srcqk_g[:, half * 4:half * 4 + 4,
                                t * TILE:(t + 1) * TILE])],
                    reads=[b_qksc[t]], writes=[b_dx[0]])
            for c in range(4):
                P.emit("pe", [lambda e, t=t, c=c: e.transpose(
                    ps_tr[0:16, :],
                    w_tm_all[:, (t * 4 + c) * 16:(t * 4 + c) * 16 + 16],
                    ident_sb[:])],
                    reads=[b_wtmall, b_ident], writes=[b_pstr])
                P.emit("dve", [lambda e, c=c: e.tensor_copy(
                    w_fm_sb[0:16, 1024 + c * 128:1024 + (c + 1) * 128],
                    ps_tr[0:16, :])],
                    reads=[b_pstr], writes=[b_wfm[2]])
            for dc in range(DC):
                par = psh_cnt[0] % 2
                psh_cnt[0] += 1
                P.emit("pe", [lambda e, dc=dc, par=par, qkb=qkb: e.matmul(
                    ps_h[par][:, 0:512], bd_vv[:, dc * 128:(dc + 1) * 128],
                    qkb[:, dc * 512:(dc + 1) * 512], start=True, stop=True)],
                    reads=[b_bdvv, b_dx[0]], writes=[b_psh[par]])
                P.emit("act", [lambda e, dc=dc, par=par: e.activation(
                    out=attn_fm[:, dc * 512:(dc + 1) * 512],
                    in_=ps_h[par][:, 0:512], func=AF.Copy)],
                    reads=[b_psh[par]], writes=[b_attnfm])

        def attn_slot_b(t):
            srcy = y_out[t * TILE:(t + 1) * TILE, :].rearrange(
                "(c p) d -> p c d", p=128)
            P.emit("sync", [lambda e, s=srcy: e.dma_start(
                out=x_sb[:].rearrange("p (c d) -> p c d", c=4), in_=s)],
                reads=[b_ysc[t]], writes=[b_xsb])
            for c in range(4):
                paro = pso_cnt[0] % 2
                pso_cnt[0] += 1
                fns = []
                for hc in range(DC):
                    for half in range(2):
                        fns.append(lambda e, hc=hc, half=half, c=c,
                                   paro=paro:
                                   e.matmul(
                            ps_out[paro][:, half * 512:half * 512 + 512],
                            attn_fm[:, hc * 512 + c * 128:
                                    hc * 512 + c * 128 + 128],
                            wslot_wp[:, hc * 1024 + half * 512:
                                     hc * 1024 + half * 512 + 512],
                            start=(hc == 0), stop=False))
                for half in range(2):
                    fns.append(lambda e, half=half, c=c, paro=paro:
                               e.matmul(
                        ps_out[paro][:, half * 512:half * 512 + 512],
                        w_fm_sb[0:15, 1024 + c * 128:1024 + (c + 1) * 128],
                        eb2_sb[0:15, half * 512:half * 512 + 512],
                        start=False, stop=False))
                for half in range(2):
                    fns.append(lambda e, half=half, paro=paro:
                               e.matmul(
                        ps_out[paro][:, half * 512:half * 512 + 512],
                        ones_sb[0:1, 0:128],
                        bias_bf[0:1, half * 512:half * 512 + 512],
                        start=False, stop=True))
                P.emit("pe", fns,
                       reads=[b_attnfm, b_wp, b_wfm[2], b_eb2,
                              b_ones, b_bias],
                       writes=[b_psout[paro]])
                P.emit("dve", [lambda e, c=c, paro=paro: e.tensor_add(
                    x_sb[:, c * 1024:(c + 1) * 1024],
                    x_sb[:, c * 1024:(c + 1) * 1024], ps_out[paro][:])],
                    reads=[b_psout[paro], b_xsb], writes=[b_xsb])
            dsty = y_out[t * TILE:(t + 1) * TILE, :].rearrange(
                "(c p) d -> p c d", p=128)
            P.emit("sync", [lambda e, d=dsty: e.dma_start(
                out=d, in_=x_sb[:].rearrange("p (c d) -> p c d", c=4))],
                reads=[b_xsb], writes=[b_xsb, b_ysc[t]])

        can_inter = (NQ > 1) and (NT * 5 + 1 <= (NQ - 1) * 16)
        N_INTER = (NQ - 1) * TPQ if can_inter else 0

        def xn_next_load(q):
            qreads = [b_xnsc[q * TPQ + i] for i in range(TPQ)]
            for dc in range(DC):
                P.emit("actdma", [lambda e, q=q, dc=dc:
                                  e.dma_start_transpose(
                    xn_q[:, dc * QT:(dc + 1) * QT],
                    xn_sc[q * QT:(q + 1) * QT, dc * 128:(dc + 1) * 128])],
                    reads=qreads, writes=[b_xnq[dc]])
            for ti in range(TPQ):
                tq0 = q * TPQ + ti
                for c in range(4):
                    P.emit("pe", [lambda e, tq0=tq0, c=c: e.transpose(
                        ps_tr[0:16, :],
                        w_tm_all[:, (tq0 * 4 + c) * 16:
                                 (tq0 * 4 + c) * 16 + 16],
                        ident_sb[:])],
                        reads=[b_wtmall, b_ident], writes=[b_pstr])
                    P.emit("dve", [lambda e, c=c, ti=ti: e.tensor_copy(
                        w_fm_sb[0:16, ti * 512 + c * 128:
                                ti * 512 + (c + 1) * 128],
                        ps_tr[0:16, :])],
                        reads=[b_pstr], writes=[b_wfm[ti]])

        def wfm_rebuild(q):
            for ti in range(TPQ):
                tq0 = q * TPQ + ti
                for c in range(4):
                    P.emit("pe", [lambda e, tq0=tq0, c=c: e.transpose(
                        ps_tr[0:16, :],
                        w_tm_all[:, (tq0 * 4 + c) * 16:
                                 (tq0 * 4 + c) * 16 + 16],
                        ident_sb[:])],
                        reads=[b_wtmall, b_ident], writes=[b_pstr])
                    P.emit("dve", [lambda e, c=c, ti=ti: e.tensor_copy(
                        w_fm_sb[0:16, ti * 512 + c * 128:
                                ti * 512 + (c + 1) * 128],
                        ps_tr[0:16, :])],
                        reads=[b_pstr], writes=[b_wfm[ti]])

        for q in range(NQ):
            if q == 0:
                wfm_rebuild(0)
                emit_phase_c()
            if q == NQ - 1 and can_inter:
                load_wpt()
            for p in range(16):
                g = q * 16 + p
                for ti in range(TPQ):
                    tq = q * TPQ + ti
                    sa, sb, wb = pair_slots[g]
                    shared = (p == 0)
                    hsl = hsl_cnt[0] % 2
                    hsl_cnt[0] += 1
                    lhs_bufs = f_first_and_h(p, tq, sa, wb, shared, hsl)
                    lhs_t = h_raw if shared else h_scl[hsl]
                    f_second(lhs_t, lhs_bufs, sb, wb, tq,
                             first_pass=shared)
                d_sched(g)
                if q == NQ - 1 and can_inter:
                    if p % 2 == 0 and p // 2 < N_INTER:
                        attn_slot_a(p // 2)
                    elif p % 2 == 1 and p // 2 < N_INTER:
                        attn_slot_b(p // 2)
                if g + 2 < len(pair_defs):
                    emit_pair_load(g + 2)
                if p == 15 and q + 1 < NQ:
                    xn_next_load(q + 1)
            for ti in range(TPQ):
                tq = q * TPQ + ti
                src = x_in[tq * TILE:(tq + 1) * TILE, :].rearrange(
                    "(c p) d -> p c d", p=128)
                P.emit("sync", [lambda e, s=src: e.dma_start(
                    out=x_sb[:].rearrange("p (c d) -> p c d", c=4), in_=s)],
                    writes=[b_xsb])
                P.emit("dve", [lambda e, ti=ti: e.tensor_add(
                    x_sb[:], acc[:, ti * 4096:(ti + 1) * 4096], x_sb[:])],
                    reads=[b_acc[ti * 4], b_acc[ti * 4 + 1],
                           b_acc[ti * 4 + 2], b_acc[ti * 4 + 3], b_xsb],
                    writes=[b_xsb])
                dsty = y_out[tq * TILE:(tq + 1) * TILE, :].rearrange(
                    "(c p) d -> p c d", p=128)
                P.emit("sync", [lambda e, d=dsty: e.dma_start(
                    out=d, in_=x_sb[:].rearrange("p (c d) -> p c d", c=4))],
                    reads=[b_xsb], writes=[b_xsb, b_ysc[tq]])

        if not can_inter:
            load_wpt()
        for t in range(N_INTER, NT):
            attn_slot_a(t)
            attn_slot_b(t)

        P.final_wait("sync", [("qsp%d" % i, P.cnt["qsp%d" % i])
                              for i in range(NQSEM)
                              if P.cnt["qsp%d" % i] > 0])
        P.flush()
    return nc


def kernel(L0_attn_x, L0_attn_Wq, L0_attn_gq, L0_attn_bq, L0_attn_Wk,
           L0_attn_gk, L0_attn_bk, L0_attn_Wv, L0_attn_gv, L0_attn_bv,
           L0_attn_Wp, L0_attn_bp, L0_ffn_router_W, L0_ffn_router_b,
           L0_ffn_sW1, L0_ffn_sb1, L0_ffn_sW2, L0_ffn_sb2, L0_ffn_eW1,
           L0_ffn_eb1, L0_ffn_eW2, L0_ffn_eb2):
    f = lambda a: np.ascontiguousarray(np.asarray(a, dtype=np.float32))
    x = f(L0_attn_x)
    B, T, _ = x.shape

    def pc(v):
        return np.ascontiguousarray(f(v).reshape(8, 128).T)

    vecs = np.concatenate(
        [pc(L0_attn_gq), pc(L0_attn_bq), pc(L0_attn_gk), pc(L0_attn_bk),
         pc(L0_attn_gv), pc(L0_attn_bv), pc(L0_ffn_sb1),
         np.ascontiguousarray(
             np.transpose(f(L0_ffn_eb1).reshape(15, 8, 128), (2, 0, 1))
             .reshape(128, 120))],
        axis=1)
    rows = np.zeros((1, 2064), dtype=np.float32)
    rows[0, 0:1024] = f(L0_attn_bp)
    rows[0, 1024:2048] = f(L0_ffn_sb2)
    rows[0, 2048:2063] = f(L0_ffn_router_b)

    def pm(w):
        w = f(w)
        return np.ascontiguousarray(
            w.reshape(8, 128, w.shape[1]).transpose(1, 0, 2).reshape(128, -1))

    eW1f, eW2f = f(L0_ffn_eW1), f(L0_ffn_eW2)
    eWcat = np.ascontiguousarray(np.stack(
        [np.concatenate([pm(eW1f[e]), pm(eW2f[e])], axis=1)
         for e in range(15)], axis=0))
    common = {
        "wqTp": pm(f(L0_attn_Wq).T),
        "wkTp": pm(f(L0_attn_Wk).T),
        "wvTp": pm(f(L0_attn_Wv).T),
        "wpTp": pm(f(L0_attn_Wp).T),
        "sWcat": np.concatenate([pm(L0_ffn_sW1), pm(L0_ffn_sW2)], axis=1),
        "eWcat": eWcat,
        "routW": f(L0_ffn_router_W),
        "eb2": f(L0_ffn_eb2),
        "vecs": vecs,
        "rows": rows,
    }
    in_maps = []
    for b in range(B):
        m = dict(common)
        m["x"] = np.ascontiguousarray(x[b])
        in_maps.append(m)

    nc = build_nc(T)
    res = run_bass_kernel_spmd(nc, in_maps, core_ids=list(range(B)))
    y = np.stack([res.results[i]["y"] for i in range(B)], axis=0)
    return y
